# version 29
# baseline (speedup 1.0000x reference)
import math
import numpy as np
import ml_dtypes
import concourse.bass as bass
import concourse.mybir as mybir
from concourse.bass_utils import run_bass_kernel_spmd

F32 = mybir.dt.float32
BF16 = mybir.dt.bfloat16
AF = mybir.ActivationFunctionType
ALU = mybir.AluOpType
AX = mybir.AxisListType

T = 2048
D = 1024
HA = 16
NEG = -1.0e30
EPS_A = 1e-8
LN_EPS = 1e-5
FH = 8
FDH = 64
INNER = 512
WIN = 512
HID = 2730
LN8 = math.log(8.0)
NHC = 22
HCS = [128] * 21 + [42]
TT = 16
NCH = 4
DC = 8
ARN = 70400

ENGS = ("T", "V", "S", "Q", "G")


class Bld:
    def __init__(self):
        self.ops = []
        self.last_w = {}
        self.readers = {}
        self.last_on_eng = {e: None for e in ENGS}
        self.dma_sems = []

    def emit(self, eng, fn, reads=(), writes=(), dma=None, tag="", extra_deps=()):
        deps = set(extra_deps)
        for k in reads:
            if k in self.last_w:
                deps.add(self.last_w[k])
        for k in writes:
            if k in self.last_w:
                deps.add(self.last_w[k])
            for r in self.readers.get(k, ()):
                deps.add(r)
        i = len(self.ops)
        deps.discard(i)
        if dma is not None and dma not in self.dma_sems:
            self.dma_sems.append(dma)
        self.ops.append(dict(eng=eng, fn=fn, deps=deps, dma=dma, tag=tag, inc=None))
        for k in reads:
            self.readers.setdefault(k, []).append(i)
        for k in writes:
            self.last_w[k] = i
            self.readers[k] = []
        self.last_on_eng[eng] = i
        return i

    def barrier(self, tag="bar"):
        prev = tuple(i for i in self.last_on_eng.values() if i is not None)
        for e in ENGS:
            self.emit(e, lambda eng: eng.nop(), tag=tag + "." + e, extra_deps=prev)

    def finalize(self, blk, sems):
        ops = self.ops
        need_inc = [False] * len(ops)
        for op in ops:
            for d in op["deps"]:
                if ops[d]["eng"] == op["eng"] == "T":
                    continue
                need_inc[d] = True
        cnt = {e: 0 for e in ENGS}
        scnt = {s: 0 for s in self.dma_sems}
        sem_hist = {s: [] for s in self.dma_sems}
        for i, op in enumerate(ops):
            if op["dma"] is not None:
                s = op["dma"]
                scnt[s] += 16
                op["inc"] = scnt[s]
                op["sem"] = s
                sem_hist[s].append((i, scnt[s]))
            elif need_inc[i]:
                cnt[op["eng"]] += 1
                op["inc"] = cnt[op["eng"]]
                op["sem"] = op["eng"]
        eng_clock = {e: {} for e in ENGS}
        op_clock = [None] * len(ops)
        import bisect
        for i, op in enumerate(ops):
            e = op["eng"]
            waits = []
            for d in sorted(op["deps"]):
                dop = ops[d]
                if dop["eng"] == e == "T":
                    continue
                if dop["dma"] is not None:
                    s = dop["sem"]
                    hist = sem_hist[s]
                    pos = bisect.bisect_left(hist, (i, -1)) - 1
                    assert pos >= 0
                    v = hist[pos][1]
                else:
                    s = dop.get("sem")
                    v = dop["inc"]
                    assert v is not None, (i, op["tag"], d, dop["tag"])
                if eng_clock[e].get(s, 0) >= v:
                    continue
                waits.append((s, v))
                for k2, v2 in op_clock[d].items():
                    if k2 != e and eng_clock[e].get(k2, 0) < v2:
                        eng_clock[e][k2] = v2
                if eng_clock[e].get(s, 0) < v:
                    eng_clock[e][s] = v
            op["waits"] = waits
            snap = dict(eng_clock[e])
            if op["inc"] is not None:
                snap[op["sem"]] = op["inc"]
            op_clock[i] = snap
        self._final_dma = [(s, scnt[s]) for s in self.dma_sems if scnt[s] > 0]
        for e, engname in [("T", "tensor"), ("V", "vector"), ("S", "scalar"),
                           ("Q", "sync"), ("G", "gpsimd")]:
            ops_e = [op for op in ops if op["eng"] == e]
            fin = self._final_dma if e == "Q" else ()

            def body(eng, ops_e=ops_e, fin=fin):
                for op in ops_e:
                    for (s, v) in op["waits"]:
                        eng.wait_ge(sems[s], v)
                    inst = op["fn"](eng)
                    if op["inc"] is not None:
                        inst.then_inc(sems[op["sem"]], 16 if op["dma"] is not None else 1)
                for (s, v) in fin:
                    eng.wait_ge(sems[s], v)
            getattr(blk, engname)(body)


def build_program(dbg=()):
    nc = bass.Bass("TRN2", target_bir_lowering=False)

    xt_d = nc.dram_tensor("xt", [D, T], F32, kind="ExternalInput").ap()
    wq_d = nc.dram_tensor("wqT", [D, D], F32, kind="ExternalInput").ap()
    wkv_d = nc.dram_tensor("wkvT", [D, 2 * D], F32, kind="ExternalInput").ap()
    wo_d = nc.dram_tensor("woT", [D, D], F32, kind="ExternalInput").ap()
    fqkv_d = nc.dram_tensor("fqkv", [2, D, 3 * INNER], F32, kind="ExternalInput").ap()
    fwo_d = nc.dram_tensor("fwo", [2, INNER, D], F32, kind="ExternalInput").ap()
    fw1_d = nc.dram_tensor("fw1p", [2, 2 * NHC, 128, 1024], F32,
                           kind="ExternalInput").ap()
    fw2_d = nc.dram_tensor("fw2", [2, HID, D], F32, kind="ExternalInput").ap()
    fb_d = nc.dram_tensor("fb", [128, 160], F32, kind="ExternalInput").ap()
    cstf_d = nc.dram_tensor("cstf", [128, 400], F32, kind="ExternalInput").ap()
    cstb_d = nc.dram_tensor("cstb", [128, 1664], BF16, kind="ExternalInput").ap()
    rot_d = nc.dram_tensor("rot", [4, 128, T], F32, kind="ExternalInput").ap()
    xin_d = nc.dram_tensor("xin_spill", [D, T], F32, kind="Internal").ap()
    out_d = nc.dram_tensor("out_xt", [D, T], F32, kind="ExternalOutput").ap()
    dbg_d = {}
    for name, shape in dbg:
        dbg_d[name] = nc.dram_tensor(name, shape, F32, kind="ExternalOutput").ap()

    import contextlib
    with contextlib.ExitStack() as stk:
        XT = stk.enter_context(nc.sbuf_tensor("XT", [128, DC * T], F32))
        AR = stk.enter_context(nc.sbuf_tensor("AR", [128, ARN], BF16))
        CSTF = stk.enter_context(nc.sbuf_tensor("CSTF", [128, 400], F32))
        CSTB = stk.enter_context(nc.sbuf_tensor("CSTB", [128, 1664], BF16))
        FBIA = stk.enter_context(nc.sbuf_tensor("FBIA", [128, 160], F32))
        PS = stk.enter_context(nc.psum_tensor("PS", [128, 4096], F32))
        sems = {e: stk.enter_context(nc.semaphore("sem" + e)) for e in ENGS}
        for s in ["q_cst", "q_xt", "q_rot", "q_spill", "q_out",
                  "g_cst", "g_wa", "g_fqkv", "g_fwo", "g_w2", "g_w1p0", "g_w1p1",
                  "g_w1p2", "g_xin"]:
            sems[s] = stk.enter_context(nc.semaphore(s))
        blk = stk.enter_context(nc.Block())

        B = Bld()
        ar = AR.ap()
        xt = XT.ap()
        ps = PS.ap()
        cstf = CSTF.ap()
        cstb = CSTB.ap()
        fbia = FBIA.ap()

        C = dict(
            IDENT=cstb[:, 0:128], MTC=cstb[:, 128:256], MTF=cstb[:, 256:384],
            PERM=cstb[:, 384:512], HSEL2=cstb[:, 512:514], ONESC=cstb[:, 514:515],
            MNEG=cstb[:, 515:643], SEL4B=cstb[0:4, 643:1155],
            ONES1B=cstb[0:1, 1280:1408],
            TRIC=cstb[:, 1408:1536], TRIF=cstb[:, 1536:1664],
            IDF=cstf[:, 0:128], ONES1F=cstf[0:1, 128:256],
            ZCOL=cstf[:, 256:257], BC2F=cstf[0:2, 257:385],
        )
        CR = ("cst", "cstb")
        FBR = ("fbia",)
        for i, v in enumerate([EPS_A, LN_EPS, LN8, -1.0, -0.5, 1.0, 0.0]):
            nc.const_aps.register(F32, v, cstf[:, 385 + i: 386 + i])

        def QS2(l): return fbia[:, l * 2: l * 2 + 1]
        def KS2(l): return fbia[:, l * 2 + 1: l * 2 + 2]
        def LNG(l, dc): return fbia[:, 4 + l * 16 + dc: 5 + l * 16 + dc]
        def LNB(l, dc): return fbia[:, 12 + l * 16 + dc: 13 + l * 16 + dc]
        def B2C(l, dc): return fbia[:, 36 + l * 8 + dc: 37 + l * 8 + dc]
        def B1A(l, c): return fbia[:, 52 + l * 43 + c: 53 + l * 43 + c]
        def B1G(l, c): return fbia[:, 52 + l * 43 + 22 + c: 53 + l * 43 + 22 + c]

        def bank(i, c0=0, c1=512, p0=0, p1=128):
            return ps[p0:p1, i * 512 + c0: i * 512 + c1]

        o = 0
        o_scr, o = o, o + 3072
        o_pt0, o = o, o + 512
        o_pt1, o = o, o + 512
        o_xb, o = o, o + 512
        o_sq, o = o, o + 512
        o_tf0, o = o, o + 1024
        o_tf1, o = o, o + 1024
        o_tf2, o = o, o + 1024
        o_negm, o = o, o + 16
        o_negmr, o = o, o + 256
        o_bcol, o = o, o + 16
        o = (o + 127) // 128 * 128
        BASE = o
        assert BASE == 8576, BASE
        SCR = ar[:, o_scr:o_scr + 3072].bitcast(F32)
        S0 = SCR[0:1, 0:512]
        S1 = SCR[0:1, 512:1024]
        S2 = SCR[0:1, 1024:1536]
        S0b = SCR[0:2, 0:512]
        S1b = SCR[0:2, 512:1024]
        ROW_AB = ar[0:1, o_scr + 1024:o_scr + 1536]
        ROW_BB = ar[0:1, o_scr + 2048:o_scr + 2560]
        RDENB = ar[0:1, o_scr + 2048:o_scr + 3072]
        PT = [ar[:, o_pt0:o_pt0 + 512], ar[:, o_pt1:o_pt1 + 512]]
        XB = ar[:, o_xb:o_xb + 512]
        SQ = ar[:, o_sq:o_sq + 512]
        TF0 = ar[:, o_tf0:o_tf0 + 1024].bitcast(F32)
        TF1 = ar[:, o_tf1:o_tf1 + 1024].bitcast(F32)
        TF1RAW = ar[:, o_tf1:o_tf1 + 1024]
        TF2 = ar[:, o_tf2:o_tf2 + 1024].bitcast(F32)
        NEGM = ar[:, o_negm:o_negm + 16].bitcast(F32)
        NEGMR = ar[0:4, o_negmr:o_negmr + 128]
        NEGMR2 = ar[0:4, o_negmr + 128:o_negmr + 256]
        BCOL = ar[:, o_bcol:o_bcol + 16].bitcast(F32)

        ctx = dict(B=B, ar=ar, xt=xt, ps=ps, bank=bank, C=C, CR=CR, FBR=FBR,
                   S0=S0, S1=S1, S2=S2, S0b=S0b, S1b=S1b, SCR01=SCR[0:1, 0:1024],
                   ROW_AB=ROW_AB, ROW_BB=ROW_BB, RDENB=RDENB,
                   TF1RAW=TF1RAW,
                   PT=PT, XB=XB, SQ=SQ, TF0=TF0, TF1=TF1,
                   TF2=TF2, NEGM=NEGM, NEGMR=NEGMR, NEGMR2=NEGMR2, BCOL=BCOL,
                   QS2=QS2, KS2=KS2,
                   LNG=LNG, LNB=LNB, B2C=B2C, B1A=B1A, B1G=B1G, dbg=dbg_d, XT=XT)

        B.emit("Q", lambda eng: eng.dma_start(out=CSTF[:], in_=cstf_d[:, :]),
               dma="q_cst", writes=CR, tag="ld_cstf")
        B.emit("Q", lambda eng: eng.dma_start(out=FBIA[:], in_=fb_d[:, :]),
               dma="q_cst", writes=FBR, tag="ld_fb")
        B.emit("G", lambda eng: eng.dma_start(out=CSTB[:], in_=cstb_d[:, :]),
               dma="g_cst", writes=("cstb",), tag="ld_cstb")
        for dc in range(DC):
            def ldx(eng, dc=dc):
                return eng.dma_start(out=XT[:, dc * T:(dc + 1) * T],
                                     in_=xt_d[dc * 128:(dc + 1) * 128, :])
            B.emit("Q", ldx, dma="q_xt", writes=(f"xt.{dc}",), tag=f"ld_xt{dc}")

        phaseA(ctx, wq_d, wkv_d, wo_d, BASE)
        B.barrier("endA")

        for dc in range(DC):
            def sp(eng, dc=dc):
                return eng.dma_start(out=xin_d[dc * 128:(dc + 1) * 128, :],
                                     in_=XT[:, dc * T:(dc + 1) * T])
            B.emit("Q", sp, dma="q_spill", reads=(f"xt.{dc}",),
                   writes=(f"xin.{dc}",), tag=f"spill{dc}")
        if "dbg_pa" in dbg_d:
            for dc in range(DC):
                def dp(eng, dc=dc):
                    return eng.dma_start(
                        out=dbg_d["dbg_pa"][dc * 128:(dc + 1) * 128, :],
                        in_=XT[:, dc * T:(dc + 1) * T])
                B.emit("Q", dp, dma="q_out", reads=(f"xt.{dc}",), tag=f"dbgpa{dc}")

        _fs = [(ar[:, 67072 + i * 512: 67072 + (i + 1) * 512], f"fsc{i}")
               for i in range(2)]
        FSCR = [_fs[0], _fs[1], _fs[0], _fs[1]]

        def final_chunk(tc):
            for dc in range(DC):
                key = f"xt.{dc}"
                deps = set()
                if key in B.last_w:
                    deps.add(B.last_w[key])
                deps.update(B.readers.get(key, ()))

                def acc(eng, dc=dc, tc=tc):
                    return eng.dma_start(
                        out=XT[:, dc * T + tc * 512: dc * T + (tc + 1) * 512],
                        in_=xin_d[dc * 128:(dc + 1) * 128, tc * 512:(tc + 1) * 512],
                        accum_op=ALU.add)
                B.emit("G", acc, dma="g_xin", reads=(f"xin.{dc}", "r_tail"),
                       writes=(f"xtf.{tc}.{dc}",), extra_deps=tuple(deps),
                       tag=f"facc.{tc}.{dc}")
            ln_chunk(ctx, tc * 512, f"fin{tc}", scr=FSCR,
                     kf=lambda dc: f"xtf.{tc}.{dc}")
            for dc in range(DC):
                def st(eng, dc=dc, tc=tc):
                    return eng.dma_start(
                        out=out_d[dc * 128:(dc + 1) * 128, tc * 512:(tc + 1) * 512],
                        in_=xt[:, dc * T + tc * 512: dc * T + (tc + 1) * 512])
                B.emit("Q", st, dma="q_out", reads=(f"xtf.{tc}.{dc}",),
                       tag=f"stout.{tc}.{dc}")

        for l in range(2):
            ffn_layer(ctx, l, fqkv_d, fwo_d, fw1_d, fw2_d, rot_d, BASE,
                      fin=final_chunk if l == 1 else None)
            if f"dbg_l{l}" in dbg_d:
                B.barrier(f"dbgl{l}")
                for dc in range(DC):
                    def dp(eng, dc=dc, l=l):
                        return eng.dma_start(
                            out=dbg_d[f"dbg_l{l}"][dc * 128:(dc + 1) * 128, :],
                            in_=XT[:, dc * T:(dc + 1) * T])
                    B.emit("Q", dp, dma="q_out", reads=(f"xt.{dc}",), tag=f"dbgl{l}.{dc}")

        B.barrier("preF")
        B.barrier("final")

        B.finalize(blk, sems)
    return nc


def ln_chunk(ctx, c0, tg, scr=None, kf=None):
    B, xt, bank = ctx["B"], ctx["xt"], ctx["bank"]
    C, CR = ctx["C"], ctx["CR"]
    S0, S1, S2 = ctx["S0"], ctx["S1"], ctx["S2"]
    TF0 = ctx["TF0"]
    if scr is None:
        scr = [(ctx["XB"], "xb"), (ctx["SQ"], "sq"), (ctx["XB"], "xb"), (ctx["SQ"], "sq")]
    if kf is None:
        kf = lambda dc: f"xt.{dc}"
    n = 512
    for dc in range(DC):
        XBp, xbk = scr[(dc % 2) * 2]
        SQp, sqk = scr[(dc % 2) * 2 + 1]

        def cst(eng, dc=dc, XBp=XBp):
            return eng.tensor_copy(out=XBp, in_=xt[:, dc * T + c0: dc * T + c0 + n])
        B.emit("V", cst, reads=(kf(dc),), writes=(xbk,), tag=f"{tg}lc{dc}")
        B.emit("S", lambda eng, XBp=XBp, SQp=SQp: eng.activation(
            out=SQp, in_=XBp, func=AF.Square),
            reads=(xbk,), writes=(sqk,), tag=f"{tg}lq{dc}")

        def s1(eng, dc=dc, XBp=XBp):
            return eng.matmul(out=bank(6, 0, n, 0, 1), lhsT=C["ONESC"], rhs=XBp,
                              start=(dc == 0), stop=(dc == DC - 1))
        B.emit("T", s1, reads=(xbk,) + CR, writes=("ps6",), tag=f"{tg}l1{dc}")

        def s2(eng, dc=dc, SQp=SQp):
            return eng.matmul(out=bank(7, 0, n, 0, 1), lhsT=C["ONESC"], rhs=SQp,
                              start=(dc == 0), stop=(dc == DC - 1))
        B.emit("T", s2, reads=(sqk,) + CR, writes=("ps7",), tag=f"{tg}l2{dc}")
    B.emit("V", lambda eng: eng.tensor_scalar(out=S0, in0=bank(6, 0, n, 0, 1),
           scalar1=1.0 / D, scalar2=None, op0=ALU.mult),
           reads=("ps6",), writes=("lnS0",), tag=f"{tg}mu")
    B.emit("V", lambda eng: eng.tensor_scalar(out=S1, in0=bank(7, 0, n, 0, 1),
           scalar1=1.0 / D, scalar2=None, op0=ALU.mult),
           reads=("ps7",), writes=("lnS1",), tag=f"{tg}ms")
    B.emit("V", lambda eng: eng.tensor_tensor(out=S2, in0=S0, in1=S0, op=ALU.mult),
           reads=("lnS0",), writes=("lnS2",), tag=f"{tg}v1")
    B.emit("V", lambda eng: eng.tensor_tensor(out=S2, in0=S1, in1=S2, op=ALU.subtract),
           reads=("lnS1", "lnS2"), writes=("lnS2",), tag=f"{tg}v2")
    B.emit("S", lambda eng: eng.activation(out=S2, in_=S2, func=AF.Ln, bias=LN_EPS),
           reads=("lnS2",), writes=("lnS2",), tag=f"{tg}ln")
    B.emit("S", lambda eng: eng.activation(out=S1, in_=S2, func=AF.Exp, scale=-0.5),
           reads=("lnS2",), writes=("lnS1",), tag=f"{tg}ex")
    B.emit("V", lambda eng: eng.scalar_tensor_tensor(out=S0, in0=S0, scalar=-1.0,
           in1=S1, op0=ALU.mult, op1=ALU.mult),
           reads=("lnS0", "lnS1"), writes=("lnS0",), tag=f"{tg}B")
    B.emit("T", lambda eng: eng.matmul(out=bank(6), lhsT=C["ONES1F"], rhs=S1,
           start=True, stop=True), reads=("lnS1",) + CR, writes=("ps6",), tag=f"{tg}bA")
    B.emit("T", lambda eng: eng.matmul(out=bank(7), lhsT=C["ONES1F"], rhs=S0,
           start=True, stop=True), reads=("lnS0",) + CR, writes=("ps7",), tag=f"{tg}bB")
    for dc in range(DC):
        def a1(eng, dc=dc):
            return eng.tensor_tensor(out=TF0[:, 0:n],
                                     in0=xt[:, dc * T + c0: dc * T + c0 + n],
                                     in1=bank(6), op=ALU.mult)
        B.emit("V", a1, reads=(kf(dc), "ps6"), writes=("tf0",), tag=f"{tg}a1.{dc}")

        def a2(eng, dc=dc):
            return eng.tensor_tensor(out=xt[:, dc * T + c0: dc * T + c0 + n],
                                     in0=TF0[:, 0:n], in1=bank(7), op=ALU.add)
        B.emit("V", a2, reads=("tf0", "ps7"), writes=(kf(dc),), tag=f"{tg}a2.{dc}")


def phaseA(ctx, wq_d, wkv_d, wo_d, base):
    B = ctx["B"]
    ar, xt, bank = ctx["ar"], ctx["xt"], ctx["bank"]
    C, CR = ctx["C"], ctx["CR"]
    PT, TF2 = ctx["PT"], ctx["TF2"]
    S0, S1, S2 = ctx["S0"], ctx["S1"], ctx["S2"]
    TF0 = ctx["TF0"]

    o_kt = base
    o_v = o_kt + 16384
    o_xbfc = o_v + 16640
    o_qtc = o_xbfc + 4096
    o_otc = o_qtc + 4096
    o_wsl = o_otc + 4096
    assert o_wsl + 16384 <= ARN, o_wsl + 16384

    def kT_t(f):
        return ar[:, o_kt + f * 2048: o_kt + (f + 1) * 2048]

    def v_t(tt, h):
        b = o_v + tt * 1040 + h * 65
        return ar[:, b: b + 65]

    def xbfc_t(dc):
        return ar[:, o_xbfc + dc * 512: o_xbfc + (dc + 1) * 512]

    def qtc_t(f):
        return ar[:, o_qtc + f * 512: o_qtc + (f + 1) * 512]

    def otc_t(ic):
        return ar[:, o_otc + ic * 512: o_otc + (ic + 1) * 512]

    def wkv_t(dc, f0, nn):
        return ar[:, o_wsl + dc * 2048 + f0: o_wsl + dc * 2048 + f0 + nn]

    def wq_t(dc, f):
        return ar[:, o_wsl + dc * 1024 + f * 128: o_wsl + dc * 1024 + (f + 1) * 128]

    def wo_t(ic, dc):
        b = o_wsl + 8192 + ic * 1024 + dc * 128
        return ar[:, b: b + 128]

    def vones(eng):
        vv = ar[:, o_v:o_v + 16640].rearrange("p (t h c) -> p t h c", t=TT, h=HA)
        return eng.memset(vv[:, :, :, 64:65], 1.0)
    B.emit("V", vones, writes=tuple(f"v.{tt}" for tt in range(TT)), tag="v_ones")

    for dc in range(DC):
        def ld(eng, dc=dc):
            return eng.dma_start(out=wkv_t(dc, 0, 2048),
                                 in_=wkv_d[dc * 128:(dc + 1) * 128, :])
        B.emit("G", ld, dma="g_wa", writes=(f"wkv.{dc}", "wslot"), tag=f"ld_wkv{dc}")

    for tc in range(NCH):
        for dc in range(DC):
            def cst(eng, dc=dc, tc=tc):
                return eng.tensor_copy(
                    out=xbfc_t(dc),
                    in_=xt[:, dc * T + tc * 512: dc * T + (tc + 1) * 512])
            B.emit("V", cst, reads=(f"xt.{dc}",), writes=(f"xbfc.{dc}",),
                   tag=f"cxkv.{tc}.{dc}")
        for f in range(8):
            bk = f % 2
            for dc in range(DC):
                def mm(eng, f=f, dc=dc, bk=bk):
                    return eng.matmul(out=bank(bk),
                                      lhsT=wkv_t(dc, f * 128, 128), rhs=xbfc_t(dc),
                                      start=(dc == 0), stop=(dc == DC - 1))
                B.emit("T", mm, reads=(f"wkv.{dc}", f"xbfc.{dc}", "wslot"),
                       writes=(f"ps{bk}",), tag=f"kmm.{tc}.{f}.{dc}")

            def ev(eng, f=f, tc=tc, bk=bk):
                return eng.tensor_copy(out=kT_t(f)[:, tc * 512:(tc + 1) * 512],
                                       in_=bank(bk))
            B.emit("V", ev, reads=(f"ps{bk}",), writes=(f"kT.{f}",), tag=f"kev.{tc}.{f}")
        for ttl in range(4):
            tt = tc * 4 + ttl
            for fc in range(2):
                bk = 2 + fc
                for dc in range(DC):
                    def mm(eng, ttl=ttl, fc=fc, dc=dc, bk=bk):
                        return eng.matmul(
                            out=bank(bk),
                            lhsT=xbfc_t(dc)[:, ttl * 128:(ttl + 1) * 128],
                            rhs=wkv_t(dc, 1024 + fc * 512, 512),
                            start=(dc == 0), stop=(dc == DC - 1))
                    B.emit("T", mm, reads=(f"wkv.{dc}", f"xbfc.{dc}", "wslot"),
                           writes=(f"ps{bk}",), tag=f"vmm.{tt}.{fc}.{dc}")

                def ev(eng, tt=tt, fc=fc, bk=bk):
                    b0 = o_v + tt * 1040 + fc * 8 * 65
                    dst = ar[:, b0: b0 + 8 * 65].rearrange("p (h c) -> p h c", h=8)
                    return eng.tensor_copy(out=dst[:, :, 0:64], in_=bank(bk))
                B.emit("V", ev, reads=(f"ps{bk}",), writes=(f"v.{tt}",),
                       tag=f"vev.{tt}.{fc}")

    for dc in range(DC):
        def ld(eng, dc=dc):
            return eng.dma_start(
                out=ar[:, o_wsl + dc * 1024: o_wsl + (dc + 1) * 1024],
                in_=wq_d[dc * 128:(dc + 1) * 128, :])
        B.emit("G", ld, dma="g_wa", writes=(f"wq.{dc}", "wslot"), tag=f"ld_wq{dc}")
    for dc in range(DC):
        def ld(eng, dc=dc):
            return eng.dma_start(
                out=ar[:, o_wsl + 8192 + dc * 1024: o_wsl + 8192 + (dc + 1) * 1024],
                in_=wo_d[dc * 128:(dc + 1) * 128, :])
        B.emit("G", ld, dma="g_wa", writes=(f"wo.{dc}", "wslot"), tag=f"ld_wo{dc}")

    PTW = [ar[:, o_xbfc + i * 1024: o_xbfc + (i + 1) * 1024] for i in range(4)]
    PTK = [("xbfc.0", "xbfc.1"), ("xbfc.2", "xbfc.3"),
           ("xbfc.4", "xbfc.5"), ("xbfc.6", "xbfc.7")]
    PARDEN = ctx["SCR01"]
    RDENB = ctx["RDENB"]
    ROW_AB, ROW_BB = ctx["ROW_AB"], ctx["ROW_BB"]
    LNT1A = ctx["TF1RAW"][:, 0:512]
    LNT1B = ctx["TF1RAW"][:, 512:1024]

    def emit_castq(tc):
        for dc in range(DC):
            def cst(eng, dc=dc, tc=tc):
                return eng.tensor_copy(
                    out=xbfc_t(dc),
                    in_=xt[:, dc * T + tc * 512: dc * T + (tc + 1) * 512])
            B.emit("V", cst, reads=(f"xt.{dc}",), writes=(f"xbfc.{dc}",),
                   tag=f"cxq.{tc}.{dc}")
        for f in range(8):
            bk = f % 2
            for dc in range(DC):
                def mm(eng, f=f, dc=dc, bk=bk):
                    return eng.matmul(out=bank(bk), lhsT=wq_t(dc, f), rhs=xbfc_t(dc),
                                      start=(dc == 0), stop=(dc == DC - 1))
                B.emit("T", mm, reads=(f"wq.{dc}", f"xbfc.{dc}", "wslot"),
                       writes=(f"ps{bk}",), tag=f"qmm.{tc}.{f}.{dc}")

            def ev(eng, f=f, bk=bk):
                return eng.tensor_scalar(out=qtc_t(f), in0=bank(bk), scalar1=0.125,
                                         scalar2=None, op0=ALU.mult)
            B.emit("V", ev, reads=(f"ps{bk}",), writes=(f"qtc.{f}",), tag=f"qev.{tc}.{f}")

    def emit_headpair(tc, hp):
        ntj = 4 * tc + 4
        npair = ntj // 2
        HP = [(2 * hp, 0, 4), (2 * hp + 1, 2, 5)]

        def stage2(p):
            zss = [[], []]
            for u in range(2):
                tj = 2 * p + u
                diag = (tj // 4 == tc)
                r = tj - 4 * tc if diag else 0
                for side in range(2):
                    h, sb, nb = HP[side]
                    fq = h // 2
                    pr = (h % 2) * 64
                    sbk = sb + u

                    def smm(eng, fq=fq, pr=pr, tj=tj, sbk=sbk, stp=(not diag)):
                        return eng.matmul(
                            out=bank(sbk),
                            lhsT=kT_t(fq)[pr:pr + 64, tj * 128:(tj + 1) * 128],
                            rhs=qtc_t(fq)[pr:pr + 64, 0:512],
                            start=True, stop=stp)
                    B.emit("T", smm, reads=(f"kT.{fq}", f"qtc.{fq}"),
                           writes=(f"ps{sbk}",), tag=f"sT.{tc}.{h}.{tj}")
                if diag:
                    for side in range(2):
                        h, sb, nb = HP[side]
                        sbk = sb + u

                        def msk(eng, r=r, sbk=sbk):
                            return eng.matmul(out=bank(sbk, r * 128, r * 128 + 128),
                                              lhsT=C["IDENT"], rhs=C["MTC"],
                                              start=False, stop=True)
                        B.emit("T", msk, reads=CR, writes=(f"ps{sbk}",),
                               tag=f"smk.{tc}.{2 * hp + side}.{tj}")
                    if r > 0:
                        zss[0].append((u, r))
                        zss[1].append((u, r))
            for side in range(2):
                h, sb, nb = HP[side]
                bkeys = (f"ps{sb}", f"ps{sb + 1}")
                pt = PTW[side * 2 + (p % 2)]
                ptk = PTK[side * 2 + (p % 2)]

                def ex(eng, sb=sb, pt=pt):
                    return eng.activation(out=pt[:, 0:1024],
                                          in_=bank(sb, 0, 1024), func=AF.Exp)
                B.emit("S", ex, reads=bkeys, writes=ptk,
                       tag=f"exp.{tc}.{hp}.{side}.{p}")
                for (u, r) in zss[side]:
                    B.emit("V", lambda eng, pt=pt, u=u, r=r: eng.memset(
                        pt[:, u * 512:u * 512 + r * 128], 0.0),
                        writes=ptk, tag=f"ptz.{tc}.{hp}.{side}.{p}.{u}")

        def pvst(p, side):
            h, sb, nb = HP[side]
            pt = PTW[side * 2 + (p % 2)]
            ptk = PTK[side * 2 + (p % 2)]
            for u in range(2):
                tj = 2 * p + u

                def pv(eng, tj=tj, h=h, pt=pt, u=u, nb=nb, first=(tj == 0),
                       last=(tj == ntj - 1)):
                    return eng.matmul(out=bank(nb, 0, 512, 0, 65),
                                      lhsT=v_t(tj, h),
                                      rhs=pt[:, u * 512:(u + 1) * 512],
                                      start=first, stop=last)
                B.emit("T", pv, reads=(f"v.{tj}",) + ptk,
                       writes=(f"ps{nb}",), tag=f"pv.{tc}.{h}.{tj}")

        stage2(0)
        for p in range(1, npair):
            stage2(p)
            pvst(p - 1, 0)
            pvst(p - 1, 1)
        pvst(npair - 1, 0)
        pvst(npair - 1, 1)
        RDK = ("lnS0", "lnS1")
        i_ln = B.emit("S", lambda eng: eng.activation(
            out=PARDEN, in_=bank(4, 0, 1024, 64, 65), func=AF.Ln, bias=EPS_A),
            reads=("ps4", "ps5"), writes=RDK, tag=f"lnden.{tc}.{hp}")
        B.emit("S", lambda eng: eng.activation(
            out=RDENB, in_=PARDEN, func=AF.Exp, scale=-1.0),
            reads=RDK, writes=RDK + ("lnS2",), tag=f"rden.{tc}.{hp}")
        for side in range(2):
            h, sb, nb = HP[side]
            rb = sb
            B.emit("V", lambda eng, nb=nb: eng.tensor_copy(out=TF2[0:64, 0:512],
                   in_=bank(nb, 0, 512, 0, 64)),
                   reads=(f"ps{nb}",), writes=("tf2",), tag=f"nev.{tc}.{h}",
                   extra_deps=(i_ln,))
            B.emit("T", lambda eng, rb=rb, side=side: eng.matmul(
                out=bank(rb, 0, 512, 0, 64), lhsT=C["ONES1B"][:, 0:64],
                rhs=RDENB[:, side * 512:(side + 1) * 512],
                start=True, stop=True),
                reads=RDK + ("lnS2",) + CR, writes=(f"ps{rb}",), tag=f"rdb.{tc}.{h}")
            ic = h // 2
            prh = (h % 2) * 64

            def oev(eng, ic=ic, prh=prh, rb=rb):
                return eng.tensor_tensor(out=otc_t(ic)[prh:prh + 64, :],
                                         in0=TF2[0:64, 0:512],
                                         in1=bank(rb, 0, 512, 0, 64), op=ALU.mult)
            B.emit("V", oev, reads=("tf2", f"ps{rb}"), writes=(f"otc.{ic}",),
                   tag=f"oev.{tc}.{h}")

    def emit_wo_stats(tc):
        for dc in range(DC):
            bk = 4 + dc % 2
            for ic in range(8):
                def mm(eng, dc=dc, ic=ic, bk=bk):
                    return eng.matmul(out=bank(bk), lhsT=wo_t(ic, dc), rhs=otc_t(ic),
                                      start=(ic == 0), stop=(ic == 7))
                B.emit("T", mm, reads=(f"wo.{dc}", f"otc.{ic}", "wslot"),
                       writes=(f"ps{bk}",), tag=f"wom.{tc}.{dc}.{ic}")

            def res(eng, dc=dc, tc=tc, bk=bk):
                sl = xt[:, dc * T + tc * 512: dc * T + (tc + 1) * 512]
                return eng.tensor_tensor(out=sl, in0=sl, in1=bank(bk), op=ALU.add)
            B.emit("V", res, reads=(f"ps{bk}", f"xt.{dc}"), writes=(f"xt.{dc}",),
                   tag=f"res.{tc}.{dc}")
            if dc % 2 == 0:
                XBp, xbk = ctx["XB"], "xb"
                SQp, sqk = ctx["SQ"], "sq"
            else:
                XBp, xbk = LNT1A, "tf1"
                SQp, sqk = LNT1B, "tf1b"

            def cst(eng, dc=dc, tc=tc, XBp=XBp):
                return eng.tensor_copy(out=XBp,
                                       in_=xt[:, dc * T + tc * 512: dc * T + (tc + 1) * 512])
            B.emit("V", cst, reads=(f"xt.{dc}",), writes=(xbk,), tag=f"palc.{tc}.{dc}")
            B.emit("S", lambda eng, XBp=XBp, SQp=SQp: eng.activation(
                out=SQp, in_=XBp, func=AF.Square),
                reads=(xbk,), writes=(sqk,), tag=f"palq.{tc}.{dc}")

            def s1(eng, dc=dc, XBp=XBp):
                return eng.matmul(out=bank(6, 0, 512, 0, 1), lhsT=C["ONESC"], rhs=XBp,
                                  start=(dc == 0), stop=(dc == DC - 1))
            B.emit("T", s1, reads=(xbk,) + CR, writes=("ps6",), tag=f"pal1.{tc}.{dc}")

            def s2(eng, dc=dc, SQp=SQp):
                return eng.matmul(out=bank(7, 0, 512, 0, 1), lhsT=C["ONESC"], rhs=SQp,
                                  start=(dc == 0), stop=(dc == DC - 1))
            B.emit("T", s2, reads=(sqk,) + CR, writes=("ps7",), tag=f"pal2.{tc}.{dc}")

    def emit_ln_tail(tc):
        tg = f"pa{tc}"
        c0 = tc * 512
        n = 512
        B.emit("V", lambda eng: eng.tensor_scalar(out=S0, in0=bank(6, 0, n, 0, 1),
               scalar1=1.0 / D, scalar2=None, op0=ALU.mult),
               reads=("ps6",), writes=("lnS0",), tag=f"{tg}mu")
        B.emit("V", lambda eng: eng.tensor_scalar(out=S1, in0=bank(7, 0, n, 0, 1),
               scalar1=1.0 / D, scalar2=None, op0=ALU.mult),
               reads=("ps7",), writes=("lnS1",), tag=f"{tg}ms")
        B.emit("V", lambda eng: eng.tensor_tensor(out=S2, in0=S0, in1=S0, op=ALU.mult),
               reads=("lnS0",), writes=("lnS2",), tag=f"{tg}v1")
        B.emit("V", lambda eng: eng.tensor_tensor(out=S2, in0=S1, in1=S2,
               op=ALU.subtract),
               reads=("lnS1", "lnS2"), writes=("lnS2",), tag=f"{tg}v2")
        B.emit("S", lambda eng: eng.activation(out=S2, in_=S2, func=AF.Ln, bias=LN_EPS),
               reads=("lnS2",), writes=("lnS2",), tag=f"{tg}ln")
        B.emit("S", lambda eng: eng.activation(out=S1, in_=S2, func=AF.Exp,
               scale=-0.5),
               reads=("lnS2",), writes=("lnS1",), tag=f"{tg}ex")
        B.emit("V", lambda eng: eng.scalar_tensor_tensor(out=S0, in0=S0, scalar=-1.0,
               in1=S1, op0=ALU.mult, op1=ALU.mult),
               reads=("lnS0", "lnS1"), writes=("lnS0",), tag=f"{tg}B")
        B.emit("T", lambda eng: eng.matmul(out=bank(6), lhsT=C["ONES1F"], rhs=S1,
               start=True, stop=True), reads=("lnS1",) + CR, writes=("ps6",),
               tag=f"{tg}bA")
        B.emit("T", lambda eng: eng.matmul(out=bank(7), lhsT=C["ONES1F"], rhs=S0,
               start=True, stop=True), reads=("lnS0",) + CR, writes=("ps7",),
               tag=f"{tg}bB")
        for dc in range(DC):
            def a1(eng, dc=dc):
                return eng.tensor_tensor(out=TF0[:, 0:n],
                                         in0=xt[:, dc * T + c0: dc * T + c0 + n],
                                         in1=bank(6), op=ALU.mult)
            B.emit("V", a1, reads=(f"xt.{dc}", "ps6"), writes=("tf0",),
                   tag=f"{tg}a1.{dc}")

            def a2(eng, dc=dc):
                return eng.tensor_tensor(out=xt[:, dc * T + c0: dc * T + c0 + n],
                                         in0=TF0[:, 0:n], in1=bank(7), op=ALU.add)
            B.emit("V", a2, reads=("tf0", "ps7"), writes=(f"xt.{dc}",),
                   tag=f"{tg}a2.{dc}")

    emit_castq(0)
    for tc in range(NCH):
        for hp in range(HA // 2):
            emit_headpair(tc, hp)
            if hp == 3 and tc >= 1:
                emit_ln_tail(tc - 1)
        emit_wo_stats(tc)
        if tc + 1 < NCH:
            emit_castq(tc + 1)
    emit_ln_tail(NCH - 1)


def ffn_layer(ctx, l, fqkv_d, fwo_d, fw1_d, fw2_d, rot_d, base, fin=None):
    B = ctx["B"]
    ar, xt, bank, ps = ctx["ar"], ctx["xt"], ctx["bank"], ctx["ps"]
    C, CR, FBR = ctx["C"], ctx["CR"], ctx["FBR"]
    S0, S1, S2 = ctx["S0"], ctx["S1"], ctx["S2"]
    S0b, S1b = ctx["S0b"], ctx["S1b"]
    PT = ctx["PT"]
    XB, SQ = ctx["XB"], ctx["SQ"]
    TF0, TF1, TF2 = ctx["TF0"], ctx["TF1"], ctx["TF2"]
    NEGM, NEGMR, NEGMR2, BCOL = ctx["NEGM"], ctx["NEGMR"], ctx["NEGMR2"], ctx["BCOL"]
    ROW_AB, ROW_BB = ctx["ROW_AB"], ctx["ROW_BB"]
    QS2, KS2 = ctx["QS2"], ctx["KS2"]
    LNG, LNB, B2C, B1A, B1G = ctx["LNG"], ctx["LNB"], ctx["B2C"], ctx["B1A"], ctx["B1G"]
    L = f"L{l}"

    o_qT = base
    o_kT = o_qT + 8192
    o_v = o_kT + 8192
    o_hT = o_v + 8320
    o_wqkv = o_hT + 8192
    o_rot = o_wqkv + 12288
    end_f1 = o_rot + 8192
    assert end_f1 <= ARN, end_f1
    o_oT = o_hT
    o_wo = o_wqkv
    o_w1 = o_wqkv + 4096
    o_f1s = end_f1
    F1SQ = [ar[:, o_f1s + i * 512: o_f1s + (i + 1) * 512] for i in range(2)]
    F1QNB = [ar[:, o_f1s + 1024 + i * 512: o_f1s + 1024 + (i + 1) * 512] for i in range(2)]
    F1SCR = ar[:, o_f1s + 2048: o_f1s + 2048 + 2048].bitcast(F32)
    F1TA = [ar[:, o_f1s + 4096 + i * 1024: o_f1s + 4096 + (i + 1) * 1024].bitcast(F32)
            for i in range(2)]
    F1TB = [ar[:, o_f1s + 6144 + i * 1024: o_f1s + 6144 + (i + 1) * 1024].bitcast(F32)
            for i in range(2)]
    assert o_f1s + 8192 <= ARN, o_f1s + 8192
    o_w2 = o_qT
    o_h2n = o_hT
    o_gg = o_qT + 22528
    u_slots = ([o_qT + 23040] +
               [o_wo + i * 1024 for i in range(4)] +
               [o_wqkv + 8192 + i * 1024 for i in range(2, 4)] +
               [o_rot + i * 1024 for i in range(8)] +
               [end_f1 + i * 1024 for i in range(5)] +
               [o_f1s + 6144, o_f1s + 7168])
    u_slots = [s for s in u_slots if s + 1024 <= ARN][:NHC]
    assert len(u_slots) == NHC, len(u_slots)

    def _slot_alias(s):
        if o_qT <= s < o_qT + 24704:
            return "r_v" if s >= o_v else ("r_kT" if s >= o_kT else "r_qT")
        if o_wo <= s < o_wo + 4096:
            return "r_wqkv"
        if o_wqkv <= s < o_wqkv + 12288:
            return "r_wqspare"
        if o_rot <= s < o_rot + 8192:
            return "r_rot"
        return "r_tail"
    u_alias = [_slot_alias(s) for s in u_slots]

    qT_t = lambda f: ar[:, o_qT + f * 2048: o_qT + (f + 1) * 2048]
    kTf_t = lambda f: ar[:, o_kT + f * 2048: o_kT + (f + 1) * 2048]
    vf_t = lambda tt, h: ar[:, o_v + tt * 520 + h * 65: o_v + tt * 520 + h * 65 + 65]
    hT_t = lambda dc: ar[:, o_hT + dc * 1024: o_hT + (dc + 1) * 1024]
    wqkv_t = lambda dc, f0, nn: ar[:, o_wqkv + dc * 1536 + f0: o_wqkv + dc * 1536 + f0 + nn]
    ROTV = lambda tab: ar[:, o_rot + tab * 2048: o_rot + (tab + 1) * 2048].bitcast(F32)
    oT_t = lambda f: ar[:, o_oT + f * 2048: o_oT + (f + 1) * 2048]
    wo_tl = lambda ic, dc: ar[:, o_wo + ic * 1024 + dc * 128: o_wo + ic * 1024 + (dc + 1) * 128]
    h2n_t = lambda dc: ar[:, o_h2n + dc * 1024: o_h2n + (dc + 1) * 1024]
    u_t = lambda c: ar[:, u_slots[c]: u_slots[c] + 1024]
    w2_tl = lambda c, dc: ar[:, o_w2 + c * 1024 + dc * 128: o_w2 + c * 1024 + (dc + 1) * 128]
    GG = ar[:, o_gg:o_gg + 512]

    def vones(eng):
        vv = ar[:, o_v:o_v + 8320].rearrange("p (t h c) -> p t h c", t=TT, h=FH)
        return eng.memset(vv[:, :, :, 64:65], 1.0)
    B.emit("V", vones, writes=tuple(f"fv.{tt}" for tt in range(TT)), tag=f"{L}vones")

    for dc in range(DC):
        def ld(eng, dc=dc):
            return eng.dma_start(out=wqkv_t(dc, 0, 1536),
                                 in_=fqkv_d[l, dc * 128:(dc + 1) * 128, :])
        B.emit("G", ld, dma="g_fqkv", writes=(f"fqkv.{dc}", "r_wqkv", "r_w1b0", "r_w1b1", "r_w1b2", "r_wqspare"), tag=f"{L}ldqkv{dc}")

    for hh in range(2):
        hc0 = hh * 1024
        for tab in range(4):
            def ldr(eng, tab=tab, hc0=hc0):
                return eng.dma_start(out=ROTV(tab)[:, 0:1024],
                                     in_=rot_d[tab, :, hc0:hc0 + 1024])
            B.emit("Q", ldr, dma="q_rot", writes=(f"rot.{tab}", "r_rot"),
                   tag=f"{L}ldrot{hh}.{tab}")
        for dc in range(DC):
            if dc < 4:
                def cp(eng, dc=dc, hc0=hc0):
                    return eng.tensor_copy(out=hT_t(dc),
                                           in_=xt[:, dc * T + hc0: dc * T + hc0 + 1024])
                B.emit("V", cp, reads=(f"xt.{dc}",), writes=(f"hT.{dc}", "r_hT"),
                       tag=f"{L}h{hh}.{dc}")
            elif hh == 0:
                def cp(eng, dc=dc):
                    return eng.tensor_copy(out=hT_t(dc)[:, 1:1024],
                                           in_=xt[:, dc * T: dc * T + 1023])
                B.emit("V", cp, reads=(f"xt.{dc}",), writes=(f"hT.{dc}", "r_hT"),
                       tag=f"{L}h{hh}.{dc}")
                B.emit("V", lambda eng, dc=dc: eng.memset(hT_t(dc)[:, 0:1], 0.0),
                       writes=(f"hT.{dc}", "r_hT"), tag=f"{L}h0z.{dc}")
            else:
                def cp(eng, dc=dc, hc0=hc0):
                    return eng.tensor_copy(
                        out=hT_t(dc), in_=xt[:, dc * T + hc0 - 1: dc * T + hc0 + 1023])
                B.emit("V", cp, reads=(f"xt.{dc}",), writes=(f"hT.{dc}", "r_hT"),
                       tag=f"{L}h{hh}.{dc}")
        def f1mm(t, hh=hh):
            qk, f = t // 4, t % 4
            fcol = qk * 512 + f * 128
            b0 = 0 if t % 2 == 0 else 6
            for ch in range(2):
                for dc in range(DC):
                    def mm(eng, dc=dc, fcol=fcol, ch=ch, b0=b0):
                        return eng.matmul(
                            out=bank(b0 + ch), lhsT=wqkv_t(dc, fcol, 128),
                            rhs=hT_t(dc)[:, ch * 512:(ch + 1) * 512],
                            start=(dc == 0), stop=(dc == DC - 1))
                    B.emit("T", mm, reads=(f"fqkv.{dc}", f"hT.{dc}", "r_wqkv", "r_wqspare",
                                           "r_w1b0", "r_w1b1", "r_w1b2", "r_hT"),
                           writes=(f"ps{b0 + ch}",), tag=f"{L}qk{qk}.{hh}.{f}.{ch}.{dc}")

        def f1post(t, hh=hh, hc0=hc0):
            qk, f = t // 4, t % 4
            b0 = 0 if t % 2 == 0 else 6
            CHR = []
            for ch in range(2):
                CHR.append(dict(
                    bk=b0 + ch,
                    qraw=(TF0 if ch == 0 else TF1)[:, 0:512],
                    kqraw="tf0" if ch == 0 else "tf1",
                    sq=F1SQ[ch], ksq=f"f1sq{ch}",
                    ss=(S0b if ch == 0 else F1SCR[0:2, 0:512]),
                    kss="lnS0" if ch == 0 else "f1ss1",
                    rn=(S1b if ch == 0 else F1SCR[0:2, 512:1024]),
                    krn="lnS1" if ch == 0 else "f1rn1",
                    qnb=F1QNB[ch], kqnb=f"f1qnb{ch}",
                    ta=F1TA[ch][:, 0:512], kta=f"f1ta{ch}",
                    tb=F1TB[ch][:, 0:512], ktb=f"f1tb{ch}",
                ))
            scol = QS2(l) if qk == 0 else KS2(l)
            bias = LN8 if qk == 0 else 0.0
            tcos = 0 if qk == 0 else 2
            tsin = 1 if qk == 0 else 3
            steps = []
            for ch in range(2):
                R = CHR[ch]
                bk = R["bk"]
                st_ = []
                st_.append(("V", lambda eng, R=R, bk=bk: eng.tensor_copy(
                    out=R["qraw"], in_=bank(bk)), (f"ps{bk}",), (R["kqraw"],), "qe"))
                st_.append(("S", lambda eng, R=R: eng.activation(
                    out=R["sq"], in_=R["qraw"], func=AF.Square),
                    (R["kqraw"],), (R["ksq"], "r_tail"), "sq"))
                st_.append(("T", lambda eng, R=R, bk=bk: eng.matmul(
                    out=bank(bk, 0, 512, 0, 2), lhsT=C["HSEL2"], rhs=R["sq"],
                    start=True, stop=True),
                    (R["ksq"],) + CR, (f"ps{bk}",), "ssm"))
                st_.append(("V", lambda eng, R=R, bk=bk: eng.tensor_scalar(
                    out=R["ss"], in0=bank(bk, 0, 512, 0, 2), scalar1=1e-24,
                    scalar2=None, op0=ALU.max),
                    (f"ps{bk}",), (R["kss"],), "cl"))
                st_.append(("S", lambda eng, R=R: eng.activation(
                    out=R["ss"], in_=R["ss"], func=AF.Ln),
                    (R["kss"],), (R["kss"],), "ln"))
                st_.append(("S", lambda eng, R=R, bias=bias: eng.activation(
                    out=R["rn"], in_=R["ss"], func=AF.Exp, scale=-0.5, bias=bias),
                    (R["kss"],), (R["krn"],), "ex"))
                st_.append(("T", lambda eng, R=R, bk=bk: eng.matmul(
                    out=bank(bk), lhsT=C["BC2F"], rhs=R["rn"], start=True, stop=True),
                    (R["krn"],) + CR, (f"ps{bk}",), "rb"))
                st_.append(("V", lambda eng, R=R, bk=bk, scol=scol: eng.scalar_tensor_tensor(
                    out=R["ta"], in0=R["qraw"], scalar=scol, in1=bank(bk),
                    op0=ALU.mult, op1=ALU.mult),
                    (R["kqraw"], f"ps{bk}") + FBR, (R["kta"], "r_tail"), "qn"))
                st_.append(("V", lambda eng, R=R: eng.tensor_copy(
                    out=R["qnb"], in_=R["ta"]),
                    (R["kta"],), (R["kqnb"], "r_tail"), "qb"))
                st_.append(("T", lambda eng, R=R, bk=bk: eng.matmul(
                    out=bank(bk), lhsT=C["PERM"], rhs=R["qnb"], start=True, stop=True),
                    (R["kqnb"],) + CR, (f"ps{bk}",), "pm"))
                st_.append(("G", lambda eng, R=R, tcos=tcos, ch=ch: eng.tensor_tensor(
                    out=R["ta"], in0=R["ta"],
                    in1=ROTV(tcos)[:, ch * 512:(ch + 1) * 512], op=ALU.mult),
                    (R["kta"], f"rot.{tcos}", "r_rot"), (R["kta"],), "rc"))
                st_.append(("V", lambda eng, R=R, tsin=tsin, ch=ch, bk=bk: eng.tensor_tensor(
                    out=R["tb"], in0=bank(bk),
                    in1=ROTV(tsin)[:, ch * 512:(ch + 1) * 512], op=ALU.mult),
                    (f"ps{bk}", f"rot.{tsin}", "r_rot"), (R["ktb"], "r_tail"), "rs"))
                dst = (qT_t(f) if qk == 0 else kTf_t(f))[:, hc0 + ch * 512: hc0 + (ch + 1) * 512]
                dk = f"fqT.{f}" if qk == 0 else f"fkT.{f}"
                st_.append(("G", lambda eng, R=R, dst=dst: eng.tensor_tensor(
                    out=dst, in0=R["ta"], in1=R["tb"], op=ALU.add),
                    (R["kta"], R["ktb"]), (dk, "r_qT" if qk == 0 else "r_kT"), "ra"))
                steps.append(st_)
            for i in range(len(steps[0])):
                for ch in range(2):
                    eng, fn, rd, wr, nm = steps[ch][i]
                    B.emit(eng, fn, reads=rd, writes=wr, tag=f"{L}{nm}{qk}.{hh}.{f}.{ch}")

        f1mm(0)
        for t in range(1, 8):
            f1mm(t)
            f1post(t - 1)
        f1post(7)
        for ttl in range(8):
            tt = hh * 8 + ttl
            for dc in range(DC):
                def mm(eng, dc=dc, ttl=ttl):
                    return eng.matmul(
                        out=bank(5), lhsT=hT_t(dc)[:, ttl * 128:(ttl + 1) * 128],
                        rhs=wqkv_t(dc, 1024, 512), start=(dc == 0), stop=(dc == DC - 1))
                B.emit("T", mm, reads=(f"fqkv.{dc}", f"hT.{dc}", "r_wqkv", "r_wqspare",
                                        "r_w1b0", "r_w1b1", "r_hT"),
                       writes=("ps5",), tag=f"{L}v.{tt}.{dc}")

            def vev(eng, tt=tt):
                b0 = o_v + tt * 520
                dst = ar[:, b0: b0 + 520].rearrange("p (h c) -> p h c", h=FH)
                return eng.tensor_copy(out=dst[:, :, 0:64], in_=bank(5))
            B.emit("V", vev, reads=("ps5",), writes=(f"fv.{tt}", "r_v"), tag=f"{L}ve.{tt}")

    for ic in range(4):
        def ld(eng, ic=ic):
            return eng.dma_start(out=ar[:, o_wo + ic * 1024: o_wo + (ic + 1) * 1024],
                                 in_=fwo_d[l, ic * 128:(ic + 1) * 128, :])
        B.emit("G", ld, dma="g_fwo", writes=(f"fwo.{ic}", "r_wqkv"), tag=f"{L}ldwo{ic}")

    negmA = NEGM[:, 0:4]
    negmB = NEGM[:, 4:8]
    negmrA = NEGMR
    negmrB = NEGMR2
    PTF = [ar[:, o_f1s + i * 1024: o_f1s + (i + 1) * 1024] for i in range(2)]
    PTFK = [("f2pt0a", "f2pt0b"), ("f2pt1a", "f2pt1b")]
    RD2 = ctx["SCR01"]
    RD2B = ctx["RDENB"]
    HW = [(hp, w) for w in range(4) for hp in range(FH // 2)]

    def emitS1p(pi):
        hp, w = HW[pi]
        fq = hp
        SIDE = [(2 * hp, 0, 0), (2 * hp + 1, 64, 6)]

        def s1stage(r, side):
            h, pr, bp = SIDE[side]
            ti = 4 * w + r
            tj0 = max(0, ti - 4)
            njt = ti - tj0 + 1
            nj = njt * 128
            c0 = bp * 512
            segs = [(0, min(nj, 512))] + ([(512, nj)] if nj > 512 else [])
            bkeys = (f"ps{bp}",) if nj <= 512 else (f"ps{bp}", f"ps{bp + 1}")
            dpos = (ti - tj0) * 128
            for (s0, s1) in segs:
                def mm(eng, fq=fq, pr=pr, ti=ti, tj0=tj0, s0=s0, s1=s1, c0=c0,
                       stp=(s1 != dpos + 128 and not (ti >= 4 and s0 == 0))):
                    return eng.matmul(
                        out=ps[:, c0 + s0:c0 + s1],
                        lhsT=qT_t(fq)[pr:pr + 64, ti * 128:(ti + 1) * 128],
                        rhs=kTf_t(fq)[pr:pr + 64, tj0 * 128 + s0: tj0 * 128 + s1],
                        start=True, stop=stp)
                B.emit("T", mm, reads=(f"fqT.{fq}", f"fkT.{fq}", "r_qT", "r_kT"),
                       writes=bkeys, tag=f"{L}s1.{h}.{w}.{r}")
            def mskd(eng, dpos=dpos, c0=c0):
                return eng.matmul(out=ps[:, c0 + dpos:c0 + dpos + 128],
                                  lhsT=C["IDENT"], rhs=C["MTF"], start=False,
                                  stop=True)
            B.emit("T", mskd, reads=CR, writes=bkeys, tag=f"{L}s1d.{h}.{w}.{r}")
            if ti >= 4:
                def mskf(eng, c0=c0):
                    return eng.matmul(out=ps[:, c0:c0 + 128], lhsT=C["IDENT"],
                                      rhs=C["MTC"], start=False, stop=True)
                B.emit("T", mskf, reads=CR, writes=bkeys, tag=f"{L}s1f.{h}.{w}.{r}")

        def s1red(r, side):
            h, pr, bp = SIDE[side]
            ti = 4 * w + r
            nj = (ti - max(0, ti - 4) + 1) * 128
            bkeys = (f"ps{bp}",) if nj <= 512 else (f"ps{bp}", f"ps{bp + 1}")
            ng = negmA if side == 0 else negmB
            ngk = "negmA" if side == 0 else "negmB"

            def red(eng, nj=nj, r=r, bp=bp, ng=ng):
                return eng.tensor_reduce(out=ng[:, r:r + 1],
                                         in_=ps[:, bp * 512:bp * 512 + nj],
                                         axis=AX.X, op=ALU.max, negate=True)
            B.emit("V", red, reads=bkeys, writes=(ngk,), tag=f"{L}s1r.{h}.{w}.{r}")

        s1stage(0, 0)
        s1stage(0, 1)
        for r in range(1, 4):
            s1stage(r, 0)
            s1stage(r, 1)
            s1red(r - 1, 0)
            s1red(r - 1, 1)
        s1red(3, 0)
        s1red(3, 1)

    def emitS2p(pi):
        hp, w = HW[pi]
        fq = hp
        icols0 = w * 512
        SIDE = [(2 * hp, 0), (2 * hp + 1, 64)]
        B.emit("T", lambda eng: eng.transpose(out=bank(2, 0, 128, 0, 4), in_=negmA,
               identity=C["IDF"]),
               reads=("negmA",) + CR, writes=("ps2",), tag=f"{L}ntrA.{hp}.{w}")
        B.emit("T", lambda eng: eng.transpose(out=bank(2, 128, 256, 0, 4), in_=negmB,
               identity=C["IDF"]),
               reads=("negmB",) + CR, writes=("ps2",), tag=f"{L}ntrB.{hp}.{w}")
        B.emit("V", lambda eng: eng.tensor_copy(out=negmrA, in_=bank(2, 0, 128, 0, 4)),
               reads=("ps2",), writes=("negmrA",), tag=f"{L}nevA.{hp}.{w}")
        B.emit("V", lambda eng: eng.tensor_copy(out=negmrB, in_=bank(2, 128, 256, 0, 4)),
               reads=("ps2",), writes=("negmrB",), tag=f"{L}nevB.{hp}.{w}")
        tj0w = max(0, 4 * w - 4)
        tj1w = 4 * w + 3

        def s2stage(tj):
            zs = []
            tp = tj - 4 * w
            ka = max(0, tp)
            kb = min(3, tp + 4)
            masks = []
            if 0 <= tp <= 3:
                masks.append((tp, "MTC"))
            if 0 <= tp + 4 <= 3:
                masks.append((tp + 4, "MTF"))
            for side in range(2):
                h, pr = SIDE[side]
                sbk = 2 + side

                def mm(eng, tj=tj, fq=fq, pr=pr, icols0=icols0, sbk=sbk):
                    return eng.matmul(
                        out=bank(sbk),
                        lhsT=kTf_t(fq)[pr:pr + 64, tj * 128:(tj + 1) * 128],
                        rhs=qT_t(fq)[pr:pr + 64, icols0: icols0 + 512],
                        start=True, stop=False)
                B.emit("T", mm, reads=(f"fqT.{fq}", f"fkT.{fq}", "r_qT", "r_kT"),
                       writes=(f"ps{sbk}",), tag=f"{L}s2.{h}.{w}.{tj}")
            for side in range(2):
                h, pr = SIDE[side]
                sbk = 2 + side
                ngr = negmrA if side == 0 else negmrB
                ngrk = "negmrA" if side == 0 else "negmrB"
                for r in range(ka, kb + 1):
                    def selm(eng, r=r, sbk=sbk, ngr=ngr, stp=(r == kb)):
                        return eng.matmul(out=bank(sbk, r * 128, (r + 1) * 128),
                                          lhsT=C["SEL4B"][:, r * 128:(r + 1) * 128],
                                          rhs=ngr, start=False, stop=stp)
                    B.emit("T", selm, reads=(ngrk,) + CR, writes=(f"ps{sbk}",),
                           tag=f"{L}s2m.{h}.{w}.{tj}.{r}")
            pt = PTF[tj % 2]
            ptk = PTFK[tj % 2]
            for side in range(2):
                def ex(eng, pt=pt, side=side):
                    return eng.activation(out=pt[:, side * 512:(side + 1) * 512],
                                          in_=bank(2 + side, 0, 512), func=AF.Exp)
                B.emit("S", ex, reads=(f"ps{2 + side}",),
                       writes=(ptk[side], "r_tail"),
                       tag=f"{L}s2e.{hp}.{w}.{tj}.{side}")
                for (r, mk) in masks:
                    tri = "TRIC" if mk == "MTC" else "TRIF"

                    def vmask(eng, pt=pt, side=side, r=r, tri=tri):
                        sl = pt[:, side * 512 + r * 128: side * 512 + (r + 1) * 128]
                        return eng.tensor_tensor(out=sl, in0=sl, in1=C[tri],
                                                 op=ALU.mult)
                    B.emit("V", vmask, reads=(ptk[side],) + CR,
                           writes=(ptk[side], "r_tail"),
                           tag=f"{L}s2vm.{hp}.{w}.{tj}.{side}.{r}")
                if ka > 0:
                    B.emit("V", lambda eng, pt=pt, side=side, ka=ka: eng.memset(
                        pt[:, side * 512:side * 512 + ka * 128], 0.0),
                        writes=(ptk[side], "r_tail"), tag=f"{L}pz0.{hp}.{w}.{tj}.{side}")
                if kb < 3:
                    B.emit("V", lambda eng, pt=pt, side=side, kb=kb: eng.memset(
                        pt[:, side * 512 + (kb + 1) * 128:side * 512 + 512], 0.0),
                        writes=(ptk[side], "r_tail"), tag=f"{L}pz1.{hp}.{w}.{tj}.{side}")

        def s2pv(tj):
            pt = PTF[tj % 2]
            ptk = PTFK[tj % 2]
            for side in range(2):
                h, pr = SIDE[side]
                nb = 4 + side

                def pv(eng, tj=tj, h=h, pt=pt, side=side, nb=nb,
                       first=(tj == tj0w), last=(tj == tj1w)):
                    return eng.matmul(out=bank(nb, 0, 512, 0, 65),
                                      lhsT=vf_t(tj, h),
                                      rhs=pt[:, side * 512:(side + 1) * 512],
                                      start=first, stop=last)
                B.emit("T", pv, reads=(f"fv.{tj}", ptk[side], "r_v"),
                       writes=(f"ps{nb}",), tag=f"{L}pv.{h}.{w}.{tj}")

        s2stage(tj0w)
        for k, tj in enumerate(range(tj0w + 1, tj1w + 1)):
            s2stage(tj)
            s2pv(tj - 1)
            if k == 2 and pi + 1 < len(HW):
                emitS1p(pi + 1)
        s2pv(tj1w)
        RDK = ("lnS0", "lnS1")
        i_ln = B.emit("S", lambda eng: eng.activation(
            out=RD2, in_=bank(4, 0, 1024, 64, 65), func=AF.Ln),
            reads=("ps4", "ps5"), writes=RDK, tag=f"{L}ld.{hp}.{w}")
        B.emit("S", lambda eng: eng.activation(
            out=RD2B, in_=RD2, func=AF.Exp, scale=-1.0),
            reads=RDK, writes=RDK + ("lnS2",), tag=f"{L}rd.{hp}.{w}")
        for side in range(2):
            h, pr = SIDE[side]
            nb = 4 + side
            rb = 2 + side
            B.emit("V", lambda eng, nb=nb: eng.tensor_copy(out=TF2[0:64, 0:512],
                   in_=bank(nb, 0, 512, 0, 64)),
                   reads=(f"ps{nb}",), writes=("tf2",), tag=f"{L}ne.{h}.{w}",
                   extra_deps=(i_ln,))
            B.emit("T", lambda eng, rb=rb, side=side: eng.matmul(
                out=bank(rb, 0, 512, 0, 64), lhsT=C["ONES1B"][:, 0:64],
                rhs=RD2B[:, side * 512:(side + 1) * 512], start=True, stop=True),
                reads=RDK + ("lnS2",) + CR, writes=(f"ps{rb}",), tag=f"{L}rb.{h}.{w}")

            def oev(eng, fq=fq, pr=pr, icols0=icols0, rb=rb):
                return eng.tensor_tensor(
                    out=oT_t(fq)[pr:pr + 64, icols0:icols0 + 512],
                    in0=TF2[0:64, 0:512], in1=bank(rb, 0, 512, 0, 64), op=ALU.mult)
            B.emit("V", oev, reads=("tf2", f"ps{rb}"), writes=(f"oT.{fq}", "r_hT"),
                   tag=f"{L}oe.{h}.{w}")

    emitS1p(0)
    for pi in range(len(HW)):
        emitS2p(pi)

    def w1load(hh, c):
        buf = c % 3
        hcs = HCS[c]

        def lda(eng, c=c, buf=buf):
            dst = ar[:, o_w1 + buf * 2048: o_w1 + buf * 2048 + 1024]
            return eng.dma_start(out=dst, in_=fw1_d[l, c])
        B.emit("G", lda, dma=f"g_w1p{buf}", writes=(f"w1a.{buf}", f"r_w1b{buf}"),
               tag=f"{L}lw1a.{hh}.{c}")

        def ldg(eng, c=c, buf=buf):
            dst = ar[:, o_w1 + buf * 2048 + 1024: o_w1 + buf * 2048 + 2048]
            return eng.dma_start(out=dst, in_=fw1_d[l, NHC + c])
        B.emit("G", ldg, dma=f"g_w1p{buf}", writes=(f"w1g.{buf}", f"r_w1b{buf}"),
               tag=f"{L}lw1g.{hh}.{c}")

    def w2load(c):
        def ld(eng, c=c):
            return eng.dma_start(
                out=ar[0:HCS[c], o_w2 + c * 1024: o_w2 + (c + 1) * 1024],
                in_=fw2_d[l, c * 128: c * 128 + HCS[c], :])
        B.emit("G", ld, dma="g_w2",
               writes=(f"fw2.{c}", "r_qT", "r_kT", "r_v"), tag=f"{L}ldw2.{c}")

    for c in range(3):
        w1load(0, c)
    for c in range(11):
        w2load(c)

    for tc in range(NCH):
        for dc in range(DC):
            bk = 6 + dc % 2
            for ic in range(4):
                def mm(eng, tc=tc, dc=dc, ic=ic, bk=bk):
                    return eng.matmul(out=bank(bk), lhsT=wo_tl(ic, dc),
                                      rhs=oT_t(ic)[:, tc * 512:(tc + 1) * 512],
                                      start=(ic == 0), stop=(ic == 3))
                B.emit("T", mm, reads=(f"fwo.{ic}", f"oT.{ic}", "r_wqkv", "r_hT"),
                       writes=(f"ps{bk}",), tag=f"{L}wo.{tc}.{dc}.{ic}")

            def res(eng, tc=tc, dc=dc, bk=bk):
                sl = xt[:, dc * T + tc * 512: dc * T + (tc + 1) * 512]
                return eng.tensor_tensor(out=sl, in0=sl, in1=bank(bk), op=ALU.add)
            B.emit("V", res, reads=(f"ps{bk}", f"xt.{dc}"), writes=(f"xt.{dc}",),
                   tag=f"{L}wr.{tc}.{dc}")


    for dc in range(DC):
        def bc(eng, dc=dc):
            return eng.tensor_copy(out=BCOL[:, dc:dc + 1],
                                   in_=xt[:, dc * T + 1023: dc * T + 1024])
        B.emit("V", bc, reads=(f"xt.{dc}",), writes=("bcol",), tag=f"{L}bc.{dc}")

    for hh in range(2):
        hc0 = hh * 1024
        for ch in range(2):
            c0 = hc0 + ch * 512
            for dc in range(DC):
                shifted = dc >= 4
                if not shifted:
                    def cst(eng, dc=dc, c0=c0):
                        return eng.tensor_copy(out=XB,
                                               in_=xt[:, dc * T + c0: dc * T + c0 + 512])
                    B.emit("V", cst, reads=(f"xt.{dc}",), writes=("xb",),
                           tag=f"{L}3c.{hh}.{ch}.{dc}")
                elif c0 == 0:
                    def cst(eng, dc=dc):
                        return eng.tensor_copy(out=XB[:, 1:512],
                                               in_=xt[:, dc * T: dc * T + 511])
                    B.emit("V", cst, reads=(f"xt.{dc}",), writes=("xb",),
                           tag=f"{L}3c.{hh}.{ch}.{dc}")
                    B.emit("V", lambda eng: eng.memset(XB[:, 0:1], 0.0),
                           writes=("xb",), tag=f"{L}3cz.{hh}.{ch}.{dc}")
                elif hh == 1 and ch == 0:
                    def cst(eng, dc=dc, c0=c0):
                        return eng.tensor_copy(out=XB[:, 1:512],
                                               in_=xt[:, dc * T + c0: dc * T + c0 + 511])
                    B.emit("V", cst, reads=(f"xt.{dc}",), writes=("xb",),
                           tag=f"{L}3c.{hh}.{ch}.{dc}")
                    B.emit("V", lambda eng, dc=dc: eng.tensor_copy(
                        out=XB[:, 0:1].bitcast(F32) if False else XB[:, 0:1],
                        in_=BCOL[:, dc:dc + 1]),
                        reads=("bcol",), writes=("xb",), tag=f"{L}3cb.{hh}.{ch}.{dc}")
                else:
                    def cst(eng, dc=dc, c0=c0):
                        return eng.tensor_copy(
                            out=XB, in_=xt[:, dc * T + c0 - 1: dc * T + c0 + 511])
                    B.emit("V", cst, reads=(f"xt.{dc}",), writes=("xb",),
                           tag=f"{L}3c.{hh}.{ch}.{dc}")
                B.emit("S", lambda eng: eng.activation(out=SQ, in_=XB, func=AF.Square),
                       reads=("xb",), writes=("sq",), tag=f"{L}3q.{hh}.{ch}.{dc}")

                def s1(eng, dc=dc):
                    return eng.matmul(out=bank(6, 0, 512, 0, 1), lhsT=C["ONESC"],
                                      rhs=XB, start=(dc == 0), stop=(dc == DC - 1))
                B.emit("T", s1, reads=("xb",) + CR, writes=("ps6",),
                       tag=f"{L}31.{hh}.{ch}.{dc}")

                def s2(eng, dc=dc):
                    return eng.matmul(out=bank(7, 0, 512, 0, 1), lhsT=C["ONESC"],
                                      rhs=SQ, start=(dc == 0), stop=(dc == DC - 1))
                B.emit("T", s2, reads=("sq",) + CR, writes=("ps7",),
                       tag=f"{L}32.{hh}.{ch}.{dc}")
            B.emit("V", lambda eng: eng.tensor_scalar(out=S0,
                   in0=bank(6, 0, 512, 0, 1), scalar1=1.0 / D, scalar2=None,
                   op0=ALU.mult), reads=("ps6",), writes=("lnS0",),
                   tag=f"{L}3mu.{hh}.{ch}")
            B.emit("V", lambda eng: eng.tensor_scalar(out=S1,
                   in0=bank(7, 0, 512, 0, 1), scalar1=1.0 / D, scalar2=None,
                   op0=ALU.mult), reads=("ps7",), writes=("lnS1",),
                   tag=f"{L}3ms.{hh}.{ch}")
            B.emit("V", lambda eng: eng.tensor_tensor(
                out=S2, in0=S0, in1=S0, op=ALU.mult),
                reads=("lnS0",), writes=("lnS2",), tag=f"{L}3v1.{hh}.{ch}")
            B.emit("V", lambda eng: eng.tensor_tensor(
                out=S2, in0=S1, in1=S2, op=ALU.subtract),
                reads=("lnS1", "lnS2"), writes=("lnS2",), tag=f"{L}3v2.{hh}.{ch}")
            B.emit("S", lambda eng: eng.activation(out=S2, in_=S2,
                   func=AF.Ln, bias=LN_EPS), reads=("lnS2",), writes=("lnS2",),
                   tag=f"{L}3l.{hh}.{ch}")
            B.emit("S", lambda eng: eng.activation(
                out=S1, in_=S2, func=AF.Exp, scale=-0.5),
                reads=("lnS2",), writes=("lnS1",), tag=f"{L}3e.{hh}.{ch}")
            B.emit("V", lambda eng: eng.scalar_tensor_tensor(
                out=S0, in0=S0, scalar=-1.0, in1=S1, op0=ALU.mult, op1=ALU.mult),
                reads=("lnS0", "lnS1"), writes=("lnS0",), tag=f"{L}3B.{hh}.{ch}")
            B.emit("T", lambda eng: eng.matmul(out=bank(6), lhsT=C["ONES1F"],
                   rhs=S1, start=True, stop=True), reads=("lnS1",) + CR,
                   writes=("ps6",), tag=f"{L}3bA.{hh}.{ch}")
            B.emit("T", lambda eng: eng.matmul(out=bank(7), lhsT=C["ONES1F"],
                   rhs=S0, start=True, stop=True), reads=("lnS0",) + CR,
                   writes=("ps7",), tag=f"{L}3bB.{hh}.{ch}")
            for dc in range(DC):
                shifted = dc >= 4
                if not shifted:
                    def a1(eng, dc=dc, c0=c0):
                        return eng.tensor_tensor(
                            out=TF0[:, 0:512], in0=xt[:, dc * T + c0: dc * T + c0 + 512],
                            in1=bank(6), op=ALU.mult)
                    B.emit("V", a1, reads=(f"xt.{dc}", "ps6"), writes=("tf0",),
                           tag=f"{L}3x1.{hh}.{ch}.{dc}")
                elif c0 == 0:
                    def a1(eng, dc=dc):
                        return eng.tensor_tensor(
                            out=TF0[:, 1:512], in0=xt[:, dc * T: dc * T + 511],
                            in1=bank(6, 1, 512), op=ALU.mult)
                    B.emit("V", a1, reads=(f"xt.{dc}", "ps6"), writes=("tf0",),
                           tag=f"{L}3x1.{hh}.{ch}.{dc}")
                    B.emit("V", lambda eng: eng.tensor_tensor(
                        out=TF0[:, 0:1], in0=C["ZCOL"], in1=bank(6, 0, 1), op=ALU.mult),
                        reads=CR + ("ps6",), writes=("tf0",), tag=f"{L}3xz.{hh}.{ch}.{dc}")
                elif hh == 1 and ch == 0:
                    def a1(eng, dc=dc, c0=c0):
                        return eng.tensor_tensor(
                            out=TF0[:, 1:512], in0=xt[:, dc * T + c0: dc * T + c0 + 511],
                            in1=bank(6, 1, 512), op=ALU.mult)
                    B.emit("V", a1, reads=(f"xt.{dc}", "ps6"), writes=("tf0",),
                           tag=f"{L}3x1.{hh}.{ch}.{dc}")
                    B.emit("V", lambda eng, dc=dc: eng.tensor_tensor(
                        out=TF0[:, 0:1], in0=BCOL[:, dc:dc + 1], in1=bank(6, 0, 1),
                        op=ALU.mult),
                        reads=("bcol", "ps6"), writes=("tf0",), tag=f"{L}3xb.{hh}.{ch}.{dc}")
                else:
                    def a1(eng, dc=dc, c0=c0):
                        return eng.tensor_tensor(
                            out=TF0[:, 0:512],
                            in0=xt[:, dc * T + c0 - 1: dc * T + c0 + 511],
                            in1=bank(6), op=ALU.mult)
                    B.emit("V", a1, reads=(f"xt.{dc}", "ps6"), writes=("tf0",),
                           tag=f"{L}3x1.{hh}.{ch}.{dc}")
                B.emit("V", lambda eng: eng.tensor_tensor(out=TF1[:, 0:512],
                       in0=TF0[:, 0:512], in1=bank(7), op=ALU.add),
                       reads=("tf0", "ps7"), writes=("tf1",), tag=f"{L}3x2.{hh}.{ch}.{dc}")

                def a3(eng, dc=dc, ch=ch):
                    return eng.tensor_scalar(
                        out=h2n_t(dc)[:, ch * 512:(ch + 1) * 512], in0=TF1[:, 0:512],
                        scalar1=LNG(l, dc), scalar2=LNB(l, dc), op0=ALU.mult,
                        op1=ALU.add)
                B.emit("V", a3, reads=("tf1",) + FBR, writes=(f"h2n.{dc}", "r_hT"),
                       tag=f"{L}3x3.{hh}.{ch}.{dc}")

        if hh == 1:
            for c in range(3):
                w1load(hh, c)
        for c in range(NHC):
            hcs = HCS[c]
            buf = c % 3

            for ch in range(2):
                for dc in range(DC):
                    def mma(eng, dc=dc, ch=ch, buf=buf, hcs=hcs):
                        w = ar[:, o_w1 + buf * 2048 + dc * 128: o_w1 + buf * 2048 + dc * 128 + 128]
                        return eng.matmul(out=bank(ch, 0, 512, 0, hcs),
                                          lhsT=w[:, 0:hcs],
                                          rhs=h2n_t(dc)[:, ch * 512:(ch + 1) * 512],
                                          start=(dc == 0), stop=(dc == DC - 1))
                    B.emit("T", mma, reads=(f"w1a.{buf}", f"h2n.{dc}", "r_hT", f"r_w1b{buf}"),
                           writes=(f"ps{ch}",), tag=f"{L}wa.{hh}.{c}.{ch}.{dc}")

                    def mmg(eng, dc=dc, ch=ch, buf=buf, hcs=hcs):
                        w = ar[:, o_w1 + buf * 2048 + 1024 + dc * 128: o_w1 + buf * 2048 + 1024 + dc * 128 + 128]
                        return eng.matmul(out=bank(2 + ch, 0, 512, 0, hcs),
                                          lhsT=w[:, 0:hcs],
                                          rhs=h2n_t(dc)[:, ch * 512:(ch + 1) * 512],
                                          start=(dc == 0), stop=(dc == DC - 1))
                    B.emit("T", mmg, reads=(f"w1g.{buf}", f"h2n.{dc}", "r_hT", f"r_w1b{buf}"),
                           writes=(f"ps{2 + ch}",), tag=f"{L}wg.{hh}.{c}.{ch}.{dc}")
                def gel(eng, ch=ch, hcs=hcs, c=c):
                    return eng.activation(out=GG[0:hcs, :],
                                          in_=bank(2 + ch, 0, 512, 0, hcs),
                                          func=AF.Gelu, bias=B1G(l, c)[0:hcs, 0:1])
                B.emit("S", gel, reads=(f"ps{2 + ch}",) + FBR, writes=("gg", "r_v"),
                       tag=f"{L}gg.{hh}.{c}.{ch}")

                def umul(eng, c=c, ch=ch, hcs=hcs):
                    return eng.scalar_tensor_tensor(
                        out=u_t(c)[0:hcs, ch * 512:(ch + 1) * 512],
                        in0=bank(ch, 0, 512, 0, hcs), scalar=B1A(l, c)[0:hcs, 0:1],
                        in1=GG[0:hcs, :], op0=ALU.add, op1=ALU.mult)
                B.emit("V", umul, reads=(f"ps{ch}", "gg", "r_v") + FBR,
                       writes=(f"u.{c}", u_alias[c]),
                       tag=f"{L}um.{hh}.{c}.{ch}")
            if c + 3 < NHC:
                w1load(hh, c + 3)
            if hh == 0 and 11 + c < NHC:
                w2load(11 + c)

        for ch in range(2):
            for dc in range(DC):
                bk = 4 + dc % 2
                for c in range(NHC):
                    def mm(eng, dc=dc, ch=ch, c=c, bk=bk):
                        return eng.matmul(out=bank(bk),
                                          lhsT=w2_tl(c, dc)[0:HCS[c], :],
                                          rhs=u_t(c)[0:HCS[c], ch * 512:(ch + 1) * 512],
                                          start=(c == 0), stop=(c == NHC - 1))
                    B.emit("T", mm, reads=(f"fw2.{c}", f"u.{c}", u_alias[c],
                                            "r_qT", "r_kT", "r_v"),
                           writes=(f"ps{bk}",), tag=f"{L}w2.{hh}.{dc}.{ch}.{c}")
                c0 = hc0 + ch * 512

                def res(eng, dc=dc, c0=c0, bk=bk):
                    sl = xt[:, dc * T + c0: dc * T + c0 + 512]
                    return eng.scalar_tensor_tensor(out=sl, in0=bank(bk),
                                                    scalar=B2C(l, dc), in1=sl,
                                                    op0=ALU.add, op1=ALU.add)
                B.emit("V", res, reads=(f"ps{bk}", f"xt.{dc}") + FBR,
                       writes=(f"xt.{dc}",), tag=f"{L}w2r.{hh}.{dc}.{ch}")
            if fin is not None:
                fin(hh * 2 + ch)



def _host_constants():
    cstf = np.zeros((128, 400), np.float32)
    cstf[:, 0:128] = np.eye(128, dtype=np.float32)
    cstf[0, 128:256] = 1.0
    cstf[0, 257:321] = 1.0
    cstf[1, 321:385] = 1.0
    for i, v in enumerate([EPS_A, LN_EPS, LN8, -1.0, -0.5, 1.0, 0.0]):
        cstf[:, 385 + i] = v
    sel = np.zeros((128, 512), np.float32)
    for c in range(4):
        sel[c, c * 128:(c + 1) * 128] = 1.0
    cstb = np.zeros((128, 1664), np.float32)
    cstb[:, 0:128] = np.eye(128, dtype=np.float32)
    idx = np.arange(128)
    cstb[:, 128:256] = np.where(idx[None, :] >= idx[:, None], 0.0, NEG)
    cstb[:, 256:384] = np.where(idx[None, :] <= idx[:, None], 0.0, NEG)
    P = np.zeros((128, 128), np.float32)
    for m in range(128):
        bse = (m // 64) * 64
        r = m % 64
        if r < 32:
            P[bse + r + 32, m] = -1.0
        else:
            P[bse + r - 32, m] = 1.0
    cstb[:, 384:512] = P
    cstb[0:64, 512] = 1.0
    cstb[64:128, 513] = 1.0
    cstb[:, 514] = 1.0
    cstb[:, 515:643] = NEG
    cstb[0:4, 643:1155] = sel[0:4, :]
    cstb[0, 1280:1408] = 1.0
    cstb[:, 1408:1536] = (idx[None, :] >= idx[:, None])
    cstb[:, 1536:1664] = (idx[None, :] <= idx[:, None])
    return cstf, cstb.astype(ml_dtypes.bfloat16)


def _host_rot():
    n = T
    half = np.arange(0, FDH, 2, dtype=np.float64)
    inv_freq = 1.0 / 10000.0 ** (half / FDH)
    t = np.arange(n, dtype=np.float64)
    freqs = t[:, None] * inv_freq[None, :]
    freqs = np.concatenate([freqs, freqs], axis=-1)
    bse = (half + 0.4 * FDH) / (1.4 * FDH)
    power = (t - n // 2) / float(WIN // 2)
    scale = bse[None, :] ** power[:, None]
    scale = np.concatenate([scale, scale], axis=-1)
    cos, sin = np.cos(freqs), np.sin(freqs)
    tabs = [cos * scale, sin * scale, cos / scale, sin / scale]
    out = np.zeros((4, 128, T), np.float32)
    for i, tb in enumerate(tabs):
        tt = tb.T.astype(np.float32)
        out[i, 0:64] = tt
        out[i, 64:128] = tt
    return out


def _host_fb(ins):
    fb = np.zeros((128, 160), np.float32)
    for l in range(2):
        qs = np.asarray(ins["L0_ffn_q_scale"][l], np.float32)
        ks = np.asarray(ins["L0_ffn_k_scale"][l], np.float32)
        fb[0:64, l * 2] = qs
        fb[64:128, l * 2] = qs
        fb[0:64, l * 2 + 1] = ks
        fb[64:128, l * 2 + 1] = ks
        fb[:, 4 + l * 16: 12 + l * 16] = np.asarray(
            ins["L0_ffn_ln_g"][l], np.float32).reshape(8, 128).T
        fb[:, 12 + l * 16: 20 + l * 16] = np.asarray(
            ins["L0_ffn_ln_b"][l], np.float32).reshape(8, 128).T
        fb[:, 36 + l * 8: 44 + l * 8] = np.asarray(
            ins["L0_ffn_b2"][l], np.float32).reshape(8, 128).T
        b1 = np.asarray(ins["L0_ffn_b1"][l], np.float32)
        b1a = np.zeros(22 * 128, np.float32)
        b1a[0:HID] = b1[0:HID]
        b1g = np.zeros(22 * 128, np.float32)
        b1g[0:HID] = b1[HID:]
        fb[:, 52 + l * 43: 52 + l * 43 + 22] = b1a.reshape(22, 128).T
        fb[:, 52 + l * 43 + 22: 52 + l * 43 + 44] = b1g.reshape(22, 128).T
    return fb


def _host_fw1p(w1):
    out = np.zeros((2, 2 * NHC, 128, 1024), np.float32)
    for l in range(2):
        for half in range(2):
            for c in range(NHC):
                hcs = HCS[c]
                blk = w1[l][:, half * HID + c * 128: half * HID + c * 128 + hcs]
                blk = blk.reshape(8, 128, hcs).transpose(1, 0, 2)
                out[l, half * NHC + c, :, :].reshape(128, 8, 128)[:, :, 0:hcs] = blk
    return np.ascontiguousarray(out)


def host_inputs(ins):
    cstf, cstb = _host_constants()
    common = {
        "wqT": np.ascontiguousarray(np.asarray(ins["L0_attn_Wq"]).T),
        "wkvT": np.ascontiguousarray(np.asarray(ins["L0_attn_Wkv"]).T),
        "woT": np.ascontiguousarray(np.asarray(ins["L0_attn_Wo"]).T),
        "fqkv": np.ascontiguousarray(np.asarray(ins["L0_ffn_wqkv"], np.float32)),
        "fwo": np.ascontiguousarray(np.asarray(ins["L0_ffn_wo"], np.float32)),
        "fw1p": _host_fw1p(np.asarray(ins["L0_ffn_w1"], np.float32)),
        "fw2": np.ascontiguousarray(np.asarray(ins["L0_ffn_w2"], np.float32)),
        "fb": _host_fb(ins), "cstf": cstf, "cstb": cstb, "rot": _host_rot(),
    }
    return common


_NC_CACHE = {}


def kernel(**ins):
    if "main" not in _NC_CACHE:
        _NC_CACHE["main"] = build_program()
    nc = _NC_CACHE["main"]
    x = np.asarray(ins["L0_attn_x"])
    common = host_inputs(ins)
    in_maps = []
    for c in range(8):
        m = dict(common)
        m["xt"] = np.ascontiguousarray(x[c].T.astype(np.float32))
        in_maps.append(m)
    res = run_bass_kernel_spmd(nc, in_maps, core_ids=list(range(8)))
    out = np.stack([np.ascontiguousarray(np.asarray(r["out_xt"]).T)
                    for r in res.results], axis=0)
    return out.astype(np.float32)


# version 30
# speedup vs baseline: 1.0295x; 1.0295x over previous
import math
import numpy as np
import ml_dtypes
import concourse.bass as bass
import concourse.mybir as mybir
from concourse.bass_utils import run_bass_kernel_spmd

F32 = mybir.dt.float32
BF16 = mybir.dt.bfloat16
AF = mybir.ActivationFunctionType
ALU = mybir.AluOpType
AX = mybir.AxisListType

T = 2048
D = 1024
HA = 16
NEG = -1.0e30
EPS_A = 1e-8
LN_EPS = 1e-5
FH = 8
FDH = 64
INNER = 512
WIN = 512
HID = 2730
LN8 = math.log(8.0)
NHC = 22
HCS = [128] * 21 + [42]
TT = 16
NCH = 4
DC = 8
ARN = 70400

ENGS = ("T", "V", "S", "Q", "G")


class Bld:
    def __init__(self):
        self.ops = []
        self.last_w = {}
        self.readers = {}
        self.last_on_eng = {e: None for e in ENGS}
        self.dma_sems = []

    def emit(self, eng, fn, reads=(), writes=(), dma=None, tag="", extra_deps=()):
        deps = set(extra_deps)
        for k in reads:
            if k in self.last_w:
                deps.add(self.last_w[k])
        for k in writes:
            if k in self.last_w:
                deps.add(self.last_w[k])
            for r in self.readers.get(k, ()):
                deps.add(r)
        i = len(self.ops)
        deps.discard(i)
        if dma is not None and dma not in self.dma_sems:
            self.dma_sems.append(dma)
        self.ops.append(dict(eng=eng, fn=fn, deps=deps, dma=dma, tag=tag, inc=None))
        for k in reads:
            self.readers.setdefault(k, []).append(i)
        for k in writes:
            self.last_w[k] = i
            self.readers[k] = []
        self.last_on_eng[eng] = i
        return i

    def barrier(self, tag="bar"):
        prev = tuple(i for i in self.last_on_eng.values() if i is not None)
        for e in ENGS:
            self.emit(e, lambda eng: eng.nop(), tag=tag + "." + e, extra_deps=prev)

    def finalize(self, blk, sems):
        ops = self.ops
        need_inc = [False] * len(ops)
        for op in ops:
            for d in op["deps"]:
                if ops[d]["eng"] == op["eng"] == "T":
                    continue
                need_inc[d] = True
        cnt = {e: 0 for e in ENGS}
        scnt = {s: 0 for s in self.dma_sems}
        sem_hist = {s: [] for s in self.dma_sems}
        for i, op in enumerate(ops):
            if op["dma"] is not None:
                s = op["dma"]
                scnt[s] += 16
                op["inc"] = scnt[s]
                op["sem"] = s
                sem_hist[s].append((i, scnt[s]))
            elif need_inc[i]:
                cnt[op["eng"]] += 1
                op["inc"] = cnt[op["eng"]]
                op["sem"] = op["eng"]
        eng_clock = {e: {} for e in ENGS}
        op_clock = [None] * len(ops)
        import bisect
        for i, op in enumerate(ops):
            e = op["eng"]
            waits = []
            for d in sorted(op["deps"]):
                dop = ops[d]
                if dop["eng"] == e == "T":
                    continue
                if dop["dma"] is not None:
                    s = dop["sem"]
                    hist = sem_hist[s]
                    pos = bisect.bisect_left(hist, (i, -1)) - 1
                    assert pos >= 0
                    v = hist[pos][1]
                else:
                    s = dop.get("sem")
                    v = dop["inc"]
                    assert v is not None, (i, op["tag"], d, dop["tag"])
                if eng_clock[e].get(s, 0) >= v:
                    continue
                waits.append((s, v))
                for k2, v2 in op_clock[d].items():
                    if k2 != e and eng_clock[e].get(k2, 0) < v2:
                        eng_clock[e][k2] = v2
                if eng_clock[e].get(s, 0) < v:
                    eng_clock[e][s] = v
            op["waits"] = waits
            snap = dict(eng_clock[e])
            if op["inc"] is not None:
                snap[op["sem"]] = op["inc"]
            op_clock[i] = snap
        self._final_dma = [(s, scnt[s]) for s in self.dma_sems if scnt[s] > 0]
        for e, engname in [("T", "tensor"), ("V", "vector"), ("S", "scalar"),
                           ("Q", "sync"), ("G", "gpsimd")]:
            ops_e = [op for op in ops if op["eng"] == e]
            fin = self._final_dma if e == "Q" else ()

            def body(eng, ops_e=ops_e, fin=fin):
                for op in ops_e:
                    for (s, v) in op["waits"]:
                        eng.wait_ge(sems[s], v)
                    inst = op["fn"](eng)
                    if op["inc"] is not None:
                        inst.then_inc(sems[op["sem"]], 16 if op["dma"] is not None else 1)
                for (s, v) in fin:
                    eng.wait_ge(sems[s], v)
            getattr(blk, engname)(body)


def build_program(dbg=()):
    nc = bass.Bass("TRN2", target_bir_lowering=False)

    xt_d = nc.dram_tensor("xt", [D, T], F32, kind="ExternalInput").ap()
    wq_d = nc.dram_tensor("wqT", [D, D], F32, kind="ExternalInput").ap()
    wkv_d = nc.dram_tensor("wkvT", [D, 2 * D], F32, kind="ExternalInput").ap()
    wo_d = nc.dram_tensor("woT", [D, D], F32, kind="ExternalInput").ap()
    fqkv_d = nc.dram_tensor("fqkv", [2, D, 3 * INNER], F32, kind="ExternalInput").ap()
    fwo_d = nc.dram_tensor("fwo", [2, INNER, D], F32, kind="ExternalInput").ap()
    fw1_d = nc.dram_tensor("fw1p", [2, 2 * NHC, 128, 1024], F32,
                           kind="ExternalInput").ap()
    fw2_d = nc.dram_tensor("fw2", [2, HID, D], F32, kind="ExternalInput").ap()
    fb_d = nc.dram_tensor("fb", [128, 160], F32, kind="ExternalInput").ap()
    cstf_d = nc.dram_tensor("cstf", [128, 400], F32, kind="ExternalInput").ap()
    cstb_d = nc.dram_tensor("cstb", [128, 1664], BF16, kind="ExternalInput").ap()
    rot_d = nc.dram_tensor("rot", [4, 128, T], F32, kind="ExternalInput").ap()
    xin_d = nc.dram_tensor("xin_spill", [D, T], F32, kind="Internal").ap()
    out_d = nc.dram_tensor("out_xt", [D, T], F32, kind="ExternalOutput").ap()
    dbg_d = {}
    for name, shape in dbg:
        dbg_d[name] = nc.dram_tensor(name, shape, F32, kind="ExternalOutput").ap()

    import contextlib
    with contextlib.ExitStack() as stk:
        XT = stk.enter_context(nc.sbuf_tensor("XT", [128, DC * T], F32))
        AR = stk.enter_context(nc.sbuf_tensor("AR", [128, ARN], BF16))
        CSTF = stk.enter_context(nc.sbuf_tensor("CSTF", [128, 400], F32))
        CSTB = stk.enter_context(nc.sbuf_tensor("CSTB", [128, 1664], BF16))
        FBIA = stk.enter_context(nc.sbuf_tensor("FBIA", [128, 160], F32))
        PS = stk.enter_context(nc.psum_tensor("PS", [128, 4096], F32))
        sems = {e: stk.enter_context(nc.semaphore("sem" + e)) for e in ENGS}
        for s in ["q_cst", "q_xt", "q_rot", "q_spill", "q_out",
                  "g_cst", "g_wa", "g_fqkv", "g_fwo", "g_w2", "g_w1p0", "g_w1p1",
                  "g_w1p2", "g_xin", "q_fqs"]:
            sems[s] = stk.enter_context(nc.semaphore(s))
        blk = stk.enter_context(nc.Block())

        B = Bld()
        ar = AR.ap()
        xt = XT.ap()
        ps = PS.ap()
        cstf = CSTF.ap()
        cstb = CSTB.ap()
        fbia = FBIA.ap()

        C = dict(
            IDENT=cstb[:, 0:128], MTC=cstb[:, 128:256], MTF=cstb[:, 256:384],
            PERM=cstb[:, 384:512], HSEL2=cstb[:, 512:514], ONESC=cstb[:, 514:515],
            MNEG=cstb[:, 515:643], SEL4B=cstb[0:4, 643:1155],
            ONES1B=cstb[0:1, 1280:1408],
            TRIC=cstb[:, 1408:1536], TRIF=cstb[:, 1536:1664],
            IDF=cstf[:, 0:128], ONES1F=cstf[0:1, 128:256],
            ZCOL=cstf[:, 256:257], BC2F=cstf[0:2, 257:385],
        )
        CR = ("cst", "cstb")
        FBR = ("fbia",)
        for i, v in enumerate([EPS_A, LN_EPS, LN8, -1.0, -0.5, 1.0, 0.0]):
            nc.const_aps.register(F32, v, cstf[:, 385 + i: 386 + i])

        def QS2(l): return fbia[:, l * 2: l * 2 + 1]
        def KS2(l): return fbia[:, l * 2 + 1: l * 2 + 2]
        def LNG(l, dc): return fbia[:, 4 + l * 16 + dc: 5 + l * 16 + dc]
        def LNB(l, dc): return fbia[:, 12 + l * 16 + dc: 13 + l * 16 + dc]
        def B2C(l, dc): return fbia[:, 36 + l * 8 + dc: 37 + l * 8 + dc]
        def B1A(l, c): return fbia[:, 52 + l * 43 + c: 53 + l * 43 + c]
        def B1G(l, c): return fbia[:, 52 + l * 43 + 22 + c: 53 + l * 43 + 22 + c]

        def bank(i, c0=0, c1=512, p0=0, p1=128):
            return ps[p0:p1, i * 512 + c0: i * 512 + c1]

        o = 0
        o_scr, o = o, o + 3072
        o_pt0, o = o, o + 512
        o_pt1, o = o, o + 512
        o_xb, o = o, o + 512
        o_sq, o = o, o + 512
        o_tf0, o = o, o + 1024
        o_tf1, o = o, o + 1024
        o_tf2, o = o, o + 1024
        o_negm, o = o, o + 16
        o_negmr, o = o, o + 256
        o_bcol, o = o, o + 16
        o = (o + 127) // 128 * 128
        BASE = o
        assert BASE == 8576, BASE
        SCR = ar[:, o_scr:o_scr + 3072].bitcast(F32)
        S0 = SCR[0:1, 0:512]
        S1 = SCR[0:1, 512:1024]
        S2 = SCR[0:1, 1024:1536]
        S0b = SCR[0:2, 0:512]
        S1b = SCR[0:2, 512:1024]
        ROW_AB = ar[0:1, o_scr + 1024:o_scr + 1536]
        ROW_BB = ar[0:1, o_scr + 2048:o_scr + 2560]
        RDENB = ar[0:1, o_scr + 2048:o_scr + 3072]
        PT = [ar[:, o_pt0:o_pt0 + 512], ar[:, o_pt1:o_pt1 + 512]]
        XB = ar[:, o_xb:o_xb + 512]
        SQ = ar[:, o_sq:o_sq + 512]
        TF0 = ar[:, o_tf0:o_tf0 + 1024].bitcast(F32)
        TF1 = ar[:, o_tf1:o_tf1 + 1024].bitcast(F32)
        TF1RAW = ar[:, o_tf1:o_tf1 + 1024]
        TF2 = ar[:, o_tf2:o_tf2 + 1024].bitcast(F32)
        NEGM = ar[:, o_negm:o_negm + 16].bitcast(F32)
        NEGMR = ar[0:4, o_negmr:o_negmr + 128]
        NEGMR2 = ar[0:4, o_negmr + 128:o_negmr + 256]
        BCOL = ar[:, o_bcol:o_bcol + 16].bitcast(F32)

        ctx = dict(B=B, ar=ar, xt=xt, ps=ps, bank=bank, C=C, CR=CR, FBR=FBR,
                   S0=S0, S1=S1, S2=S2, S0b=S0b, S1b=S1b, SCR01=SCR[0:1, 0:1024],
                   ROW_AB=ROW_AB, ROW_BB=ROW_BB, RDENB=RDENB,
                   TF1RAW=TF1RAW,
                   PT=PT, XB=XB, SQ=SQ, TF0=TF0, TF1=TF1,
                   TF2=TF2, NEGM=NEGM, NEGMR=NEGMR, NEGMR2=NEGMR2, BCOL=BCOL,
                   QS2=QS2, KS2=KS2,
                   LNG=LNG, LNB=LNB, B2C=B2C, B1A=B1A, B1G=B1G, dbg=dbg_d, XT=XT)

        B.emit("Q", lambda eng: eng.dma_start(out=CSTF[:], in_=cstf_d[:, :]),
               dma="q_cst", writes=CR, tag="ld_cstf")
        B.emit("Q", lambda eng: eng.dma_start(out=FBIA[:], in_=fb_d[:, :]),
               dma="q_cst", writes=FBR, tag="ld_fb")
        B.emit("G", lambda eng: eng.dma_start(out=CSTB[:], in_=cstb_d[:, :]),
               dma="g_cst", writes=("cstb",), tag="ld_cstb")
        for dc in range(DC):
            def ldx(eng, dc=dc):
                return eng.dma_start(out=XT[:, dc * T:(dc + 1) * T],
                                     in_=xt_d[dc * 128:(dc + 1) * 128, :])
            B.emit("Q", ldx, dma="q_xt", writes=(f"xt.{dc}",), tag=f"ld_xt{dc}")

        phaseA(ctx, wq_d, wkv_d, wo_d, BASE)
        B.barrier("endA")

        for dc in range(DC):
            def sp(eng, dc=dc):
                return eng.dma_start(out=xin_d[dc * 128:(dc + 1) * 128, :],
                                     in_=XT[:, dc * T:(dc + 1) * T])
            B.emit("Q", sp, dma="q_spill", reads=(f"xt.{dc}",),
                   writes=(f"xin.{dc}",), tag=f"spill{dc}")
        if "dbg_pa" in dbg_d:
            for dc in range(DC):
                def dp(eng, dc=dc):
                    return eng.dma_start(
                        out=dbg_d["dbg_pa"][dc * 128:(dc + 1) * 128, :],
                        in_=XT[:, dc * T:(dc + 1) * T])
                B.emit("Q", dp, dma="q_out", reads=(f"xt.{dc}",), tag=f"dbgpa{dc}")

        _fs = [(ar[:, 67072 + i * 512: 67072 + (i + 1) * 512], f"fsc{i}")
               for i in range(2)]
        FSCR = [_fs[0], _fs[1], _fs[0], _fs[1]]

        def final_chunk(tc):
            for dc in range(DC):
                key = f"xt.{dc}"
                deps = set()
                if key in B.last_w:
                    deps.add(B.last_w[key])
                deps.update(B.readers.get(key, ()))

                def acc(eng, dc=dc, tc=tc):
                    return eng.dma_start(
                        out=XT[:, dc * T + tc * 512: dc * T + (tc + 1) * 512],
                        in_=xin_d[dc * 128:(dc + 1) * 128, tc * 512:(tc + 1) * 512],
                        accum_op=ALU.add)
                B.emit("G", acc, dma="g_xin", reads=(f"xin.{dc}", "r_tail"),
                       writes=(f"xtf.{tc}.{dc}",), extra_deps=tuple(deps),
                       tag=f"facc.{tc}.{dc}")
            ln_chunk(ctx, tc * 512, f"fin{tc}", scr=FSCR,
                     kf=lambda dc: f"xtf.{tc}.{dc}")
            for dc in range(DC):
                def st(eng, dc=dc, tc=tc):
                    return eng.dma_start(
                        out=out_d[dc * 128:(dc + 1) * 128, tc * 512:(tc + 1) * 512],
                        in_=xt[:, dc * T + tc * 512: dc * T + (tc + 1) * 512])
                B.emit("Q", st, dma="q_out", reads=(f"xtf.{tc}.{dc}",),
                       tag=f"stout.{tc}.{dc}")

        for l in range(2):
            ffn_layer(ctx, l, fqkv_d, fwo_d, fw1_d, fw2_d, rot_d, BASE,
                      fin=final_chunk if l == 1 else None)
            if f"dbg_l{l}" in dbg_d:
                B.barrier(f"dbgl{l}")
                for dc in range(DC):
                    def dp(eng, dc=dc, l=l):
                        return eng.dma_start(
                            out=dbg_d[f"dbg_l{l}"][dc * 128:(dc + 1) * 128, :],
                            in_=XT[:, dc * T:(dc + 1) * T])
                    B.emit("Q", dp, dma="q_out", reads=(f"xt.{dc}",), tag=f"dbgl{l}.{dc}")

        B.barrier("preF")
        B.barrier("final")

        B.finalize(blk, sems)
    return nc


def ln_chunk(ctx, c0, tg, scr=None, kf=None):
    B, xt, bank = ctx["B"], ctx["xt"], ctx["bank"]
    C, CR = ctx["C"], ctx["CR"]
    S0, S1, S2 = ctx["S0"], ctx["S1"], ctx["S2"]
    TF0 = ctx["TF0"]
    if scr is None:
        scr = [(ctx["XB"], "xb"), (ctx["SQ"], "sq"), (ctx["XB"], "xb"), (ctx["SQ"], "sq")]
    if kf is None:
        kf = lambda dc: f"xt.{dc}"
    n = 512
    for dc in range(DC):
        XBp, xbk = scr[(dc % 2) * 2]
        SQp, sqk = scr[(dc % 2) * 2 + 1]

        def cst(eng, dc=dc, XBp=XBp):
            return eng.tensor_copy(out=XBp, in_=xt[:, dc * T + c0: dc * T + c0 + n])
        B.emit("V", cst, reads=(kf(dc),), writes=(xbk,), tag=f"{tg}lc{dc}")
        B.emit("S", lambda eng, XBp=XBp, SQp=SQp: eng.activation(
            out=SQp, in_=XBp, func=AF.Square),
            reads=(xbk,), writes=(sqk,), tag=f"{tg}lq{dc}")

        def s1(eng, dc=dc, XBp=XBp):
            return eng.matmul(out=bank(6, 0, n, 0, 1), lhsT=C["ONESC"], rhs=XBp,
                              start=(dc == 0), stop=(dc == DC - 1))
        B.emit("T", s1, reads=(xbk,) + CR, writes=("ps6",), tag=f"{tg}l1{dc}")

        def s2(eng, dc=dc, SQp=SQp):
            return eng.matmul(out=bank(7, 0, n, 0, 1), lhsT=C["ONESC"], rhs=SQp,
                              start=(dc == 0), stop=(dc == DC - 1))
        B.emit("T", s2, reads=(sqk,) + CR, writes=("ps7",), tag=f"{tg}l2{dc}")
    B.emit("V", lambda eng: eng.tensor_scalar(out=S0, in0=bank(6, 0, n, 0, 1),
           scalar1=1.0 / D, scalar2=None, op0=ALU.mult),
           reads=("ps6",), writes=("lnS0",), tag=f"{tg}mu")
    B.emit("V", lambda eng: eng.tensor_scalar(out=S1, in0=bank(7, 0, n, 0, 1),
           scalar1=1.0 / D, scalar2=None, op0=ALU.mult),
           reads=("ps7",), writes=("lnS1",), tag=f"{tg}ms")
    B.emit("V", lambda eng: eng.tensor_tensor(out=S2, in0=S0, in1=S0, op=ALU.mult),
           reads=("lnS0",), writes=("lnS2",), tag=f"{tg}v1")
    B.emit("V", lambda eng: eng.tensor_tensor(out=S2, in0=S1, in1=S2, op=ALU.subtract),
           reads=("lnS1", "lnS2"), writes=("lnS2",), tag=f"{tg}v2")
    B.emit("S", lambda eng: eng.activation(out=S2, in_=S2, func=AF.Ln, bias=LN_EPS),
           reads=("lnS2",), writes=("lnS2",), tag=f"{tg}ln")
    B.emit("S", lambda eng: eng.activation(out=S1, in_=S2, func=AF.Exp, scale=-0.5),
           reads=("lnS2",), writes=("lnS1",), tag=f"{tg}ex")
    B.emit("V", lambda eng: eng.scalar_tensor_tensor(out=S0, in0=S0, scalar=-1.0,
           in1=S1, op0=ALU.mult, op1=ALU.mult),
           reads=("lnS0", "lnS1"), writes=("lnS0",), tag=f"{tg}B")
    B.emit("T", lambda eng: eng.matmul(out=bank(6), lhsT=C["ONES1F"], rhs=S1,
           start=True, stop=True), reads=("lnS1",) + CR, writes=("ps6",), tag=f"{tg}bA")
    B.emit("T", lambda eng: eng.matmul(out=bank(7), lhsT=C["ONES1F"], rhs=S0,
           start=True, stop=True), reads=("lnS0",) + CR, writes=("ps7",), tag=f"{tg}bB")
    for dc in range(DC):
        def a1(eng, dc=dc):
            return eng.tensor_tensor(out=TF0[:, 0:n],
                                     in0=xt[:, dc * T + c0: dc * T + c0 + n],
                                     in1=bank(6), op=ALU.mult)
        B.emit("V", a1, reads=(kf(dc), "ps6"), writes=("tf0",), tag=f"{tg}a1.{dc}")

        def a2(eng, dc=dc):
            return eng.tensor_tensor(out=xt[:, dc * T + c0: dc * T + c0 + n],
                                     in0=TF0[:, 0:n], in1=bank(7), op=ALU.add)
        B.emit("V", a2, reads=("tf0", "ps7"), writes=(kf(dc),), tag=f"{tg}a2.{dc}")


def phaseA(ctx, wq_d, wkv_d, wo_d, base):
    B = ctx["B"]
    ar, xt, bank = ctx["ar"], ctx["xt"], ctx["bank"]
    C, CR = ctx["C"], ctx["CR"]
    PT, TF2 = ctx["PT"], ctx["TF2"]
    S0, S1, S2 = ctx["S0"], ctx["S1"], ctx["S2"]
    TF0 = ctx["TF0"]

    o_kt = base
    o_v = o_kt + 16384
    o_xbfc = o_v + 16640
    o_qtc = o_xbfc + 4096
    o_otc = o_qtc + 4096
    o_wsl = o_otc + 4096
    assert o_wsl + 16384 <= ARN, o_wsl + 16384

    def kT_t(f):
        return ar[:, o_kt + f * 2048: o_kt + (f + 1) * 2048]

    def v_t(tt, h):
        b = o_v + tt * 1040 + h * 65
        return ar[:, b: b + 65]

    def xbfc_t(dc):
        return ar[:, o_xbfc + dc * 512: o_xbfc + (dc + 1) * 512]

    def qtc_t(f):
        return ar[:, o_qtc + f * 512: o_qtc + (f + 1) * 512]

    def otc_t(ic):
        return ar[:, o_otc + ic * 512: o_otc + (ic + 1) * 512]

    def wkv_t(dc, f0, nn):
        return ar[:, o_wsl + dc * 2048 + f0: o_wsl + dc * 2048 + f0 + nn]

    def wq_t(dc, f):
        return ar[:, o_wsl + dc * 1024 + f * 128: o_wsl + dc * 1024 + (f + 1) * 128]

    def wo_t(ic, dc):
        b = o_wsl + 8192 + ic * 1024 + dc * 128
        return ar[:, b: b + 128]

    def vones(eng):
        vv = ar[:, o_v:o_v + 16640].rearrange("p (t h c) -> p t h c", t=TT, h=HA)
        return eng.memset(vv[:, :, :, 64:65], 1.0)
    B.emit("V", vones, writes=tuple(f"v.{tt}" for tt in range(TT)), tag="v_ones")

    for dc in range(DC):
        def ld(eng, dc=dc):
            return eng.dma_start(out=wkv_t(dc, 0, 2048),
                                 in_=wkv_d[dc * 128:(dc + 1) * 128, :])
        B.emit("G", ld, dma="g_wa", writes=(f"wkv.{dc}", "wslot"), tag=f"ld_wkv{dc}")

    for tc in range(NCH):
        for dc in range(DC):
            def cst(eng, dc=dc, tc=tc):
                return eng.tensor_copy(
                    out=xbfc_t(dc),
                    in_=xt[:, dc * T + tc * 512: dc * T + (tc + 1) * 512])
            B.emit("V", cst, reads=(f"xt.{dc}",), writes=(f"xbfc.{dc}",),
                   tag=f"cxkv.{tc}.{dc}")
        for f in range(8):
            bk = f % 2
            for dc in range(DC):
                def mm(eng, f=f, dc=dc, bk=bk):
                    return eng.matmul(out=bank(bk),
                                      lhsT=wkv_t(dc, f * 128, 128), rhs=xbfc_t(dc),
                                      start=(dc == 0), stop=(dc == DC - 1))
                B.emit("T", mm, reads=(f"wkv.{dc}", f"xbfc.{dc}", "wslot"),
                       writes=(f"ps{bk}",), tag=f"kmm.{tc}.{f}.{dc}")

            def ev(eng, f=f, tc=tc, bk=bk):
                return eng.tensor_copy(out=kT_t(f)[:, tc * 512:(tc + 1) * 512],
                                       in_=bank(bk))
            B.emit("V", ev, reads=(f"ps{bk}",), writes=(f"kT.{f}",), tag=f"kev.{tc}.{f}")
        for ttl in range(4):
            tt = tc * 4 + ttl
            for fc in range(2):
                bk = 2 + fc
                for dc in range(DC):
                    def mm(eng, ttl=ttl, fc=fc, dc=dc, bk=bk):
                        return eng.matmul(
                            out=bank(bk),
                            lhsT=xbfc_t(dc)[:, ttl * 128:(ttl + 1) * 128],
                            rhs=wkv_t(dc, 1024 + fc * 512, 512),
                            start=(dc == 0), stop=(dc == DC - 1))
                    B.emit("T", mm, reads=(f"wkv.{dc}", f"xbfc.{dc}", "wslot"),
                           writes=(f"ps{bk}",), tag=f"vmm.{tt}.{fc}.{dc}")

                def ev(eng, tt=tt, fc=fc, bk=bk):
                    b0 = o_v + tt * 1040 + fc * 8 * 65
                    dst = ar[:, b0: b0 + 8 * 65].rearrange("p (h c) -> p h c", h=8)
                    return eng.tensor_copy(out=dst[:, :, 0:64], in_=bank(bk))
                B.emit("V", ev, reads=(f"ps{bk}",), writes=(f"v.{tt}",),
                       tag=f"vev.{tt}.{fc}")

    for dc in range(DC):
        def ld(eng, dc=dc):
            return eng.dma_start(
                out=ar[:, o_wsl + dc * 1024: o_wsl + (dc + 1) * 1024],
                in_=wq_d[dc * 128:(dc + 1) * 128, :])
        B.emit("G", ld, dma="g_wa", writes=(f"wq.{dc}", "wslot"), tag=f"ld_wq{dc}")
    for dc in range(DC):
        def ld(eng, dc=dc):
            return eng.dma_start(
                out=ar[:, o_wsl + 8192 + dc * 1024: o_wsl + 8192 + (dc + 1) * 1024],
                in_=wo_d[dc * 128:(dc + 1) * 128, :])
        B.emit("G", ld, dma="g_wa", writes=(f"wo.{dc}", "wslot"), tag=f"ld_wo{dc}")

    PTW = [ar[:, o_xbfc + i * 1024: o_xbfc + (i + 1) * 1024] for i in range(4)]
    PTK = [("xbfc.0", "xbfc.1"), ("xbfc.2", "xbfc.3"),
           ("xbfc.4", "xbfc.5"), ("xbfc.6", "xbfc.7")]
    PARDEN = ctx["SCR01"]
    RDENB = ctx["RDENB"]
    ROW_AB, ROW_BB = ctx["ROW_AB"], ctx["ROW_BB"]
    LNT1A = ctx["TF1RAW"][:, 0:512]
    LNT1B = ctx["TF1RAW"][:, 512:1024]

    def emit_castq(tc):
        for dc in range(DC):
            def cst(eng, dc=dc, tc=tc):
                return eng.tensor_copy(
                    out=xbfc_t(dc),
                    in_=xt[:, dc * T + tc * 512: dc * T + (tc + 1) * 512])
            B.emit("V", cst, reads=(f"xt.{dc}",), writes=(f"xbfc.{dc}",),
                   tag=f"cxq.{tc}.{dc}")
        for f in range(8):
            bk = f % 2
            for dc in range(DC):
                def mm(eng, f=f, dc=dc, bk=bk):
                    return eng.matmul(out=bank(bk), lhsT=wq_t(dc, f), rhs=xbfc_t(dc),
                                      start=(dc == 0), stop=(dc == DC - 1))
                B.emit("T", mm, reads=(f"wq.{dc}", f"xbfc.{dc}", "wslot"),
                       writes=(f"ps{bk}",), tag=f"qmm.{tc}.{f}.{dc}")

            def ev(eng, f=f, bk=bk):
                return eng.tensor_scalar(out=qtc_t(f), in0=bank(bk), scalar1=0.125,
                                         scalar2=None, op0=ALU.mult)
            B.emit("V", ev, reads=(f"ps{bk}",), writes=(f"qtc.{f}",), tag=f"qev.{tc}.{f}")

    def emit_headpair(tc, hp):
        ntj = 4 * tc + 4
        npair = ntj // 2
        HP = [(2 * hp, 0, 4), (2 * hp + 1, 2, 5)]

        def stage2(p):
            zss = [[], []]
            for u in range(2):
                tj = 2 * p + u
                diag = (tj // 4 == tc)
                r = tj - 4 * tc if diag else 0
                for side in range(2):
                    h, sb, nb = HP[side]
                    fq = h // 2
                    pr = (h % 2) * 64
                    sbk = sb + u

                    def smm(eng, fq=fq, pr=pr, tj=tj, sbk=sbk, stp=(not diag)):
                        return eng.matmul(
                            out=bank(sbk),
                            lhsT=kT_t(fq)[pr:pr + 64, tj * 128:(tj + 1) * 128],
                            rhs=qtc_t(fq)[pr:pr + 64, 0:512],
                            start=True, stop=stp)
                    B.emit("T", smm, reads=(f"kT.{fq}", f"qtc.{fq}"),
                           writes=(f"ps{sbk}",), tag=f"sT.{tc}.{h}.{tj}")
                if diag:
                    for side in range(2):
                        h, sb, nb = HP[side]
                        sbk = sb + u

                        def msk(eng, r=r, sbk=sbk):
                            return eng.matmul(out=bank(sbk, r * 128, r * 128 + 128),
                                              lhsT=C["IDENT"], rhs=C["MTC"],
                                              start=False, stop=True)
                        B.emit("T", msk, reads=CR, writes=(f"ps{sbk}",),
                               tag=f"smk.{tc}.{2 * hp + side}.{tj}")
                    if r > 0:
                        zss[0].append((u, r))
                        zss[1].append((u, r))
            for side in range(2):
                h, sb, nb = HP[side]
                bkeys = (f"ps{sb}", f"ps{sb + 1}")
                pt = PTW[side * 2 + (p % 2)]
                ptk = PTK[side * 2 + (p % 2)]

                def ex(eng, sb=sb, pt=pt):
                    return eng.activation(out=pt[:, 0:1024],
                                          in_=bank(sb, 0, 1024), func=AF.Exp)
                B.emit("S", ex, reads=bkeys, writes=ptk,
                       tag=f"exp.{tc}.{hp}.{side}.{p}")
                for (u, r) in zss[side]:
                    B.emit("V", lambda eng, pt=pt, u=u, r=r: eng.memset(
                        pt[:, u * 512:u * 512 + r * 128], 0.0),
                        writes=ptk, tag=f"ptz.{tc}.{hp}.{side}.{p}.{u}")

        def pvst(p, side):
            h, sb, nb = HP[side]
            pt = PTW[side * 2 + (p % 2)]
            ptk = PTK[side * 2 + (p % 2)]
            for u in range(2):
                tj = 2 * p + u

                def pv(eng, tj=tj, h=h, pt=pt, u=u, nb=nb, first=(tj == 0),
                       last=(tj == ntj - 1)):
                    return eng.matmul(out=bank(nb, 0, 512, 0, 65),
                                      lhsT=v_t(tj, h),
                                      rhs=pt[:, u * 512:(u + 1) * 512],
                                      start=first, stop=last)
                B.emit("T", pv, reads=(f"v.{tj}",) + ptk,
                       writes=(f"ps{nb}",), tag=f"pv.{tc}.{h}.{tj}")

        stage2(0)
        for p in range(1, npair):
            stage2(p)
            pvst(p - 1, 0)
            pvst(p - 1, 1)
        pvst(npair - 1, 0)
        pvst(npair - 1, 1)
        RDK = ("lnS0", "lnS1")
        i_ln = B.emit("S", lambda eng: eng.activation(
            out=PARDEN, in_=bank(4, 0, 1024, 64, 65), func=AF.Ln, bias=EPS_A),
            reads=("ps4", "ps5"), writes=RDK, tag=f"lnden.{tc}.{hp}")
        B.emit("S", lambda eng: eng.activation(
            out=RDENB, in_=PARDEN, func=AF.Exp, scale=-1.0),
            reads=RDK, writes=RDK + ("lnS2",), tag=f"rden.{tc}.{hp}")
        for side in range(2):
            h, sb, nb = HP[side]
            rb = sb
            B.emit("V", lambda eng, nb=nb: eng.tensor_copy(out=TF2[0:64, 0:512],
                   in_=bank(nb, 0, 512, 0, 64)),
                   reads=(f"ps{nb}",), writes=("tf2",), tag=f"nev.{tc}.{h}",
                   extra_deps=(i_ln,))
            B.emit("T", lambda eng, rb=rb, side=side: eng.matmul(
                out=bank(rb, 0, 512, 0, 64), lhsT=C["ONES1B"][:, 0:64],
                rhs=RDENB[:, side * 512:(side + 1) * 512],
                start=True, stop=True),
                reads=RDK + ("lnS2",) + CR, writes=(f"ps{rb}",), tag=f"rdb.{tc}.{h}")
            ic = h // 2
            prh = (h % 2) * 64

            def oev(eng, ic=ic, prh=prh, rb=rb):
                return eng.tensor_tensor(out=otc_t(ic)[prh:prh + 64, :],
                                         in0=TF2[0:64, 0:512],
                                         in1=bank(rb, 0, 512, 0, 64), op=ALU.mult)
            B.emit("V", oev, reads=("tf2", f"ps{rb}"), writes=(f"otc.{ic}",),
                   tag=f"oev.{tc}.{h}")

    def emit_wo_stats(tc):
        for dc in range(DC):
            bk = 4 + dc % 2
            for ic in range(8):
                def mm(eng, dc=dc, ic=ic, bk=bk):
                    return eng.matmul(out=bank(bk), lhsT=wo_t(ic, dc), rhs=otc_t(ic),
                                      start=(ic == 0), stop=(ic == 7))
                B.emit("T", mm, reads=(f"wo.{dc}", f"otc.{ic}", "wslot"),
                       writes=(f"ps{bk}",), tag=f"wom.{tc}.{dc}.{ic}")

            def res(eng, dc=dc, tc=tc, bk=bk):
                sl = xt[:, dc * T + tc * 512: dc * T + (tc + 1) * 512]
                return eng.tensor_tensor(out=sl, in0=sl, in1=bank(bk), op=ALU.add)
            B.emit("V", res, reads=(f"ps{bk}", f"xt.{dc}"), writes=(f"xt.{dc}",),
                   tag=f"res.{tc}.{dc}")
            if dc % 2 == 0:
                XBp, xbk = ctx["XB"], "xb"
                SQp, sqk = ctx["SQ"], "sq"
            else:
                XBp, xbk = LNT1A, "tf1"
                SQp, sqk = LNT1B, "tf1b"

            def cst(eng, dc=dc, tc=tc, XBp=XBp):
                return eng.tensor_copy(out=XBp,
                                       in_=xt[:, dc * T + tc * 512: dc * T + (tc + 1) * 512])
            B.emit("V", cst, reads=(f"xt.{dc}",), writes=(xbk,), tag=f"palc.{tc}.{dc}")
            B.emit("S", lambda eng, XBp=XBp, SQp=SQp: eng.activation(
                out=SQp, in_=XBp, func=AF.Square),
                reads=(xbk,), writes=(sqk,), tag=f"palq.{tc}.{dc}")

            def s1(eng, dc=dc, XBp=XBp):
                return eng.matmul(out=bank(6, 0, 512, 0, 1), lhsT=C["ONESC"], rhs=XBp,
                                  start=(dc == 0), stop=(dc == DC - 1))
            B.emit("T", s1, reads=(xbk,) + CR, writes=("ps6",), tag=f"pal1.{tc}.{dc}")

            def s2(eng, dc=dc, SQp=SQp):
                return eng.matmul(out=bank(7, 0, 512, 0, 1), lhsT=C["ONESC"], rhs=SQp,
                                  start=(dc == 0), stop=(dc == DC - 1))
            B.emit("T", s2, reads=(sqk,) + CR, writes=("ps7",), tag=f"pal2.{tc}.{dc}")

    def emit_ln_tail(tc):
        tg = f"pa{tc}"
        c0 = tc * 512
        n = 512
        B.emit("V", lambda eng: eng.tensor_scalar(out=S0, in0=bank(6, 0, n, 0, 1),
               scalar1=1.0 / D, scalar2=None, op0=ALU.mult),
               reads=("ps6",), writes=("lnS0",), tag=f"{tg}mu")
        B.emit("V", lambda eng: eng.tensor_scalar(out=S1, in0=bank(7, 0, n, 0, 1),
               scalar1=1.0 / D, scalar2=None, op0=ALU.mult),
               reads=("ps7",), writes=("lnS1",), tag=f"{tg}ms")
        B.emit("V", lambda eng: eng.tensor_tensor(out=S2, in0=S0, in1=S0, op=ALU.mult),
               reads=("lnS0",), writes=("lnS2",), tag=f"{tg}v1")
        B.emit("V", lambda eng: eng.tensor_tensor(out=S2, in0=S1, in1=S2,
               op=ALU.subtract),
               reads=("lnS1", "lnS2"), writes=("lnS2",), tag=f"{tg}v2")
        B.emit("S", lambda eng: eng.activation(out=S2, in_=S2, func=AF.Ln, bias=LN_EPS),
               reads=("lnS2",), writes=("lnS2",), tag=f"{tg}ln")
        B.emit("S", lambda eng: eng.activation(out=S1, in_=S2, func=AF.Exp,
               scale=-0.5),
               reads=("lnS2",), writes=("lnS1",), tag=f"{tg}ex")
        B.emit("V", lambda eng: eng.scalar_tensor_tensor(out=S0, in0=S0, scalar=-1.0,
               in1=S1, op0=ALU.mult, op1=ALU.mult),
               reads=("lnS0", "lnS1"), writes=("lnS0",), tag=f"{tg}B")
        B.emit("T", lambda eng: eng.matmul(out=bank(6), lhsT=C["ONES1F"], rhs=S1,
               start=True, stop=True), reads=("lnS1",) + CR, writes=("ps6",),
               tag=f"{tg}bA")
        B.emit("T", lambda eng: eng.matmul(out=bank(7), lhsT=C["ONES1F"], rhs=S0,
               start=True, stop=True), reads=("lnS0",) + CR, writes=("ps7",),
               tag=f"{tg}bB")
        for dc in range(DC):
            def a1(eng, dc=dc):
                return eng.tensor_tensor(out=TF0[:, 0:n],
                                         in0=xt[:, dc * T + c0: dc * T + c0 + n],
                                         in1=bank(6), op=ALU.mult)
            B.emit("V", a1, reads=(f"xt.{dc}", "ps6"), writes=("tf0",),
                   tag=f"{tg}a1.{dc}")

            def a2(eng, dc=dc):
                return eng.tensor_tensor(out=xt[:, dc * T + c0: dc * T + c0 + n],
                                         in0=TF0[:, 0:n], in1=bank(7), op=ALU.add)
            B.emit("V", a2, reads=("tf0", "ps7"), writes=(f"xt.{dc}",),
                   tag=f"{tg}a2.{dc}")

    emit_castq(0)
    for tc in range(NCH):
        for hp in range(HA // 2):
            emit_headpair(tc, hp)
            if hp == 3 and tc >= 1:
                emit_ln_tail(tc - 1)
        emit_wo_stats(tc)
        if tc + 1 < NCH:
            emit_castq(tc + 1)
    emit_ln_tail(NCH - 1)


def ffn_layer(ctx, l, fqkv_d, fwo_d, fw1_d, fw2_d, rot_d, base, fin=None):
    B = ctx["B"]
    ar, xt, bank, ps = ctx["ar"], ctx["xt"], ctx["bank"], ctx["ps"]
    C, CR, FBR = ctx["C"], ctx["CR"], ctx["FBR"]
    S0, S1, S2 = ctx["S0"], ctx["S1"], ctx["S2"]
    S0b, S1b = ctx["S0b"], ctx["S1b"]
    PT = ctx["PT"]
    XB, SQ = ctx["XB"], ctx["SQ"]
    TF0, TF1, TF2 = ctx["TF0"], ctx["TF1"], ctx["TF2"]
    NEGM, NEGMR, NEGMR2, BCOL = ctx["NEGM"], ctx["NEGMR"], ctx["NEGMR2"], ctx["BCOL"]
    ROW_AB, ROW_BB = ctx["ROW_AB"], ctx["ROW_BB"]
    QS2, KS2 = ctx["QS2"], ctx["KS2"]
    LNG, LNB, B2C, B1A, B1G = ctx["LNG"], ctx["LNB"], ctx["B2C"], ctx["B1A"], ctx["B1G"]
    L = f"L{l}"

    o_qT = base
    o_kT = o_qT + 8192
    o_v = o_kT + 8192
    o_hT = o_v + 8320
    o_wqkv = o_hT + 8192
    o_rot = o_wqkv + 12288
    end_f1 = o_rot + 8192
    assert end_f1 <= ARN, end_f1
    o_oT = o_hT
    o_wo = o_wqkv
    o_w1 = o_wqkv + 4096
    o_f1s = end_f1
    F1SQ = [ar[:, o_f1s + i * 512: o_f1s + (i + 1) * 512] for i in range(2)]
    F1QNB = [ar[:, o_f1s + 1024 + i * 512: o_f1s + 1024 + (i + 1) * 512] for i in range(2)]
    F1SCR = ar[:, o_f1s + 2048: o_f1s + 2048 + 2048].bitcast(F32)
    F1TA = [ar[:, o_f1s + 4096 + i * 1024: o_f1s + 4096 + (i + 1) * 1024].bitcast(F32)
            for i in range(2)]
    F1TB = [ar[:, o_f1s + 6144 + i * 1024: o_f1s + 6144 + (i + 1) * 1024].bitcast(F32)
            for i in range(2)]
    assert o_f1s + 8192 <= ARN, o_f1s + 8192
    o_w2 = o_qT
    o_h2n = o_hT
    o_gg = o_qT + 22528
    u_slots = ([o_qT + 23040] +
               [o_wo + i * 1024 for i in range(4)] +
               [o_wqkv + 8192 + i * 1024 for i in range(2, 4)] +
               [o_rot + i * 1024 for i in range(8)] +
               [end_f1 + i * 1024 for i in range(5)] +
               [o_f1s + 6144, o_f1s + 7168])
    u_slots = [s for s in u_slots if s + 1024 <= ARN][:NHC]
    assert len(u_slots) == NHC, len(u_slots)

    def _slot_alias(s):
        if o_qT <= s < o_qT + 24704:
            return "r_v" if s >= o_v else ("r_kT" if s >= o_kT else "r_qT")
        if o_wo <= s < o_wo + 4096:
            return "r_wqkv"
        if o_wqkv <= s < o_wqkv + 12288:
            return "r_wqspare"
        if o_rot <= s < o_rot + 8192:
            return "r_rot"
        return "r_tail"
    u_alias = [_slot_alias(s) for s in u_slots]

    qT_t = lambda f: ar[:, o_qT + f * 2048: o_qT + (f + 1) * 2048]
    kTf_t = lambda f: ar[:, o_kT + f * 2048: o_kT + (f + 1) * 2048]
    vf_t = lambda tt, h: ar[:, o_v + tt * 520 + h * 65: o_v + tt * 520 + h * 65 + 65]
    hT_t = lambda dc: ar[:, o_hT + dc * 1024: o_hT + (dc + 1) * 1024]
    wqkv_t = lambda dc, f0, nn: ar[:, o_wqkv + dc * 1536 + f0: o_wqkv + dc * 1536 + f0 + nn]
    ROTV = lambda tab: ar[:, o_rot + tab * 2048: o_rot + (tab + 1) * 2048].bitcast(F32)
    oT_t = lambda f: ar[:, o_oT + f * 2048: o_oT + (f + 1) * 2048]
    wo_tl = lambda ic, dc: ar[:, o_wo + ic * 1024 + dc * 128: o_wo + ic * 1024 + (dc + 1) * 128]
    h2n_t = lambda dc: ar[:, o_h2n + dc * 1024: o_h2n + (dc + 1) * 1024]
    u_t = lambda c: ar[:, u_slots[c]: u_slots[c] + 1024]
    w2_tl = lambda c, dc: ar[:, o_w2 + c * 1024 + dc * 128: o_w2 + c * 1024 + (dc + 1) * 128]
    GG = ar[:, o_gg:o_gg + 512]

    fqs_t = lambda dc: ar[:, o_qT + dc * 3072: o_qT + dc * 3072 + 3072].bitcast(F32)
    for dc in range(DC):
        def ld(eng, dc=dc):
            return eng.dma_start(out=fqs_t(dc),
                                 in_=fqkv_d[l, dc * 128:(dc + 1) * 128, :])
        B.emit("Q", ld, dma="q_fqs",
               writes=(f"fqs.{dc}", "r_qT", "r_kT", "r_v"), tag=f"{L}ldqs{dc}")
    for dc in range(DC):
        def cst(eng, dc=dc):
            return eng.tensor_copy(out=wqkv_t(dc, 0, 1536), in_=fqs_t(dc))
        B.emit("V", cst, reads=(f"fqs.{dc}", "r_qT", "r_kT", "r_v"),
               writes=(f"fqkv.{dc}", "r_wqkv", "r_w1b0", "r_w1b1", "r_w1b2",
                       "r_wqspare"), tag=f"{L}cqs{dc}")

    def vones(eng):
        vv = ar[:, o_v:o_v + 8320].rearrange("p (t h c) -> p t h c", t=TT, h=FH)
        return eng.memset(vv[:, :, :, 64:65], 1.0)
    B.emit("V", vones,
           writes=tuple(f"fv.{tt}" for tt in range(TT)) +
           ("fqs.4", "fqs.5", "fqs.6", "fqs.7"), tag=f"{L}vones")

    for hh in range(2):
        hc0 = hh * 1024
        for tab in range(4):
            def ldr(eng, tab=tab, hc0=hc0):
                return eng.dma_start(out=ROTV(tab)[:, 0:1024],
                                     in_=rot_d[tab, :, hc0:hc0 + 1024])
            B.emit("Q", ldr, dma="q_rot", writes=(f"rot.{tab}", "r_rot"),
                   tag=f"{L}ldrot{hh}.{tab}")
        for dc in range(DC):
            if dc < 4:
                def cp(eng, dc=dc, hc0=hc0):
                    return eng.tensor_copy(out=hT_t(dc),
                                           in_=xt[:, dc * T + hc0: dc * T + hc0 + 1024])
                B.emit("V", cp, reads=(f"xt.{dc}",), writes=(f"hT.{dc}", "r_hT"),
                       tag=f"{L}h{hh}.{dc}")
            elif hh == 0:
                def cp(eng, dc=dc):
                    return eng.tensor_copy(out=hT_t(dc)[:, 1:1024],
                                           in_=xt[:, dc * T: dc * T + 1023])
                B.emit("V", cp, reads=(f"xt.{dc}",), writes=(f"hT.{dc}", "r_hT"),
                       tag=f"{L}h{hh}.{dc}")
                B.emit("V", lambda eng, dc=dc: eng.memset(hT_t(dc)[:, 0:1], 0.0),
                       writes=(f"hT.{dc}", "r_hT"), tag=f"{L}h0z.{dc}")
            else:
                def cp(eng, dc=dc, hc0=hc0):
                    return eng.tensor_copy(
                        out=hT_t(dc), in_=xt[:, dc * T + hc0 - 1: dc * T + hc0 + 1023])
                B.emit("V", cp, reads=(f"xt.{dc}",), writes=(f"hT.{dc}", "r_hT"),
                       tag=f"{L}h{hh}.{dc}")
        def f1mm(t, hh=hh):
            qk, f = t // 4, t % 4
            fcol = qk * 512 + f * 128
            b0 = 0 if t % 2 == 0 else 6
            for ch in range(2):
                for dc in range(DC):
                    def mm(eng, dc=dc, fcol=fcol, ch=ch, b0=b0):
                        return eng.matmul(
                            out=bank(b0 + ch), lhsT=wqkv_t(dc, fcol, 128),
                            rhs=hT_t(dc)[:, ch * 512:(ch + 1) * 512],
                            start=(dc == 0), stop=(dc == DC - 1))
                    B.emit("T", mm, reads=(f"fqkv.{dc}", f"hT.{dc}", "r_wqkv", "r_wqspare",
                                           "r_w1b0", "r_w1b1", "r_w1b2", "r_hT"),
                           writes=(f"ps{b0 + ch}",), tag=f"{L}qk{qk}.{hh}.{f}.{ch}.{dc}")

        def f1post(t, hh=hh, hc0=hc0):
            qk, f = t // 4, t % 4
            b0 = 0 if t % 2 == 0 else 6
            CHR = []
            for ch in range(2):
                CHR.append(dict(
                    bk=b0 + ch,
                    qraw=(TF0 if ch == 0 else TF1)[:, 0:512],
                    kqraw="tf0" if ch == 0 else "tf1",
                    sq=F1SQ[ch], ksq=f"f1sq{ch}",
                    ss=(S0b if ch == 0 else F1SCR[0:2, 0:512]),
                    kss="lnS0" if ch == 0 else "f1ss1",
                    rn=(S1b if ch == 0 else F1SCR[0:2, 512:1024]),
                    krn="lnS1" if ch == 0 else "f1rn1",
                    qnb=F1QNB[ch], kqnb=f"f1qnb{ch}",
                    ta=F1TA[ch][:, 0:512], kta=f"f1ta{ch}",
                    tb=F1TB[ch][:, 0:512], ktb=f"f1tb{ch}",
                ))
            scol = QS2(l) if qk == 0 else KS2(l)
            bias = LN8 if qk == 0 else 0.0
            tcos = 0 if qk == 0 else 2
            tsin = 1 if qk == 0 else 3
            steps = []
            for ch in range(2):
                R = CHR[ch]
                bk = R["bk"]
                st_ = []
                st_.append(("V", lambda eng, R=R, bk=bk: eng.tensor_copy(
                    out=R["qraw"], in_=bank(bk)), (f"ps{bk}",), (R["kqraw"],), "qe"))
                st_.append(("S", lambda eng, R=R: eng.activation(
                    out=R["sq"], in_=R["qraw"], func=AF.Square),
                    (R["kqraw"],), (R["ksq"], "r_tail"), "sq"))
                st_.append(("T", lambda eng, R=R, bk=bk: eng.matmul(
                    out=bank(bk, 0, 512, 0, 2), lhsT=C["HSEL2"], rhs=R["sq"],
                    start=True, stop=True),
                    (R["ksq"],) + CR, (f"ps{bk}",), "ssm"))
                st_.append(("V", lambda eng, R=R, bk=bk: eng.tensor_scalar(
                    out=R["ss"], in0=bank(bk, 0, 512, 0, 2), scalar1=1e-24,
                    scalar2=None, op0=ALU.max),
                    (f"ps{bk}",), (R["kss"],), "cl"))
                st_.append(("S", lambda eng, R=R: eng.activation(
                    out=R["ss"], in_=R["ss"], func=AF.Ln),
                    (R["kss"],), (R["kss"],), "ln"))
                st_.append(("S", lambda eng, R=R, bias=bias: eng.activation(
                    out=R["rn"], in_=R["ss"], func=AF.Exp, scale=-0.5, bias=bias),
                    (R["kss"],), (R["krn"],), "ex"))
                st_.append(("T", lambda eng, R=R, bk=bk: eng.matmul(
                    out=bank(bk), lhsT=C["BC2F"], rhs=R["rn"], start=True, stop=True),
                    (R["krn"],) + CR, (f"ps{bk}",), "rb"))
                st_.append(("V", lambda eng, R=R, bk=bk, scol=scol: eng.scalar_tensor_tensor(
                    out=R["ta"], in0=R["qraw"], scalar=scol, in1=bank(bk),
                    op0=ALU.mult, op1=ALU.mult),
                    (R["kqraw"], f"ps{bk}") + FBR, (R["kta"], "r_tail"), "qn"))
                st_.append(("V", lambda eng, R=R: eng.tensor_copy(
                    out=R["qnb"], in_=R["ta"]),
                    (R["kta"],), (R["kqnb"], "r_tail"), "qb"))
                st_.append(("T", lambda eng, R=R, bk=bk: eng.matmul(
                    out=bank(bk), lhsT=C["PERM"], rhs=R["qnb"], start=True, stop=True),
                    (R["kqnb"],) + CR, (f"ps{bk}",), "pm"))
                st_.append(("G", lambda eng, R=R, tcos=tcos, ch=ch: eng.tensor_tensor(
                    out=R["ta"], in0=R["ta"],
                    in1=ROTV(tcos)[:, ch * 512:(ch + 1) * 512], op=ALU.mult),
                    (R["kta"], f"rot.{tcos}", "r_rot"), (R["kta"],), "rc"))
                st_.append(("V", lambda eng, R=R, tsin=tsin, ch=ch, bk=bk: eng.tensor_tensor(
                    out=R["tb"], in0=bank(bk),
                    in1=ROTV(tsin)[:, ch * 512:(ch + 1) * 512], op=ALU.mult),
                    (f"ps{bk}", f"rot.{tsin}", "r_rot"), (R["ktb"], "r_tail"), "rs"))
                dst = (qT_t(f) if qk == 0 else kTf_t(f))[:, hc0 + ch * 512: hc0 + (ch + 1) * 512]
                dk = f"fqT.{f}" if qk == 0 else f"fkT.{f}"
                st_.append(("G", lambda eng, R=R, dst=dst: eng.tensor_tensor(
                    out=dst, in0=R["ta"], in1=R["tb"], op=ALU.add),
                    (R["kta"], R["ktb"]), (dk, "r_qT" if qk == 0 else "r_kT"), "ra"))
                steps.append(st_)
            for i in range(len(steps[0])):
                for ch in range(2):
                    eng, fn, rd, wr, nm = steps[ch][i]
                    B.emit(eng, fn, reads=rd, writes=wr, tag=f"{L}{nm}{qk}.{hh}.{f}.{ch}")

        f1mm(0)
        for t in range(1, 8):
            f1mm(t)
            f1post(t - 1)
        f1post(7)
        for ttl in range(8):
            tt = hh * 8 + ttl
            for dc in range(DC):
                def mm(eng, dc=dc, ttl=ttl):
                    return eng.matmul(
                        out=bank(5), lhsT=hT_t(dc)[:, ttl * 128:(ttl + 1) * 128],
                        rhs=wqkv_t(dc, 1024, 512), start=(dc == 0), stop=(dc == DC - 1))
                B.emit("T", mm, reads=(f"fqkv.{dc}", f"hT.{dc}", "r_wqkv", "r_wqspare",
                                        "r_w1b0", "r_w1b1", "r_hT"),
                       writes=("ps5",), tag=f"{L}v.{tt}.{dc}")

            def vev(eng, tt=tt):
                b0 = o_v + tt * 520
                dst = ar[:, b0: b0 + 520].rearrange("p (h c) -> p h c", h=FH)
                return eng.tensor_copy(out=dst[:, :, 0:64], in_=bank(5))
            B.emit("V", vev, reads=("ps5",), writes=(f"fv.{tt}", "r_v"), tag=f"{L}ve.{tt}")

    for ic in range(4):
        def ld(eng, ic=ic):
            return eng.dma_start(out=ar[:, o_wo + ic * 1024: o_wo + (ic + 1) * 1024],
                                 in_=fwo_d[l, ic * 128:(ic + 1) * 128, :])
        B.emit("G", ld, dma="g_fwo", writes=(f"fwo.{ic}", "r_wqkv"), tag=f"{L}ldwo{ic}")

    negmA = NEGM[:, 0:4]
    negmB = NEGM[:, 4:8]
    negmrA = NEGMR
    negmrB = NEGMR2
    PTF = [ar[:, o_f1s + i * 1024: o_f1s + (i + 1) * 1024] for i in range(2)]
    PTFK = ["f2pt0", "f2pt1"]
    RD2 = ctx["SCR01"]
    RD2B = ctx["RDENB"]
    HW = [(hp, w) for w in range(4) for hp in range(FH // 2)]

    def emitS1p(pi):
        hp, w = HW[pi]
        fq = hp
        SIDE = [(2 * hp, 0, 0), (2 * hp + 1, 64, 6)]

        def s1stage(r, side):
            h, pr, bp = SIDE[side]
            ti = 4 * w + r
            tj0 = max(0, ti - 4)
            njt = ti - tj0 + 1
            nj = njt * 128
            c0 = bp * 512
            segs = [(0, min(nj, 512))] + ([(512, nj)] if nj > 512 else [])
            bkeys = (f"ps{bp}",) if nj <= 512 else (f"ps{bp}", f"ps{bp + 1}")
            dpos = (ti - tj0) * 128
            for (s0, s1) in segs:
                def mm(eng, fq=fq, pr=pr, ti=ti, tj0=tj0, s0=s0, s1=s1, c0=c0,
                       stp=(s1 != dpos + 128 and not (ti >= 4 and s0 == 0))):
                    return eng.matmul(
                        out=ps[:, c0 + s0:c0 + s1],
                        lhsT=qT_t(fq)[pr:pr + 64, ti * 128:(ti + 1) * 128],
                        rhs=kTf_t(fq)[pr:pr + 64, tj0 * 128 + s0: tj0 * 128 + s1],
                        start=True, stop=stp)
                B.emit("T", mm, reads=(f"fqT.{fq}", f"fkT.{fq}", "r_qT", "r_kT"),
                       writes=bkeys, tag=f"{L}s1.{h}.{w}.{r}")
            def mskd(eng, dpos=dpos, c0=c0):
                return eng.matmul(out=ps[:, c0 + dpos:c0 + dpos + 128],
                                  lhsT=C["IDENT"], rhs=C["MTF"], start=False,
                                  stop=True)
            B.emit("T", mskd, reads=CR, writes=bkeys, tag=f"{L}s1d.{h}.{w}.{r}")
            if ti >= 4:
                def mskf(eng, c0=c0):
                    return eng.matmul(out=ps[:, c0:c0 + 128], lhsT=C["IDENT"],
                                      rhs=C["MTC"], start=False, stop=True)
                B.emit("T", mskf, reads=CR, writes=bkeys, tag=f"{L}s1f.{h}.{w}.{r}")

        def s1red(r, side):
            h, pr, bp = SIDE[side]
            ti = 4 * w + r
            nj = (ti - max(0, ti - 4) + 1) * 128
            bkeys = (f"ps{bp}",) if nj <= 512 else (f"ps{bp}", f"ps{bp + 1}")
            ng = negmA if side == 0 else negmB
            ngk = "negmA" if side == 0 else "negmB"

            def red(eng, nj=nj, r=r, bp=bp, ng=ng):
                return eng.tensor_reduce(out=ng[:, r:r + 1],
                                         in_=ps[:, bp * 512:bp * 512 + nj],
                                         axis=AX.X, op=ALU.max, negate=True)
            B.emit("V", red, reads=bkeys, writes=(ngk,), tag=f"{L}s1r.{h}.{w}.{r}")

        s1stage(0, 0)
        s1stage(0, 1)
        for r in range(1, 4):
            s1stage(r, 0)
            s1stage(r, 1)
            s1red(r - 1, 0)
            s1red(r - 1, 1)
        s1red(3, 0)
        s1red(3, 1)

    def emitS2p(pi):
        hp, w = HW[pi]
        fq = hp
        icols0 = w * 512
        SIDE = [(2 * hp, 0), (2 * hp + 1, 64)]
        B.emit("T", lambda eng: eng.transpose(out=bank(2, 0, 128, 0, 4), in_=negmA,
               identity=C["IDF"]),
               reads=("negmA",) + CR, writes=("ps2",), tag=f"{L}ntrA.{hp}.{w}")
        B.emit("T", lambda eng: eng.transpose(out=bank(2, 128, 256, 0, 4), in_=negmB,
               identity=C["IDF"]),
               reads=("negmB",) + CR, writes=("ps2",), tag=f"{L}ntrB.{hp}.{w}")
        B.emit("V", lambda eng: eng.tensor_copy(out=negmrA, in_=bank(2, 0, 128, 0, 4)),
               reads=("ps2",), writes=("negmrA",), tag=f"{L}nevA.{hp}.{w}")
        B.emit("V", lambda eng: eng.tensor_copy(out=negmrB, in_=bank(2, 128, 256, 0, 4)),
               reads=("ps2",), writes=("negmrB",), tag=f"{L}nevB.{hp}.{w}")
        tj0w = max(0, 4 * w - 4)
        tj1w = 4 * w + 3

        def s2stage(tj):
            zs = []
            tp = tj - 4 * w
            ka = max(0, tp)
            kb = min(3, tp + 4)
            masks = []
            if 0 <= tp <= 3:
                masks.append((tp, "MTC"))
            if 0 <= tp + 4 <= 3:
                masks.append((tp + 4, "MTF"))
            for side in range(2):
                h, pr = SIDE[side]
                sbk = 2 + side

                def mm(eng, tj=tj, fq=fq, pr=pr, icols0=icols0, sbk=sbk):
                    return eng.matmul(
                        out=bank(sbk),
                        lhsT=kTf_t(fq)[pr:pr + 64, tj * 128:(tj + 1) * 128],
                        rhs=qT_t(fq)[pr:pr + 64, icols0: icols0 + 512],
                        start=True, stop=False)
                B.emit("T", mm, reads=(f"fqT.{fq}", f"fkT.{fq}", "r_qT", "r_kT"),
                       writes=(f"ps{sbk}",), tag=f"{L}s2.{h}.{w}.{tj}")
            for side in range(2):
                h, pr = SIDE[side]
                sbk = 2 + side
                ngr = negmrA if side == 0 else negmrB
                ngrk = "negmrA" if side == 0 else "negmrB"
                for r in range(ka, kb + 1):
                    def selm(eng, r=r, sbk=sbk, ngr=ngr, stp=(r == kb)):
                        return eng.matmul(out=bank(sbk, r * 128, (r + 1) * 128),
                                          lhsT=C["SEL4B"][:, r * 128:(r + 1) * 128],
                                          rhs=ngr, start=False, stop=stp)
                    B.emit("T", selm, reads=(ngrk,) + CR, writes=(f"ps{sbk}",),
                           tag=f"{L}s2m.{h}.{w}.{tj}.{r}")
            pt = PTF[tj % 2]
            ptk = PTFK[tj % 2]

            def ex(eng, pt=pt):
                return eng.activation(out=pt[:, 0:1024], in_=bank(2, 0, 1024),
                                      func=AF.Exp)
            B.emit("S", ex, reads=("ps2", "ps3"), writes=(ptk, "r_tail"),
                   tag=f"{L}s2e.{hp}.{w}.{tj}")
            for side in range(2):
                for (r, mk) in masks:
                    tri = "TRIC" if mk == "MTC" else "TRIF"

                    def vmask(eng, pt=pt, side=side, r=r, tri=tri):
                        sl = pt[:, side * 512 + r * 128: side * 512 + (r + 1) * 128]
                        return eng.tensor_tensor(out=sl, in0=sl, in1=C[tri],
                                                 op=ALU.mult)
                    B.emit("V", vmask, reads=(ptk,) + CR, writes=(ptk, "r_tail"),
                           tag=f"{L}s2vm.{hp}.{w}.{tj}.{side}.{r}")
            for side in range(2):
                if ka > 0:
                    B.emit("V", lambda eng, pt=pt, side=side, ka=ka: eng.memset(
                        pt[:, side * 512:side * 512 + ka * 128], 0.0),
                        writes=(ptk, "r_tail"), tag=f"{L}pz0.{hp}.{w}.{tj}.{side}")
                if kb < 3:
                    B.emit("V", lambda eng, pt=pt, side=side, kb=kb: eng.memset(
                        pt[:, side * 512 + (kb + 1) * 128:side * 512 + 512], 0.0),
                        writes=(ptk, "r_tail"), tag=f"{L}pz1.{hp}.{w}.{tj}.{side}")

        def s2pv(tj):
            pt = PTF[tj % 2]
            ptk = PTFK[tj % 2]
            for side in range(2):
                h, pr = SIDE[side]
                nb = 4 + side

                def pv(eng, tj=tj, h=h, pt=pt, side=side, nb=nb,
                       first=(tj == tj0w), last=(tj == tj1w)):
                    return eng.matmul(out=bank(nb, 0, 512, 0, 65),
                                      lhsT=vf_t(tj, h),
                                      rhs=pt[:, side * 512:(side + 1) * 512],
                                      start=first, stop=last)
                B.emit("T", pv, reads=(f"fv.{tj}", ptk, "r_v"), writes=(f"ps{nb}",),
                       tag=f"{L}pv.{h}.{w}.{tj}")

        s2stage(tj0w)
        for k, tj in enumerate(range(tj0w + 1, tj1w + 1)):
            s2stage(tj)
            s2pv(tj - 1)
            if k == 2 and pi + 1 < len(HW):
                emitS1p(pi + 1)
        s2pv(tj1w)
        RDK = ("lnS0", "lnS1")
        i_ln = B.emit("S", lambda eng: eng.activation(
            out=RD2, in_=bank(4, 0, 1024, 64, 65), func=AF.Ln),
            reads=("ps4", "ps5"), writes=RDK, tag=f"{L}ld.{hp}.{w}")
        B.emit("S", lambda eng: eng.activation(
            out=RD2B, in_=RD2, func=AF.Exp, scale=-1.0),
            reads=RDK, writes=RDK + ("lnS2",), tag=f"{L}rd.{hp}.{w}")
        for side in range(2):
            h, pr = SIDE[side]
            nb = 4 + side
            rb = 2 + side
            B.emit("V", lambda eng, nb=nb: eng.tensor_copy(out=TF2[0:64, 0:512],
                   in_=bank(nb, 0, 512, 0, 64)),
                   reads=(f"ps{nb}",), writes=("tf2",), tag=f"{L}ne.{h}.{w}",
                   extra_deps=(i_ln,))
            B.emit("T", lambda eng, rb=rb, side=side: eng.matmul(
                out=bank(rb, 0, 512, 0, 64), lhsT=C["ONES1B"][:, 0:64],
                rhs=RD2B[:, side * 512:(side + 1) * 512], start=True, stop=True),
                reads=RDK + ("lnS2",) + CR, writes=(f"ps{rb}",), tag=f"{L}rb.{h}.{w}")

            def oev(eng, fq=fq, pr=pr, icols0=icols0, rb=rb):
                return eng.tensor_tensor(
                    out=oT_t(fq)[pr:pr + 64, icols0:icols0 + 512],
                    in0=TF2[0:64, 0:512], in1=bank(rb, 0, 512, 0, 64), op=ALU.mult)
            B.emit("V", oev, reads=("tf2", f"ps{rb}"), writes=(f"oT.{fq}", "r_hT"),
                   tag=f"{L}oe.{h}.{w}")

    emitS1p(0)
    for pi in range(len(HW)):
        emitS2p(pi)

    def w1load(hh, c):
        buf = c % 3
        hcs = HCS[c]

        def lda(eng, c=c, buf=buf):
            dst = ar[:, o_w1 + buf * 2048: o_w1 + buf * 2048 + 1024]
            return eng.dma_start(out=dst, in_=fw1_d[l, c])
        B.emit("G", lda, dma=f"g_w1p{buf}", writes=(f"w1a.{buf}", f"r_w1b{buf}"),
               tag=f"{L}lw1a.{hh}.{c}")

        def ldg(eng, c=c, buf=buf):
            dst = ar[:, o_w1 + buf * 2048 + 1024: o_w1 + buf * 2048 + 2048]
            return eng.dma_start(out=dst, in_=fw1_d[l, NHC + c])
        B.emit("G", ldg, dma=f"g_w1p{buf}", writes=(f"w1g.{buf}", f"r_w1b{buf}"),
               tag=f"{L}lw1g.{hh}.{c}")

    def w2load(c):
        def ld(eng, c=c):
            return eng.dma_start(
                out=ar[0:HCS[c], o_w2 + c * 1024: o_w2 + (c + 1) * 1024],
                in_=fw2_d[l, c * 128: c * 128 + HCS[c], :])
        B.emit("G", ld, dma="g_w2",
               writes=(f"fw2.{c}", "r_qT", "r_kT", "r_v"), tag=f"{L}ldw2.{c}")

    for c in range(3):
        w1load(0, c)
    for c in range(11):
        w2load(c)

    for tc in range(NCH):
        for dc in range(DC):
            bk = 6 + dc % 2
            for ic in range(4):
                def mm(eng, tc=tc, dc=dc, ic=ic, bk=bk):
                    return eng.matmul(out=bank(bk), lhsT=wo_tl(ic, dc),
                                      rhs=oT_t(ic)[:, tc * 512:(tc + 1) * 512],
                                      start=(ic == 0), stop=(ic == 3))
                B.emit("T", mm, reads=(f"fwo.{ic}", f"oT.{ic}", "r_wqkv", "r_hT"),
                       writes=(f"ps{bk}",), tag=f"{L}wo.{tc}.{dc}.{ic}")

            def res(eng, tc=tc, dc=dc, bk=bk):
                sl = xt[:, dc * T + tc * 512: dc * T + (tc + 1) * 512]
                return eng.tensor_tensor(out=sl, in0=sl, in1=bank(bk), op=ALU.add)
            B.emit("V", res, reads=(f"ps{bk}", f"xt.{dc}"), writes=(f"xt.{dc}",),
                   tag=f"{L}wr.{tc}.{dc}")


    for dc in range(DC):
        def bc(eng, dc=dc):
            return eng.tensor_copy(out=BCOL[:, dc:dc + 1],
                                   in_=xt[:, dc * T + 1023: dc * T + 1024])
        B.emit("V", bc, reads=(f"xt.{dc}",), writes=("bcol",), tag=f"{L}bc.{dc}")

    for hh in range(2):
        hc0 = hh * 1024
        for ch in range(2):
            c0 = hc0 + ch * 512
            for dc in range(DC):
                shifted = dc >= 4
                if not shifted:
                    def cst(eng, dc=dc, c0=c0):
                        return eng.tensor_copy(out=XB,
                                               in_=xt[:, dc * T + c0: dc * T + c0 + 512])
                    B.emit("V", cst, reads=(f"xt.{dc}",), writes=("xb",),
                           tag=f"{L}3c.{hh}.{ch}.{dc}")
                elif c0 == 0:
                    def cst(eng, dc=dc):
                        return eng.tensor_copy(out=XB[:, 1:512],
                                               in_=xt[:, dc * T: dc * T + 511])
                    B.emit("V", cst, reads=(f"xt.{dc}",), writes=("xb",),
                           tag=f"{L}3c.{hh}.{ch}.{dc}")
                    B.emit("V", lambda eng: eng.memset(XB[:, 0:1], 0.0),
                           writes=("xb",), tag=f"{L}3cz.{hh}.{ch}.{dc}")
                elif hh == 1 and ch == 0:
                    def cst(eng, dc=dc, c0=c0):
                        return eng.tensor_copy(out=XB[:, 1:512],
                                               in_=xt[:, dc * T + c0: dc * T + c0 + 511])
                    B.emit("V", cst, reads=(f"xt.{dc}",), writes=("xb",),
                           tag=f"{L}3c.{hh}.{ch}.{dc}")
                    B.emit("V", lambda eng, dc=dc: eng.tensor_copy(
                        out=XB[:, 0:1].bitcast(F32) if False else XB[:, 0:1],
                        in_=BCOL[:, dc:dc + 1]),
                        reads=("bcol",), writes=("xb",), tag=f"{L}3cb.{hh}.{ch}.{dc}")
                else:
                    def cst(eng, dc=dc, c0=c0):
                        return eng.tensor_copy(
                            out=XB, in_=xt[:, dc * T + c0 - 1: dc * T + c0 + 511])
                    B.emit("V", cst, reads=(f"xt.{dc}",), writes=("xb",),
                           tag=f"{L}3c.{hh}.{ch}.{dc}")
                B.emit("S", lambda eng: eng.activation(out=SQ, in_=XB, func=AF.Square),
                       reads=("xb",), writes=("sq",), tag=f"{L}3q.{hh}.{ch}.{dc}")

                def s1(eng, dc=dc):
                    return eng.matmul(out=bank(6, 0, 512, 0, 1), lhsT=C["ONESC"],
                                      rhs=XB, start=(dc == 0), stop=(dc == DC - 1))
                B.emit("T", s1, reads=("xb",) + CR, writes=("ps6",),
                       tag=f"{L}31.{hh}.{ch}.{dc}")

                def s2(eng, dc=dc):
                    return eng.matmul(out=bank(7, 0, 512, 0, 1), lhsT=C["ONESC"],
                                      rhs=SQ, start=(dc == 0), stop=(dc == DC - 1))
                B.emit("T", s2, reads=("sq",) + CR, writes=("ps7",),
                       tag=f"{L}32.{hh}.{ch}.{dc}")
            B.emit("V", lambda eng: eng.tensor_scalar(out=S0,
                   in0=bank(6, 0, 512, 0, 1), scalar1=1.0 / D, scalar2=None,
                   op0=ALU.mult), reads=("ps6",), writes=("lnS0",),
                   tag=f"{L}3mu.{hh}.{ch}")
            B.emit("V", lambda eng: eng.tensor_scalar(out=S1,
                   in0=bank(7, 0, 512, 0, 1), scalar1=1.0 / D, scalar2=None,
                   op0=ALU.mult), reads=("ps7",), writes=("lnS1",),
                   tag=f"{L}3ms.{hh}.{ch}")
            B.emit("V", lambda eng: eng.tensor_tensor(
                out=S2, in0=S0, in1=S0, op=ALU.mult),
                reads=("lnS0",), writes=("lnS2",), tag=f"{L}3v1.{hh}.{ch}")
            B.emit("V", lambda eng: eng.tensor_tensor(
                out=S2, in0=S1, in1=S2, op=ALU.subtract),
                reads=("lnS1", "lnS2"), writes=("lnS2",), tag=f"{L}3v2.{hh}.{ch}")
            B.emit("S", lambda eng: eng.activation(out=S2, in_=S2,
                   func=AF.Ln, bias=LN_EPS), reads=("lnS2",), writes=("lnS2",),
                   tag=f"{L}3l.{hh}.{ch}")
            B.emit("S", lambda eng: eng.activation(
                out=S1, in_=S2, func=AF.Exp, scale=-0.5),
                reads=("lnS2",), writes=("lnS1",), tag=f"{L}3e.{hh}.{ch}")
            B.emit("V", lambda eng: eng.scalar_tensor_tensor(
                out=S0, in0=S0, scalar=-1.0, in1=S1, op0=ALU.mult, op1=ALU.mult),
                reads=("lnS0", "lnS1"), writes=("lnS0",), tag=f"{L}3B.{hh}.{ch}")
            B.emit("T", lambda eng: eng.matmul(out=bank(6), lhsT=C["ONES1F"],
                   rhs=S1, start=True, stop=True), reads=("lnS1",) + CR,
                   writes=("ps6",), tag=f"{L}3bA.{hh}.{ch}")
            B.emit("T", lambda eng: eng.matmul(out=bank(7), lhsT=C["ONES1F"],
                   rhs=S0, start=True, stop=True), reads=("lnS0",) + CR,
                   writes=("ps7",), tag=f"{L}3bB.{hh}.{ch}")
            for dc in range(DC):
                shifted = dc >= 4
                if not shifted:
                    def a1(eng, dc=dc, c0=c0):
                        return eng.tensor_tensor(
                            out=TF0[:, 0:512], in0=xt[:, dc * T + c0: dc * T + c0 + 512],
                            in1=bank(6), op=ALU.mult)
                    B.emit("V", a1, reads=(f"xt.{dc}", "ps6"), writes=("tf0",),
                           tag=f"{L}3x1.{hh}.{ch}.{dc}")
                elif c0 == 0:
                    def a1(eng, dc=dc):
                        return eng.tensor_tensor(
                            out=TF0[:, 1:512], in0=xt[:, dc * T: dc * T + 511],
                            in1=bank(6, 1, 512), op=ALU.mult)
                    B.emit("V", a1, reads=(f"xt.{dc}", "ps6"), writes=("tf0",),
                           tag=f"{L}3x1.{hh}.{ch}.{dc}")
                    B.emit("V", lambda eng: eng.tensor_tensor(
                        out=TF0[:, 0:1], in0=C["ZCOL"], in1=bank(6, 0, 1), op=ALU.mult),
                        reads=CR + ("ps6",), writes=("tf0",), tag=f"{L}3xz.{hh}.{ch}.{dc}")
                elif hh == 1 and ch == 0:
                    def a1(eng, dc=dc, c0=c0):
                        return eng.tensor_tensor(
                            out=TF0[:, 1:512], in0=xt[:, dc * T + c0: dc * T + c0 + 511],
                            in1=bank(6, 1, 512), op=ALU.mult)
                    B.emit("V", a1, reads=(f"xt.{dc}", "ps6"), writes=("tf0",),
                           tag=f"{L}3x1.{hh}.{ch}.{dc}")
                    B.emit("V", lambda eng, dc=dc: eng.tensor_tensor(
                        out=TF0[:, 0:1], in0=BCOL[:, dc:dc + 1], in1=bank(6, 0, 1),
                        op=ALU.mult),
                        reads=("bcol", "ps6"), writes=("tf0",), tag=f"{L}3xb.{hh}.{ch}.{dc}")
                else:
                    def a1(eng, dc=dc, c0=c0):
                        return eng.tensor_tensor(
                            out=TF0[:, 0:512],
                            in0=xt[:, dc * T + c0 - 1: dc * T + c0 + 511],
                            in1=bank(6), op=ALU.mult)
                    B.emit("V", a1, reads=(f"xt.{dc}", "ps6"), writes=("tf0",),
                           tag=f"{L}3x1.{hh}.{ch}.{dc}")
                B.emit("V", lambda eng: eng.tensor_tensor(out=TF1[:, 0:512],
                       in0=TF0[:, 0:512], in1=bank(7), op=ALU.add),
                       reads=("tf0", "ps7"), writes=("tf1",), tag=f"{L}3x2.{hh}.{ch}.{dc}")

                def a3(eng, dc=dc, ch=ch):
                    return eng.tensor_scalar(
                        out=h2n_t(dc)[:, ch * 512:(ch + 1) * 512], in0=TF1[:, 0:512],
                        scalar1=LNG(l, dc), scalar2=LNB(l, dc), op0=ALU.mult,
                        op1=ALU.add)
                B.emit("V", a3, reads=("tf1",) + FBR, writes=(f"h2n.{dc}", "r_hT"),
                       tag=f"{L}3x3.{hh}.{ch}.{dc}")

        if hh == 1:
            for c in range(3):
                w1load(hh, c)
        for c in range(NHC):
            hcs = HCS[c]
            buf = c % 3

            for ch in range(2):
                for dc in range(DC):
                    def mma(eng, dc=dc, ch=ch, buf=buf, hcs=hcs):
                        w = ar[:, o_w1 + buf * 2048 + dc * 128: o_w1 + buf * 2048 + dc * 128 + 128]
                        return eng.matmul(out=bank(ch, 0, 512, 0, hcs),
                                          lhsT=w[:, 0:hcs],
                                          rhs=h2n_t(dc)[:, ch * 512:(ch + 1) * 512],
                                          start=(dc == 0), stop=(dc == DC - 1))
                    B.emit("T", mma, reads=(f"w1a.{buf}", f"h2n.{dc}", "r_hT", f"r_w1b{buf}"),
                           writes=(f"ps{ch}",), tag=f"{L}wa.{hh}.{c}.{ch}.{dc}")

                    def mmg(eng, dc=dc, ch=ch, buf=buf, hcs=hcs):
                        w = ar[:, o_w1 + buf * 2048 + 1024 + dc * 128: o_w1 + buf * 2048 + 1024 + dc * 128 + 128]
                        return eng.matmul(out=bank(2 + ch, 0, 512, 0, hcs),
                                          lhsT=w[:, 0:hcs],
                                          rhs=h2n_t(dc)[:, ch * 512:(ch + 1) * 512],
                                          start=(dc == 0), stop=(dc == DC - 1))
                    B.emit("T", mmg, reads=(f"w1g.{buf}", f"h2n.{dc}", "r_hT", f"r_w1b{buf}"),
                           writes=(f"ps{2 + ch}",), tag=f"{L}wg.{hh}.{c}.{ch}.{dc}")
                def gel(eng, ch=ch, hcs=hcs, c=c):
                    return eng.activation(out=GG[0:hcs, :],
                                          in_=bank(2 + ch, 0, 512, 0, hcs),
                                          func=AF.Gelu, bias=B1G(l, c)[0:hcs, 0:1])
                B.emit("S", gel, reads=(f"ps{2 + ch}",) + FBR, writes=("gg", "r_v"),
                       tag=f"{L}gg.{hh}.{c}.{ch}")

                def umul(eng, c=c, ch=ch, hcs=hcs):
                    return eng.scalar_tensor_tensor(
                        out=u_t(c)[0:hcs, ch * 512:(ch + 1) * 512],
                        in0=bank(ch, 0, 512, 0, hcs), scalar=B1A(l, c)[0:hcs, 0:1],
                        in1=GG[0:hcs, :], op0=ALU.add, op1=ALU.mult)
                B.emit("V", umul, reads=(f"ps{ch}", "gg", "r_v") + FBR,
                       writes=(f"u.{c}", u_alias[c]),
                       tag=f"{L}um.{hh}.{c}.{ch}")
            if c + 3 < NHC:
                w1load(hh, c + 3)
            if hh == 0 and 11 + c < NHC:
                w2load(11 + c)

        for ch in range(2):
            for dc in range(DC):
                bk = 4 + dc % 2
                for c in range(NHC):
                    def mm(eng, dc=dc, ch=ch, c=c, bk=bk):
                        return eng.matmul(out=bank(bk),
                                          lhsT=w2_tl(c, dc)[0:HCS[c], :],
                                          rhs=u_t(c)[0:HCS[c], ch * 512:(ch + 1) * 512],
                                          start=(c == 0), stop=(c == NHC - 1))
                    B.emit("T", mm, reads=(f"fw2.{c}", f"u.{c}", u_alias[c],
                                            "r_qT", "r_kT", "r_v"),
                           writes=(f"ps{bk}",), tag=f"{L}w2.{hh}.{dc}.{ch}.{c}")
                c0 = hc0 + ch * 512

                def res(eng, dc=dc, c0=c0, bk=bk):
                    sl = xt[:, dc * T + c0: dc * T + c0 + 512]
                    return eng.scalar_tensor_tensor(out=sl, in0=bank(bk),
                                                    scalar=B2C(l, dc), in1=sl,
                                                    op0=ALU.add, op1=ALU.add)
                B.emit("V", res, reads=(f"ps{bk}", f"xt.{dc}") + FBR,
                       writes=(f"xt.{dc}",), tag=f"{L}w2r.{hh}.{dc}.{ch}")
            if fin is not None:
                fin(hh * 2 + ch)



def _host_constants():
    cstf = np.zeros((128, 400), np.float32)
    cstf[:, 0:128] = np.eye(128, dtype=np.float32)
    cstf[0, 128:256] = 1.0
    cstf[0, 257:321] = 1.0
    cstf[1, 321:385] = 1.0
    for i, v in enumerate([EPS_A, LN_EPS, LN8, -1.0, -0.5, 1.0, 0.0]):
        cstf[:, 385 + i] = v
    sel = np.zeros((128, 512), np.float32)
    for c in range(4):
        sel[c, c * 128:(c + 1) * 128] = 1.0
    cstb = np.zeros((128, 1664), np.float32)
    cstb[:, 0:128] = np.eye(128, dtype=np.float32)
    idx = np.arange(128)
    cstb[:, 128:256] = np.where(idx[None, :] >= idx[:, None], 0.0, NEG)
    cstb[:, 256:384] = np.where(idx[None, :] <= idx[:, None], 0.0, NEG)
    P = np.zeros((128, 128), np.float32)
    for m in range(128):
        bse = (m // 64) * 64
        r = m % 64
        if r < 32:
            P[bse + r + 32, m] = -1.0
        else:
            P[bse + r - 32, m] = 1.0
    cstb[:, 384:512] = P
    cstb[0:64, 512] = 1.0
    cstb[64:128, 513] = 1.0
    cstb[:, 514] = 1.0
    cstb[:, 515:643] = NEG
    cstb[0:4, 643:1155] = sel[0:4, :]
    cstb[0, 1280:1408] = 1.0
    cstb[:, 1408:1536] = (idx[None, :] >= idx[:, None])
    cstb[:, 1536:1664] = (idx[None, :] <= idx[:, None])
    return cstf, cstb.astype(ml_dtypes.bfloat16)


def _host_rot():
    n = T
    half = np.arange(0, FDH, 2, dtype=np.float64)
    inv_freq = 1.0 / 10000.0 ** (half / FDH)
    t = np.arange(n, dtype=np.float64)
    freqs = t[:, None] * inv_freq[None, :]
    freqs = np.concatenate([freqs, freqs], axis=-1)
    bse = (half + 0.4 * FDH) / (1.4 * FDH)
    power = (t - n // 2) / float(WIN // 2)
    scale = bse[None, :] ** power[:, None]
    scale = np.concatenate([scale, scale], axis=-1)
    cos, sin = np.cos(freqs), np.sin(freqs)
    tabs = [cos * scale, sin * scale, cos / scale, sin / scale]
    out = np.zeros((4, 128, T), np.float32)
    for i, tb in enumerate(tabs):
        tt = tb.T.astype(np.float32)
        out[i, 0:64] = tt
        out[i, 64:128] = tt
    return out


def _host_fb(ins):
    fb = np.zeros((128, 160), np.float32)
    for l in range(2):
        qs = np.asarray(ins["L0_ffn_q_scale"][l], np.float32)
        ks = np.asarray(ins["L0_ffn_k_scale"][l], np.float32)
        fb[0:64, l * 2] = qs
        fb[64:128, l * 2] = qs
        fb[0:64, l * 2 + 1] = ks
        fb[64:128, l * 2 + 1] = ks
        fb[:, 4 + l * 16: 12 + l * 16] = np.asarray(
            ins["L0_ffn_ln_g"][l], np.float32).reshape(8, 128).T
        fb[:, 12 + l * 16: 20 + l * 16] = np.asarray(
            ins["L0_ffn_ln_b"][l], np.float32).reshape(8, 128).T
        fb[:, 36 + l * 8: 44 + l * 8] = np.asarray(
            ins["L0_ffn_b2"][l], np.float32).reshape(8, 128).T
        b1 = np.asarray(ins["L0_ffn_b1"][l], np.float32)
        b1a = np.zeros(22 * 128, np.float32)
        b1a[0:HID] = b1[0:HID]
        b1g = np.zeros(22 * 128, np.float32)
        b1g[0:HID] = b1[HID:]
        fb[:, 52 + l * 43: 52 + l * 43 + 22] = b1a.reshape(22, 128).T
        fb[:, 52 + l * 43 + 22: 52 + l * 43 + 44] = b1g.reshape(22, 128).T
    return fb


def _host_fw1p(w1):
    out = np.zeros((2, 2 * NHC, 128, 1024), np.float32)
    for l in range(2):
        for half in range(2):
            for c in range(NHC):
                hcs = HCS[c]
                blk = w1[l][:, half * HID + c * 128: half * HID + c * 128 + hcs]
                blk = blk.reshape(8, 128, hcs).transpose(1, 0, 2)
                out[l, half * NHC + c, :, :].reshape(128, 8, 128)[:, :, 0:hcs] = blk
    return np.ascontiguousarray(out)


def host_inputs(ins):
    cstf, cstb = _host_constants()
    common = {
        "wqT": np.ascontiguousarray(np.asarray(ins["L0_attn_Wq"]).T),
        "wkvT": np.ascontiguousarray(np.asarray(ins["L0_attn_Wkv"]).T),
        "woT": np.ascontiguousarray(np.asarray(ins["L0_attn_Wo"]).T),
        "fqkv": np.ascontiguousarray(np.asarray(ins["L0_ffn_wqkv"], np.float32)),
        "fwo": np.ascontiguousarray(np.asarray(ins["L0_ffn_wo"], np.float32)),
        "fw1p": _host_fw1p(np.asarray(ins["L0_ffn_w1"], np.float32)),
        "fw2": np.ascontiguousarray(np.asarray(ins["L0_ffn_w2"], np.float32)),
        "fb": _host_fb(ins), "cstf": cstf, "cstb": cstb, "rot": _host_rot(),
    }
    return common


_NC_CACHE = {}


def kernel(**ins):
    if "main" not in _NC_CACHE:
        _NC_CACHE["main"] = build_program()
    nc = _NC_CACHE["main"]
    x = np.asarray(ins["L0_attn_x"])
    common = host_inputs(ins)
    in_maps = []
    for c in range(8):
        m = dict(common)
        m["xt"] = np.ascontiguousarray(x[c].T.astype(np.float32))
        in_maps.append(m)
    res = run_bass_kernel_spmd(nc, in_maps, core_ids=list(range(8)))
    out = np.stack([np.ascontiguousarray(np.asarray(r["out_xt"]).T)
                    for r in res.results], axis=0)
    return out.astype(np.float32)


# version 31
# speedup vs baseline: 1.0457x; 1.0157x over previous
import math
import numpy as np
import ml_dtypes
import concourse.bass as bass
import concourse.mybir as mybir
from concourse.bass_utils import run_bass_kernel_spmd

F32 = mybir.dt.float32
BF16 = mybir.dt.bfloat16
AF = mybir.ActivationFunctionType
ALU = mybir.AluOpType
AX = mybir.AxisListType

T = 2048
D = 1024
HA = 16
NEG = -1.0e30
EPS_A = 1e-8
LN_EPS = 1e-5
FH = 8
FDH = 64
INNER = 512
WIN = 512
HID = 2730
LN8 = math.log(8.0)
NHC = 22
HCS = [128] * 21 + [42]
TT = 16
NCH = 4
DC = 8
ARN = 70400

ENGS = ("T", "V", "S", "Q", "G")


class Bld:
    def __init__(self):
        self.ops = []
        self.last_w = {}
        self.readers = {}
        self.last_on_eng = {e: None for e in ENGS}
        self.dma_sems = []

    def emit(self, eng, fn, reads=(), writes=(), dma=None, tag="", extra_deps=()):
        deps = set(extra_deps)
        for k in reads:
            if k in self.last_w:
                deps.add(self.last_w[k])
        for k in writes:
            if k in self.last_w:
                deps.add(self.last_w[k])
            for r in self.readers.get(k, ()):
                deps.add(r)
        i = len(self.ops)
        deps.discard(i)
        if dma is not None and dma not in self.dma_sems:
            self.dma_sems.append(dma)
        self.ops.append(dict(eng=eng, fn=fn, deps=deps, dma=dma, tag=tag, inc=None))
        for k in reads:
            self.readers.setdefault(k, []).append(i)
        for k in writes:
            self.last_w[k] = i
            self.readers[k] = []
        self.last_on_eng[eng] = i
        return i

    def barrier(self, tag="bar"):
        prev = tuple(i for i in self.last_on_eng.values() if i is not None)
        for e in ENGS:
            self.emit(e, lambda eng: eng.nop(), tag=tag + "." + e, extra_deps=prev)

    def finalize(self, blk, sems):
        ops = self.ops
        need_inc = [False] * len(ops)
        for op in ops:
            for d in op["deps"]:
                if ops[d]["eng"] == op["eng"] == "T":
                    continue
                need_inc[d] = True
        cnt = {e: 0 for e in ENGS}
        scnt = {s: 0 for s in self.dma_sems}
        sem_hist = {s: [] for s in self.dma_sems}
        for i, op in enumerate(ops):
            if op["dma"] is not None:
                s = op["dma"]
                scnt[s] += 16
                op["inc"] = scnt[s]
                op["sem"] = s
                sem_hist[s].append((i, scnt[s]))
            elif need_inc[i]:
                cnt[op["eng"]] += 1
                op["inc"] = cnt[op["eng"]]
                op["sem"] = op["eng"]
        eng_clock = {e: {} for e in ENGS}
        op_clock = [None] * len(ops)
        import bisect
        for i, op in enumerate(ops):
            e = op["eng"]
            waits = []
            for d in sorted(op["deps"]):
                dop = ops[d]
                if dop["eng"] == e == "T":
                    continue
                if dop["dma"] is not None:
                    s = dop["sem"]
                    hist = sem_hist[s]
                    pos = bisect.bisect_left(hist, (i, -1)) - 1
                    assert pos >= 0
                    v = hist[pos][1]
                else:
                    s = dop.get("sem")
                    v = dop["inc"]
                    assert v is not None, (i, op["tag"], d, dop["tag"])
                if eng_clock[e].get(s, 0) >= v:
                    continue
                waits.append((s, v))
                for k2, v2 in op_clock[d].items():
                    if k2 != e and eng_clock[e].get(k2, 0) < v2:
                        eng_clock[e][k2] = v2
                if eng_clock[e].get(s, 0) < v:
                    eng_clock[e][s] = v
            op["waits"] = waits
            snap = dict(eng_clock[e])
            if op["inc"] is not None:
                snap[op["sem"]] = op["inc"]
            op_clock[i] = snap
        self._final_dma = [(s, scnt[s]) for s in self.dma_sems if scnt[s] > 0]
        for e, engname in [("T", "tensor"), ("V", "vector"), ("S", "scalar"),
                           ("Q", "sync"), ("G", "gpsimd")]:
            ops_e = [op for op in ops if op["eng"] == e]
            fin = self._final_dma if e == "Q" else ()

            def body(eng, ops_e=ops_e, fin=fin):
                for op in ops_e:
                    for (s, v) in op["waits"]:
                        eng.wait_ge(sems[s], v)
                    inst = op["fn"](eng)
                    if op["inc"] is not None:
                        inst.then_inc(sems[op["sem"]], 16 if op["dma"] is not None else 1)
                for (s, v) in fin:
                    eng.wait_ge(sems[s], v)
            getattr(blk, engname)(body)


def build_program(dbg=()):
    nc = bass.Bass("TRN2", target_bir_lowering=False)

    xt_d = nc.dram_tensor("xt", [D, T], F32, kind="ExternalInput").ap()
    wq_d = nc.dram_tensor("wqT", [D, D], F32, kind="ExternalInput").ap()
    wkv_d = nc.dram_tensor("wkvT", [D, 2 * D], F32, kind="ExternalInput").ap()
    wo_d = nc.dram_tensor("woT", [D, D], F32, kind="ExternalInput").ap()
    fqkv_d = nc.dram_tensor("fqkv", [2, D, 3 * INNER], F32, kind="ExternalInput").ap()
    fwo_d = nc.dram_tensor("fwo", [2, INNER, D], F32, kind="ExternalInput").ap()
    fw1_d = nc.dram_tensor("fw1p", [2, 2 * NHC, 128, 1024], F32,
                           kind="ExternalInput").ap()
    fw2_d = nc.dram_tensor("fw2", [2, HID, D], F32, kind="ExternalInput").ap()
    fb_d = nc.dram_tensor("fb", [128, 160], F32, kind="ExternalInput").ap()
    cstf_d = nc.dram_tensor("cstf", [128, 400], F32, kind="ExternalInput").ap()
    cstb_d = nc.dram_tensor("cstb", [128, 1664], BF16, kind="ExternalInput").ap()
    rot_d = nc.dram_tensor("rot", [4, 128, T], F32, kind="ExternalInput").ap()
    xin_d = nc.dram_tensor("xin_spill", [D, T], F32, kind="Internal").ap()
    out_d = nc.dram_tensor("out_xt", [D, T], F32, kind="ExternalOutput").ap()
    dbg_d = {}
    for name, shape in dbg:
        dbg_d[name] = nc.dram_tensor(name, shape, F32, kind="ExternalOutput").ap()

    import contextlib
    with contextlib.ExitStack() as stk:
        XT = stk.enter_context(nc.sbuf_tensor("XT", [128, DC * T], F32))
        AR = stk.enter_context(nc.sbuf_tensor("AR", [128, ARN], BF16))
        CSTF = stk.enter_context(nc.sbuf_tensor("CSTF", [128, 400], F32))
        CSTB = stk.enter_context(nc.sbuf_tensor("CSTB", [128, 1664], BF16))
        FBIA = stk.enter_context(nc.sbuf_tensor("FBIA", [128, 160], F32))
        PS = stk.enter_context(nc.psum_tensor("PS", [128, 4096], F32))
        sems = {e: stk.enter_context(nc.semaphore("sem" + e)) for e in ENGS}
        for s in ["q_cst", "q_xt", "q_rot", "q_spill", "q_out",
                  "g_cst", "g_wa", "g_fqkv", "g_fwo", "g_w2", "g_w1p0", "g_w1p1",
                  "g_w1p2", "g_xin"] + [f"q_wqs{dc}" for dc in range(DC)]:
            sems[s] = stk.enter_context(nc.semaphore(s))
        blk = stk.enter_context(nc.Block())

        B = Bld()
        ar = AR.ap()
        xt = XT.ap()
        ps = PS.ap()
        cstf = CSTF.ap()
        cstb = CSTB.ap()
        fbia = FBIA.ap()

        C = dict(
            IDENT=cstb[:, 0:128], MTC=cstb[:, 128:256], MTF=cstb[:, 256:384],
            PERM=cstb[:, 384:512], HSEL2=cstb[:, 512:514], ONESC=cstb[:, 514:515],
            MNEG=cstb[:, 515:643], SEL4B=cstb[0:4, 643:1155],
            ONES1B=cstb[0:1, 1280:1408],
            TRIC=cstb[:, 1408:1536], TRIF=cstb[:, 1536:1664],
            IDF=cstf[:, 0:128], ONES1F=cstf[0:1, 128:256],
            ZCOL=cstf[:, 256:257], BC2F=cstf[0:2, 257:385],
        )
        CR = ("cst", "cstb")
        FBR = ("fbia",)
        for i, v in enumerate([EPS_A, LN_EPS, LN8, -1.0, -0.5, 1.0, 0.0]):
            nc.const_aps.register(F32, v, cstf[:, 385 + i: 386 + i])

        def QS2(l): return fbia[:, l * 2: l * 2 + 1]
        def KS2(l): return fbia[:, l * 2 + 1: l * 2 + 2]
        def LNG(l, dc): return fbia[:, 4 + l * 16 + dc: 5 + l * 16 + dc]
        def LNB(l, dc): return fbia[:, 12 + l * 16 + dc: 13 + l * 16 + dc]
        def B2C(l, dc): return fbia[:, 36 + l * 8 + dc: 37 + l * 8 + dc]
        def B1A(l, c): return fbia[:, 52 + l * 43 + c: 53 + l * 43 + c]
        def B1G(l, c): return fbia[:, 52 + l * 43 + 22 + c: 53 + l * 43 + 22 + c]

        def bank(i, c0=0, c1=512, p0=0, p1=128):
            return ps[p0:p1, i * 512 + c0: i * 512 + c1]

        o = 0
        o_scr, o = o, o + 3072
        o_pt0, o = o, o + 512
        o_pt1, o = o, o + 512
        o_xb, o = o, o + 512
        o_sq, o = o, o + 512
        o_tf0, o = o, o + 1024
        o_tf1, o = o, o + 1024
        o_tf2, o = o, o + 1024
        o_negm, o = o, o + 16
        o_negmr, o = o, o + 256
        o_bcol, o = o, o + 16
        o = (o + 127) // 128 * 128
        BASE = o
        assert BASE == 8576, BASE
        SCR = ar[:, o_scr:o_scr + 3072].bitcast(F32)
        S0 = SCR[0:1, 0:512]
        S1 = SCR[0:1, 512:1024]
        S2 = SCR[0:1, 1024:1536]
        S0b = SCR[0:2, 0:512]
        S1b = SCR[0:2, 512:1024]
        ROW_AB = ar[0:1, o_scr + 1024:o_scr + 1536]
        ROW_BB = ar[0:1, o_scr + 2048:o_scr + 2560]
        RDENB = ar[0:1, o_scr + 2048:o_scr + 3072]
        PT = [ar[:, o_pt0:o_pt0 + 512], ar[:, o_pt1:o_pt1 + 512]]
        XB = ar[:, o_xb:o_xb + 512]
        SQ = ar[:, o_sq:o_sq + 512]
        TF0 = ar[:, o_tf0:o_tf0 + 1024].bitcast(F32)
        TF1 = ar[:, o_tf1:o_tf1 + 1024].bitcast(F32)
        TF1RAW = ar[:, o_tf1:o_tf1 + 1024]
        TF2 = ar[:, o_tf2:o_tf2 + 1024].bitcast(F32)
        NEGM = ar[:, o_negm:o_negm + 16].bitcast(F32)
        NEGMR = ar[0:4, o_negmr:o_negmr + 128]
        NEGMR2 = ar[0:4, o_negmr + 128:o_negmr + 256]
        BCOL = ar[:, o_bcol:o_bcol + 16].bitcast(F32)

        ctx = dict(B=B, ar=ar, xt=xt, ps=ps, bank=bank, C=C, CR=CR, FBR=FBR,
                   S0=S0, S1=S1, S2=S2, S0b=S0b, S1b=S1b, SCR01=SCR[0:1, 0:1024],
                   ROW_AB=ROW_AB, ROW_BB=ROW_BB, RDENB=RDENB,
                   TF1RAW=TF1RAW,
                   PT=PT, XB=XB, SQ=SQ, TF0=TF0, TF1=TF1,
                   TF2=TF2, NEGM=NEGM, NEGMR=NEGMR, NEGMR2=NEGMR2, BCOL=BCOL,
                   QS2=QS2, KS2=KS2,
                   LNG=LNG, LNB=LNB, B2C=B2C, B1A=B1A, B1G=B1G, dbg=dbg_d, XT=XT)

        B.emit("Q", lambda eng: eng.dma_start(out=CSTF[:], in_=cstf_d[:, :]),
               dma="q_cst", writes=CR, tag="ld_cstf")
        B.emit("Q", lambda eng: eng.dma_start(out=FBIA[:], in_=fb_d[:, :]),
               dma="q_cst", writes=FBR, tag="ld_fb")
        B.emit("G", lambda eng: eng.dma_start(out=CSTB[:], in_=cstb_d[:, :]),
               dma="g_cst", writes=("cstb",), tag="ld_cstb")
        for dc in range(DC):
            def ldx(eng, dc=dc):
                return eng.dma_start(out=XT[:, dc * T:(dc + 1) * T],
                                     in_=xt_d[dc * 128:(dc + 1) * 128, :])
            B.emit("Q", ldx, dma="q_xt", writes=(f"xt.{dc}",), tag=f"ld_xt{dc}")

        phaseA(ctx, wq_d, wkv_d, wo_d, BASE)
        B.barrier("endA")

        for dc in range(DC):
            def sp(eng, dc=dc):
                return eng.dma_start(out=xin_d[dc * 128:(dc + 1) * 128, :],
                                     in_=XT[:, dc * T:(dc + 1) * T])
            B.emit("Q", sp, dma="q_spill", reads=(f"xt.{dc}",),
                   writes=(f"xin.{dc}",), tag=f"spill{dc}")
        if "dbg_pa" in dbg_d:
            for dc in range(DC):
                def dp(eng, dc=dc):
                    return eng.dma_start(
                        out=dbg_d["dbg_pa"][dc * 128:(dc + 1) * 128, :],
                        in_=XT[:, dc * T:(dc + 1) * T])
                B.emit("Q", dp, dma="q_out", reads=(f"xt.{dc}",), tag=f"dbgpa{dc}")

        _fs = [(ar[:, 67072 + i * 512: 67072 + (i + 1) * 512], f"fsc{i}")
               for i in range(2)]
        FSCR = [_fs[0], _fs[1], _fs[0], _fs[1]]

        def final_chunk(tc):
            for dc in range(DC):
                key = f"xt.{dc}"
                deps = set()
                if key in B.last_w:
                    deps.add(B.last_w[key])
                deps.update(B.readers.get(key, ()))

                def acc(eng, dc=dc, tc=tc):
                    return eng.dma_start(
                        out=XT[:, dc * T + tc * 512: dc * T + (tc + 1) * 512],
                        in_=xin_d[dc * 128:(dc + 1) * 128, tc * 512:(tc + 1) * 512],
                        accum_op=ALU.add)
                B.emit("G", acc, dma="g_xin", reads=(f"xin.{dc}", "r_tail"),
                       writes=(f"xtf.{tc}.{dc}",), extra_deps=tuple(deps),
                       tag=f"facc.{tc}.{dc}")
            ln_chunk(ctx, tc * 512, f"fin{tc}", scr=FSCR,
                     kf=lambda dc: f"xtf.{tc}.{dc}")
            for dc in range(DC):
                def st(eng, dc=dc, tc=tc):
                    return eng.dma_start(
                        out=out_d[dc * 128:(dc + 1) * 128, tc * 512:(tc + 1) * 512],
                        in_=xt[:, dc * T + tc * 512: dc * T + (tc + 1) * 512])
                B.emit("Q", st, dma="q_out", reads=(f"xtf.{tc}.{dc}",),
                       tag=f"stout.{tc}.{dc}")

        for l in range(2):
            ffn_layer(ctx, l, fqkv_d, fwo_d, fw1_d, fw2_d, rot_d, BASE,
                      fin=final_chunk if l == 1 else None)
            if f"dbg_l{l}" in dbg_d:
                B.barrier(f"dbgl{l}")
                for dc in range(DC):
                    def dp(eng, dc=dc, l=l):
                        return eng.dma_start(
                            out=dbg_d[f"dbg_l{l}"][dc * 128:(dc + 1) * 128, :],
                            in_=XT[:, dc * T:(dc + 1) * T])
                    B.emit("Q", dp, dma="q_out", reads=(f"xt.{dc}",), tag=f"dbgl{l}.{dc}")

        B.barrier("preF")
        B.barrier("final")

        B.finalize(blk, sems)
    return nc


def ln_chunk(ctx, c0, tg, scr=None, kf=None):
    B, xt, bank = ctx["B"], ctx["xt"], ctx["bank"]
    C, CR = ctx["C"], ctx["CR"]
    S0, S1, S2 = ctx["S0"], ctx["S1"], ctx["S2"]
    TF0 = ctx["TF0"]
    if scr is None:
        scr = [(ctx["XB"], "xb"), (ctx["SQ"], "sq"), (ctx["XB"], "xb"), (ctx["SQ"], "sq")]
    if kf is None:
        kf = lambda dc: f"xt.{dc}"
    n = 512
    for dc in range(DC):
        XBp, xbk = scr[(dc % 2) * 2]
        SQp, sqk = scr[(dc % 2) * 2 + 1]

        def cst(eng, dc=dc, XBp=XBp):
            return eng.tensor_copy(out=XBp, in_=xt[:, dc * T + c0: dc * T + c0 + n])
        B.emit("V", cst, reads=(kf(dc),), writes=(xbk,), tag=f"{tg}lc{dc}")
        B.emit("S", lambda eng, XBp=XBp, SQp=SQp: eng.activation(
            out=SQp, in_=XBp, func=AF.Square),
            reads=(xbk,), writes=(sqk,), tag=f"{tg}lq{dc}")

        def s1(eng, dc=dc, XBp=XBp):
            return eng.matmul(out=bank(6, 0, n, 0, 1), lhsT=C["ONESC"], rhs=XBp,
                              start=(dc == 0), stop=(dc == DC - 1))
        B.emit("T", s1, reads=(xbk,) + CR, writes=("ps6",), tag=f"{tg}l1{dc}")

        def s2(eng, dc=dc, SQp=SQp):
            return eng.matmul(out=bank(7, 0, n, 0, 1), lhsT=C["ONESC"], rhs=SQp,
                              start=(dc == 0), stop=(dc == DC - 1))
        B.emit("T", s2, reads=(sqk,) + CR, writes=("ps7",), tag=f"{tg}l2{dc}")
    B.emit("V", lambda eng: eng.tensor_scalar(out=S0, in0=bank(6, 0, n, 0, 1),
           scalar1=1.0 / D, scalar2=None, op0=ALU.mult),
           reads=("ps6",), writes=("lnS0",), tag=f"{tg}mu")
    B.emit("V", lambda eng: eng.tensor_scalar(out=S1, in0=bank(7, 0, n, 0, 1),
           scalar1=1.0 / D, scalar2=None, op0=ALU.mult),
           reads=("ps7",), writes=("lnS1",), tag=f"{tg}ms")
    B.emit("V", lambda eng: eng.tensor_tensor(out=S2, in0=S0, in1=S0, op=ALU.mult),
           reads=("lnS0",), writes=("lnS2",), tag=f"{tg}v1")
    B.emit("V", lambda eng: eng.tensor_tensor(out=S2, in0=S1, in1=S2, op=ALU.subtract),
           reads=("lnS1", "lnS2"), writes=("lnS2",), tag=f"{tg}v2")
    B.emit("S", lambda eng: eng.activation(out=S2, in_=S2, func=AF.Ln, bias=LN_EPS),
           reads=("lnS2",), writes=("lnS2",), tag=f"{tg}ln")
    B.emit("S", lambda eng: eng.activation(out=S1, in_=S2, func=AF.Exp, scale=-0.5),
           reads=("lnS2",), writes=("lnS1",), tag=f"{tg}ex")
    B.emit("V", lambda eng: eng.scalar_tensor_tensor(out=S0, in0=S0, scalar=-1.0,
           in1=S1, op0=ALU.mult, op1=ALU.mult),
           reads=("lnS0", "lnS1"), writes=("lnS0",), tag=f"{tg}B")
    B.emit("T", lambda eng: eng.matmul(out=bank(6), lhsT=C["ONES1F"], rhs=S1,
           start=True, stop=True), reads=("lnS1",) + CR, writes=("ps6",), tag=f"{tg}bA")
    B.emit("T", lambda eng: eng.matmul(out=bank(7), lhsT=C["ONES1F"], rhs=S0,
           start=True, stop=True), reads=("lnS0",) + CR, writes=("ps7",), tag=f"{tg}bB")
    for dc in range(DC):
        def a1(eng, dc=dc):
            return eng.tensor_tensor(out=TF0[:, 0:n],
                                     in0=xt[:, dc * T + c0: dc * T + c0 + n],
                                     in1=bank(6), op=ALU.mult)
        B.emit("V", a1, reads=(kf(dc), "ps6"), writes=("tf0",), tag=f"{tg}a1.{dc}")

        def a2(eng, dc=dc):
            return eng.tensor_tensor(out=xt[:, dc * T + c0: dc * T + c0 + n],
                                     in0=TF0[:, 0:n], in1=bank(7), op=ALU.add)
        B.emit("V", a2, reads=("tf0", "ps7"), writes=(kf(dc),), tag=f"{tg}a2.{dc}")


def phaseA(ctx, wq_d, wkv_d, wo_d, base):
    B = ctx["B"]
    ar, xt, bank = ctx["ar"], ctx["xt"], ctx["bank"]
    C, CR = ctx["C"], ctx["CR"]
    PT, TF2 = ctx["PT"], ctx["TF2"]
    S0, S1, S2 = ctx["S0"], ctx["S1"], ctx["S2"]
    TF0 = ctx["TF0"]

    o_kt = base
    o_v = o_kt + 16384
    o_xbfc = o_v + 16640
    o_qtc = o_xbfc + 4096
    o_otc = o_qtc + 4096
    o_wsl = o_otc + 4096
    assert o_wsl + 16384 <= ARN, o_wsl + 16384

    def kT_t(f):
        return ar[:, o_kt + f * 2048: o_kt + (f + 1) * 2048]

    def v_t(tt, h):
        b = o_v + tt * 1040 + h * 65
        return ar[:, b: b + 65]

    def xbfc_t(dc):
        return ar[:, o_xbfc + dc * 512: o_xbfc + (dc + 1) * 512]

    def qtc_t(f):
        return ar[:, o_qtc + f * 512: o_qtc + (f + 1) * 512]

    def otc_t(ic):
        return ar[:, o_otc + ic * 512: o_otc + (ic + 1) * 512]

    def wkv_t(dc, f0, nn):
        return ar[:, o_wsl + dc * 2048 + f0: o_wsl + dc * 2048 + f0 + nn]

    def wq_t(dc, f):
        return ar[:, o_wsl + dc * 1024 + f * 128: o_wsl + dc * 1024 + (f + 1) * 128]

    def wo_t(ic, dc):
        b = o_wsl + 8192 + ic * 1024 + dc * 128
        return ar[:, b: b + 128]

    def vones(eng):
        vv = ar[:, o_v:o_v + 16640].rearrange("p (t h c) -> p t h c", t=TT, h=HA)
        return eng.memset(vv[:, :, :, 64:65], 1.0)
    B.emit("V", vones, writes=tuple(f"v.{tt}" for tt in range(TT)), tag="v_ones")

    for dc in range(DC):
        def ld(eng, dc=dc):
            return eng.dma_start(out=wkv_t(dc, 0, 2048),
                                 in_=wkv_d[dc * 128:(dc + 1) * 128, :])
        B.emit("G", ld, dma="g_wa", writes=(f"wkv.{dc}", "wslot"), tag=f"ld_wkv{dc}")

    for tc in range(NCH):
        for dc in range(DC):
            def cst(eng, dc=dc, tc=tc):
                return eng.tensor_copy(
                    out=xbfc_t(dc),
                    in_=xt[:, dc * T + tc * 512: dc * T + (tc + 1) * 512])
            B.emit("V", cst, reads=(f"xt.{dc}",), writes=(f"xbfc.{dc}",),
                   tag=f"cxkv.{tc}.{dc}")
        for f in range(8):
            bk = f % 2
            for dc in range(DC):
                def mm(eng, f=f, dc=dc, bk=bk):
                    return eng.matmul(out=bank(bk),
                                      lhsT=wkv_t(dc, f * 128, 128), rhs=xbfc_t(dc),
                                      start=(dc == 0), stop=(dc == DC - 1))
                B.emit("T", mm, reads=(f"wkv.{dc}", f"xbfc.{dc}", "wslot"),
                       writes=(f"ps{bk}",), tag=f"kmm.{tc}.{f}.{dc}")

            def ev(eng, f=f, tc=tc, bk=bk):
                return eng.tensor_copy(out=kT_t(f)[:, tc * 512:(tc + 1) * 512],
                                       in_=bank(bk))
            B.emit("V", ev, reads=(f"ps{bk}",), writes=(f"kT.{f}",), tag=f"kev.{tc}.{f}")
        for ttl in range(4):
            tt = tc * 4 + ttl
            for fc in range(2):
                bk = 2 + fc
                for dc in range(DC):
                    def mm(eng, ttl=ttl, fc=fc, dc=dc, bk=bk):
                        return eng.matmul(
                            out=bank(bk),
                            lhsT=xbfc_t(dc)[:, ttl * 128:(ttl + 1) * 128],
                            rhs=wkv_t(dc, 1024 + fc * 512, 512),
                            start=(dc == 0), stop=(dc == DC - 1))
                    B.emit("T", mm, reads=(f"wkv.{dc}", f"xbfc.{dc}", "wslot"),
                           writes=(f"ps{bk}",), tag=f"vmm.{tt}.{fc}.{dc}")

                def ev(eng, tt=tt, fc=fc, bk=bk):
                    b0 = o_v + tt * 1040 + fc * 8 * 65
                    dst = ar[:, b0: b0 + 8 * 65].rearrange("p (h c) -> p h c", h=8)
                    return eng.tensor_copy(out=dst[:, :, 0:64], in_=bank(bk))
                B.emit("V", ev, reads=(f"ps{bk}",), writes=(f"v.{tt}",),
                       tag=f"vev.{tt}.{fc}")

    def wqs_slot(s):
        return ar[:, o_qtc + s * 2048: o_qtc + (s + 1) * 2048].bitcast(F32)
    SLOTK = [tuple(f"qtc.{f}" for f in range(4)),
             tuple(f"qtc.{f}" for f in range(4, 8)),
             tuple(f"otc.{ic}" for ic in range(4)),
             tuple(f"otc.{ic}" for ic in range(4, 8))]
    for dc in range(DC):
        s = dc % 4

        def ldq(eng, dc=dc, s=s):
            return eng.dma_start(out=wqs_slot(s),
                                 in_=wq_d[dc * 128:(dc + 1) * 128, :])
        B.emit("Q", ldq, dma=f"q_wqs{dc}", writes=(f"wqs.{dc}",) + SLOTK[s],
               tag=f"ldwqs{dc}")

        def cq(eng, dc=dc, s=s):
            return eng.tensor_copy(
                out=ar[:, o_wsl + dc * 1024: o_wsl + (dc + 1) * 1024],
                in_=wqs_slot(s))
        B.emit("V", cq, reads=(f"wqs.{dc}",) + SLOTK[s],
               writes=(f"wq.{dc}", "wslot"), tag=f"cwq{dc}")
    for dc in range(DC):
        def ld(eng, dc=dc):
            return eng.dma_start(
                out=ar[:, o_wsl + 8192 + dc * 1024: o_wsl + 8192 + (dc + 1) * 1024],
                in_=wo_d[dc * 128:(dc + 1) * 128, :])
        B.emit("G", ld, dma="g_wa", writes=(f"wo.{dc}", "wslot"), tag=f"ld_wo{dc}")

    PTW = [ar[:, o_xbfc + i * 1024: o_xbfc + (i + 1) * 1024] for i in range(4)]
    PTK = [("xbfc.0", "xbfc.1"), ("xbfc.2", "xbfc.3"),
           ("xbfc.4", "xbfc.5"), ("xbfc.6", "xbfc.7")]
    PARDEN = ctx["SCR01"]
    RDENB = ctx["RDENB"]
    ROW_AB, ROW_BB = ctx["ROW_AB"], ctx["ROW_BB"]
    LNT1A = ctx["TF1RAW"][:, 0:512]
    LNT1B = ctx["TF1RAW"][:, 512:1024]

    def emit_castq(tc):
        for dc in range(DC):
            def cst(eng, dc=dc, tc=tc):
                return eng.tensor_copy(
                    out=xbfc_t(dc),
                    in_=xt[:, dc * T + tc * 512: dc * T + (tc + 1) * 512])
            B.emit("V", cst, reads=(f"xt.{dc}",), writes=(f"xbfc.{dc}",),
                   tag=f"cxq.{tc}.{dc}")
        for f in range(8):
            bk = f % 2
            for dc in range(DC):
                def mm(eng, f=f, dc=dc, bk=bk):
                    return eng.matmul(out=bank(bk), lhsT=wq_t(dc, f), rhs=xbfc_t(dc),
                                      start=(dc == 0), stop=(dc == DC - 1))
                B.emit("T", mm, reads=(f"wq.{dc}", f"xbfc.{dc}", "wslot"),
                       writes=(f"ps{bk}",), tag=f"qmm.{tc}.{f}.{dc}")

            def ev(eng, f=f, bk=bk):
                return eng.tensor_scalar(out=qtc_t(f), in0=bank(bk), scalar1=0.125,
                                         scalar2=None, op0=ALU.mult)
            B.emit("V", ev, reads=(f"ps{bk}",), writes=(f"qtc.{f}",), tag=f"qev.{tc}.{f}")

    def emit_headpair(tc, hp):
        ntj = 4 * tc + 4
        npair = ntj // 2
        HP = [(2 * hp, 0, 4), (2 * hp + 1, 2, 5)]

        def stage2(p):
            zss = [[], []]
            for u in range(2):
                tj = 2 * p + u
                diag = (tj // 4 == tc)
                r = tj - 4 * tc if diag else 0
                for side in range(2):
                    h, sb, nb = HP[side]
                    fq = h // 2
                    pr = (h % 2) * 64
                    sbk = sb + u

                    def smm(eng, fq=fq, pr=pr, tj=tj, sbk=sbk, stp=(not diag)):
                        return eng.matmul(
                            out=bank(sbk),
                            lhsT=kT_t(fq)[pr:pr + 64, tj * 128:(tj + 1) * 128],
                            rhs=qtc_t(fq)[pr:pr + 64, 0:512],
                            start=True, stop=stp)
                    B.emit("T", smm, reads=(f"kT.{fq}", f"qtc.{fq}"),
                           writes=(f"ps{sbk}",), tag=f"sT.{tc}.{h}.{tj}")
                if diag:
                    for side in range(2):
                        h, sb, nb = HP[side]
                        sbk = sb + u

                        def msk(eng, r=r, sbk=sbk):
                            return eng.matmul(out=bank(sbk, r * 128, r * 128 + 128),
                                              lhsT=C["IDENT"], rhs=C["MTC"],
                                              start=False, stop=True)
                        B.emit("T", msk, reads=CR, writes=(f"ps{sbk}",),
                               tag=f"smk.{tc}.{2 * hp + side}.{tj}")
                    if r > 0:
                        zss[0].append((u, r))
                        zss[1].append((u, r))
            for side in range(2):
                h, sb, nb = HP[side]
                bkeys = (f"ps{sb}", f"ps{sb + 1}")
                pt = PTW[side * 2 + (p % 2)]
                ptk = PTK[side * 2 + (p % 2)]

                def ex(eng, sb=sb, pt=pt):
                    return eng.activation(out=pt[:, 0:1024],
                                          in_=bank(sb, 0, 1024), func=AF.Exp)
                B.emit("S", ex, reads=bkeys, writes=ptk,
                       tag=f"exp.{tc}.{hp}.{side}.{p}")
                for (u, r) in zss[side]:
                    B.emit("V", lambda eng, pt=pt, u=u, r=r: eng.memset(
                        pt[:, u * 512:u * 512 + r * 128], 0.0),
                        writes=ptk, tag=f"ptz.{tc}.{hp}.{side}.{p}.{u}")

        def pvst(p, side):
            h, sb, nb = HP[side]
            pt = PTW[side * 2 + (p % 2)]
            ptk = PTK[side * 2 + (p % 2)]
            for u in range(2):
                tj = 2 * p + u

                def pv(eng, tj=tj, h=h, pt=pt, u=u, nb=nb, first=(tj == 0),
                       last=(tj == ntj - 1)):
                    return eng.matmul(out=bank(nb, 0, 512, 0, 65),
                                      lhsT=v_t(tj, h),
                                      rhs=pt[:, u * 512:(u + 1) * 512],
                                      start=first, stop=last)
                B.emit("T", pv, reads=(f"v.{tj}",) + ptk,
                       writes=(f"ps{nb}",), tag=f"pv.{tc}.{h}.{tj}")

        stage2(0)
        for p in range(1, npair):
            stage2(p)
            pvst(p - 1, 0)
            pvst(p - 1, 1)
        pvst(npair - 1, 0)
        pvst(npair - 1, 1)
        RDK = ("lnS0", "lnS1")
        i_ln = B.emit("S", lambda eng: eng.activation(
            out=PARDEN, in_=bank(4, 0, 1024, 64, 65), func=AF.Ln, bias=EPS_A),
            reads=("ps4", "ps5"), writes=RDK, tag=f"lnden.{tc}.{hp}")
        B.emit("S", lambda eng: eng.activation(
            out=RDENB, in_=PARDEN, func=AF.Exp, scale=-1.0),
            reads=RDK, writes=RDK + ("lnS2",), tag=f"rden.{tc}.{hp}")
        for side in range(2):
            h, sb, nb = HP[side]
            rb = sb
            B.emit("V", lambda eng, nb=nb: eng.tensor_copy(out=TF2[0:64, 0:512],
                   in_=bank(nb, 0, 512, 0, 64)),
                   reads=(f"ps{nb}",), writes=("tf2",), tag=f"nev.{tc}.{h}",
                   extra_deps=(i_ln,))
            B.emit("T", lambda eng, rb=rb, side=side: eng.matmul(
                out=bank(rb, 0, 512, 0, 64), lhsT=C["ONES1B"][:, 0:64],
                rhs=RDENB[:, side * 512:(side + 1) * 512],
                start=True, stop=True),
                reads=RDK + ("lnS2",) + CR, writes=(f"ps{rb}",), tag=f"rdb.{tc}.{h}")
            ic = h // 2
            prh = (h % 2) * 64

            def oev(eng, ic=ic, prh=prh, rb=rb):
                return eng.tensor_tensor(out=otc_t(ic)[prh:prh + 64, :],
                                         in0=TF2[0:64, 0:512],
                                         in1=bank(rb, 0, 512, 0, 64), op=ALU.mult)
            B.emit("V", oev, reads=("tf2", f"ps{rb}"), writes=(f"otc.{ic}",),
                   tag=f"oev.{tc}.{h}")

    def emit_wo_stats(tc):
        for dc in range(DC):
            bk = 4 + dc % 2
            for ic in range(8):
                def mm(eng, dc=dc, ic=ic, bk=bk):
                    return eng.matmul(out=bank(bk), lhsT=wo_t(ic, dc), rhs=otc_t(ic),
                                      start=(ic == 0), stop=(ic == 7))
                B.emit("T", mm, reads=(f"wo.{dc}", f"otc.{ic}", "wslot"),
                       writes=(f"ps{bk}",), tag=f"wom.{tc}.{dc}.{ic}")

            def res(eng, dc=dc, tc=tc, bk=bk):
                sl = xt[:, dc * T + tc * 512: dc * T + (tc + 1) * 512]
                return eng.tensor_tensor(out=sl, in0=sl, in1=bank(bk), op=ALU.add)
            B.emit("V", res, reads=(f"ps{bk}", f"xt.{dc}"), writes=(f"xt.{dc}",),
                   tag=f"res.{tc}.{dc}")
            if dc % 2 == 0:
                XBp, xbk = ctx["XB"], "xb"
                SQp, sqk = ctx["SQ"], "sq"
            else:
                XBp, xbk = LNT1A, "tf1"
                SQp, sqk = LNT1B, "tf1b"

            def cst(eng, dc=dc, tc=tc, XBp=XBp):
                return eng.tensor_copy(out=XBp,
                                       in_=xt[:, dc * T + tc * 512: dc * T + (tc + 1) * 512])
            B.emit("V", cst, reads=(f"xt.{dc}",), writes=(xbk,), tag=f"palc.{tc}.{dc}")
            B.emit("S", lambda eng, XBp=XBp, SQp=SQp: eng.activation(
                out=SQp, in_=XBp, func=AF.Square),
                reads=(xbk,), writes=(sqk,), tag=f"palq.{tc}.{dc}")

            def s1(eng, dc=dc, XBp=XBp):
                return eng.matmul(out=bank(6, 0, 512, 0, 1), lhsT=C["ONESC"], rhs=XBp,
                                  start=(dc == 0), stop=(dc == DC - 1))
            B.emit("T", s1, reads=(xbk,) + CR, writes=("ps6",), tag=f"pal1.{tc}.{dc}")

            def s2(eng, dc=dc, SQp=SQp):
                return eng.matmul(out=bank(7, 0, 512, 0, 1), lhsT=C["ONESC"], rhs=SQp,
                                  start=(dc == 0), stop=(dc == DC - 1))
            B.emit("T", s2, reads=(sqk,) + CR, writes=("ps7",), tag=f"pal2.{tc}.{dc}")

    def emit_ln_tail(tc):
        tg = f"pa{tc}"
        c0 = tc * 512
        n = 512
        B.emit("V", lambda eng: eng.tensor_scalar(out=S0, in0=bank(6, 0, n, 0, 1),
               scalar1=1.0 / D, scalar2=None, op0=ALU.mult),
               reads=("ps6",), writes=("lnS0",), tag=f"{tg}mu")
        B.emit("V", lambda eng: eng.tensor_scalar(out=S1, in0=bank(7, 0, n, 0, 1),
               scalar1=1.0 / D, scalar2=None, op0=ALU.mult),
               reads=("ps7",), writes=("lnS1",), tag=f"{tg}ms")
        B.emit("V", lambda eng: eng.tensor_tensor(out=S2, in0=S0, in1=S0, op=ALU.mult),
               reads=("lnS0",), writes=("lnS2",), tag=f"{tg}v1")
        B.emit("V", lambda eng: eng.tensor_tensor(out=S2, in0=S1, in1=S2,
               op=ALU.subtract),
               reads=("lnS1", "lnS2"), writes=("lnS2",), tag=f"{tg}v2")
        B.emit("S", lambda eng: eng.activation(out=S2, in_=S2, func=AF.Ln, bias=LN_EPS),
               reads=("lnS2",), writes=("lnS2",), tag=f"{tg}ln")
        B.emit("S", lambda eng: eng.activation(out=S1, in_=S2, func=AF.Exp,
               scale=-0.5),
               reads=("lnS2",), writes=("lnS1",), tag=f"{tg}ex")
        B.emit("V", lambda eng: eng.scalar_tensor_tensor(out=S0, in0=S0, scalar=-1.0,
               in1=S1, op0=ALU.mult, op1=ALU.mult),
               reads=("lnS0", "lnS1"), writes=("lnS0",), tag=f"{tg}B")
        B.emit("T", lambda eng: eng.matmul(out=bank(6), lhsT=C["ONES1F"], rhs=S1,
               start=True, stop=True), reads=("lnS1",) + CR, writes=("ps6",),
               tag=f"{tg}bA")
        B.emit("T", lambda eng: eng.matmul(out=bank(7), lhsT=C["ONES1F"], rhs=S0,
               start=True, stop=True), reads=("lnS0",) + CR, writes=("ps7",),
               tag=f"{tg}bB")
        for dc in range(DC):
            def a1(eng, dc=dc):
                return eng.tensor_tensor(out=TF0[:, 0:n],
                                         in0=xt[:, dc * T + c0: dc * T + c0 + n],
                                         in1=bank(6), op=ALU.mult)
            B.emit("V", a1, reads=(f"xt.{dc}", "ps6"), writes=("tf0",),
                   tag=f"{tg}a1.{dc}")

            def a2(eng, dc=dc):
                return eng.tensor_tensor(out=xt[:, dc * T + c0: dc * T + c0 + n],
                                         in0=TF0[:, 0:n], in1=bank(7), op=ALU.add)
            B.emit("V", a2, reads=("tf0", "ps7"), writes=(f"xt.{dc}",),
                   tag=f"{tg}a2.{dc}")

    emit_castq(0)
    for tc in range(NCH):
        for hp in range(HA // 2):
            emit_headpair(tc, hp)
            if hp == 3 and tc >= 1:
                emit_ln_tail(tc - 1)
        emit_wo_stats(tc)
        if tc + 1 < NCH:
            emit_castq(tc + 1)
    emit_ln_tail(NCH - 1)


def ffn_layer(ctx, l, fqkv_d, fwo_d, fw1_d, fw2_d, rot_d, base, fin=None):
    B = ctx["B"]
    ar, xt, bank, ps = ctx["ar"], ctx["xt"], ctx["bank"], ctx["ps"]
    C, CR, FBR = ctx["C"], ctx["CR"], ctx["FBR"]
    S0, S1, S2 = ctx["S0"], ctx["S1"], ctx["S2"]
    S0b, S1b = ctx["S0b"], ctx["S1b"]
    PT = ctx["PT"]
    XB, SQ = ctx["XB"], ctx["SQ"]
    TF0, TF1, TF2 = ctx["TF0"], ctx["TF1"], ctx["TF2"]
    NEGM, NEGMR, NEGMR2, BCOL = ctx["NEGM"], ctx["NEGMR"], ctx["NEGMR2"], ctx["BCOL"]
    ROW_AB, ROW_BB = ctx["ROW_AB"], ctx["ROW_BB"]
    QS2, KS2 = ctx["QS2"], ctx["KS2"]
    LNG, LNB, B2C, B1A, B1G = ctx["LNG"], ctx["LNB"], ctx["B2C"], ctx["B1A"], ctx["B1G"]
    L = f"L{l}"

    o_qT = base
    o_kT = o_qT + 8192
    o_v = o_kT + 8192
    o_hT = o_v + 8320
    o_wqkv = o_hT + 8192
    o_rot = o_wqkv + 12288
    end_f1 = o_rot + 8192
    assert end_f1 <= ARN, end_f1
    o_oT = o_hT
    o_wo = o_wqkv
    o_w1 = o_wqkv + 4096
    o_f1s = end_f1
    F1SQ = [ar[:, o_f1s + i * 512: o_f1s + (i + 1) * 512] for i in range(2)]
    F1QNB = [ar[:, o_f1s + 1024 + i * 512: o_f1s + 1024 + (i + 1) * 512] for i in range(2)]
    F1SCR = ar[:, o_f1s + 2048: o_f1s + 2048 + 2048].bitcast(F32)
    F1TA = [ar[:, o_f1s + 4096 + i * 1024: o_f1s + 4096 + (i + 1) * 1024].bitcast(F32)
            for i in range(2)]
    F1TB = [ar[:, o_f1s + 6144 + i * 1024: o_f1s + 6144 + (i + 1) * 1024].bitcast(F32)
            for i in range(2)]
    assert o_f1s + 8192 <= ARN, o_f1s + 8192
    o_w2 = o_qT
    o_h2n = o_hT
    o_gg = o_qT + 22528
    u_slots = ([o_qT + 23040] +
               [o_wo + i * 1024 for i in range(4)] +
               [o_wqkv + 8192 + i * 1024 for i in range(2, 4)] +
               [o_rot + i * 1024 for i in range(8)] +
               [end_f1 + i * 1024 for i in range(5)] +
               [o_f1s + 6144, o_f1s + 7168])
    u_slots = [s for s in u_slots if s + 1024 <= ARN][:NHC]
    assert len(u_slots) == NHC, len(u_slots)

    def _slot_alias(s):
        if o_qT <= s < o_qT + 24704:
            return "r_v" if s >= o_v else ("r_kT" if s >= o_kT else "r_qT")
        if o_wo <= s < o_wo + 4096:
            return "r_wqkv"
        if o_wqkv <= s < o_wqkv + 12288:
            return "r_wqspare"
        if o_rot <= s < o_rot + 8192:
            return "r_rot"
        return "r_tail"
    u_alias = [_slot_alias(s) for s in u_slots]

    qT_t = lambda f: ar[:, o_qT + f * 2048: o_qT + (f + 1) * 2048]
    kTf_t = lambda f: ar[:, o_kT + f * 2048: o_kT + (f + 1) * 2048]
    vf_t = lambda tt, h: ar[:, o_v + tt * 520 + h * 65: o_v + tt * 520 + h * 65 + 65]
    hT_t = lambda dc: ar[:, o_hT + dc * 1024: o_hT + (dc + 1) * 1024]
    wqkv_t = lambda dc, f0, nn: ar[:, o_wqkv + dc * 1536 + f0: o_wqkv + dc * 1536 + f0 + nn]
    ROTV = lambda tab: ar[:, o_rot + tab * 2048: o_rot + (tab + 1) * 2048].bitcast(F32)
    oT_t = lambda f: ar[:, o_oT + f * 2048: o_oT + (f + 1) * 2048]
    wo_tl = lambda ic, dc: ar[:, o_wo + ic * 1024 + dc * 128: o_wo + ic * 1024 + (dc + 1) * 128]
    h2n_t = lambda dc: ar[:, o_h2n + dc * 1024: o_h2n + (dc + 1) * 1024]
    u_t = lambda c: ar[:, u_slots[c]: u_slots[c] + 1024]
    w2_tl = lambda c, dc: ar[:, o_w2 + c * 1024 + dc * 128: o_w2 + c * 1024 + (dc + 1) * 128]
    GG = ar[:, o_gg:o_gg + 512]

    def vones(eng):
        vv = ar[:, o_v:o_v + 8320].rearrange("p (t h c) -> p t h c", t=TT, h=FH)
        return eng.memset(vv[:, :, :, 64:65], 1.0)
    B.emit("V", vones, writes=tuple(f"fv.{tt}" for tt in range(TT)), tag=f"{L}vones")

    for dc in range(DC):
        def ld(eng, dc=dc):
            return eng.dma_start(out=wqkv_t(dc, 0, 1536),
                                 in_=fqkv_d[l, dc * 128:(dc + 1) * 128, :])
        B.emit("G", ld, dma="g_fqkv", writes=(f"fqkv.{dc}", "r_wqkv", "r_w1b0", "r_w1b1", "r_w1b2", "r_wqspare"), tag=f"{L}ldqkv{dc}")

    for hh in range(2):
        hc0 = hh * 1024
        for tab in range(4):
            def ldr(eng, tab=tab, hc0=hc0):
                return eng.dma_start(out=ROTV(tab)[:, 0:1024],
                                     in_=rot_d[tab, :, hc0:hc0 + 1024])
            B.emit("Q", ldr, dma="q_rot", writes=(f"rot.{tab}", "r_rot"),
                   tag=f"{L}ldrot{hh}.{tab}")
        for dc in range(DC):
            if dc < 4:
                def cp(eng, dc=dc, hc0=hc0):
                    return eng.tensor_copy(out=hT_t(dc),
                                           in_=xt[:, dc * T + hc0: dc * T + hc0 + 1024])
                B.emit("V", cp, reads=(f"xt.{dc}",), writes=(f"hT.{dc}", "r_hT"),
                       tag=f"{L}h{hh}.{dc}")
            elif hh == 0:
                def cp(eng, dc=dc):
                    return eng.tensor_copy(out=hT_t(dc)[:, 1:1024],
                                           in_=xt[:, dc * T: dc * T + 1023])
                B.emit("V", cp, reads=(f"xt.{dc}",), writes=(f"hT.{dc}", "r_hT"),
                       tag=f"{L}h{hh}.{dc}")
                B.emit("V", lambda eng, dc=dc: eng.memset(hT_t(dc)[:, 0:1], 0.0),
                       writes=(f"hT.{dc}", "r_hT"), tag=f"{L}h0z.{dc}")
            else:
                def cp(eng, dc=dc, hc0=hc0):
                    return eng.tensor_copy(
                        out=hT_t(dc), in_=xt[:, dc * T + hc0 - 1: dc * T + hc0 + 1023])
                B.emit("V", cp, reads=(f"xt.{dc}",), writes=(f"hT.{dc}", "r_hT"),
                       tag=f"{L}h{hh}.{dc}")
        def f1mm(t, hh=hh):
            qk, f = t // 4, t % 4
            fcol = qk * 512 + f * 128
            b0 = 0 if t % 2 == 0 else 6
            for ch in range(2):
                for dc in range(DC):
                    def mm(eng, dc=dc, fcol=fcol, ch=ch, b0=b0):
                        return eng.matmul(
                            out=bank(b0 + ch), lhsT=wqkv_t(dc, fcol, 128),
                            rhs=hT_t(dc)[:, ch * 512:(ch + 1) * 512],
                            start=(dc == 0), stop=(dc == DC - 1))
                    B.emit("T", mm, reads=(f"fqkv.{dc}", f"hT.{dc}", "r_wqkv", "r_wqspare",
                                           "r_w1b0", "r_w1b1", "r_w1b2", "r_hT"),
                           writes=(f"ps{b0 + ch}",), tag=f"{L}qk{qk}.{hh}.{f}.{ch}.{dc}")

        def f1post(t, hh=hh, hc0=hc0):
            qk, f = t // 4, t % 4
            b0 = 0 if t % 2 == 0 else 6
            CHR = []
            for ch in range(2):
                CHR.append(dict(
                    bk=b0 + ch,
                    qraw=(TF0 if ch == 0 else TF1)[:, 0:512],
                    kqraw="tf0" if ch == 0 else "tf1",
                    sq=F1SQ[ch], ksq=f"f1sq{ch}",
                    ss=(S0b if ch == 0 else F1SCR[0:2, 0:512]),
                    kss="lnS0" if ch == 0 else "f1ss1",
                    rn=(S1b if ch == 0 else F1SCR[0:2, 512:1024]),
                    krn="lnS1" if ch == 0 else "f1rn1",
                    qnb=F1QNB[ch], kqnb=f"f1qnb{ch}",
                    ta=F1TA[ch][:, 0:512], kta=f"f1ta{ch}",
                    tb=F1TB[ch][:, 0:512], ktb=f"f1tb{ch}",
                ))
            scol = QS2(l) if qk == 0 else KS2(l)
            bias = LN8 if qk == 0 else 0.0
            tcos = 0 if qk == 0 else 2
            tsin = 1 if qk == 0 else 3
            steps = []
            for ch in range(2):
                R = CHR[ch]
                bk = R["bk"]
                st_ = []
                st_.append(("V", lambda eng, R=R, bk=bk: eng.tensor_copy(
                    out=R["qraw"], in_=bank(bk)), (f"ps{bk}",), (R["kqraw"],), "qe"))
                st_.append(("S", lambda eng, R=R: eng.activation(
                    out=R["sq"], in_=R["qraw"], func=AF.Square),
                    (R["kqraw"],), (R["ksq"], "r_tail"), "sq"))
                st_.append(("T", lambda eng, R=R, bk=bk: eng.matmul(
                    out=bank(bk, 0, 512, 0, 2), lhsT=C["HSEL2"], rhs=R["sq"],
                    start=True, stop=True),
                    (R["ksq"],) + CR, (f"ps{bk}",), "ssm"))
                st_.append(("V", lambda eng, R=R, bk=bk: eng.tensor_scalar(
                    out=R["ss"], in0=bank(bk, 0, 512, 0, 2), scalar1=1e-24,
                    scalar2=None, op0=ALU.max),
                    (f"ps{bk}",), (R["kss"],), "cl"))
                st_.append(("S", lambda eng, R=R: eng.activation(
                    out=R["ss"], in_=R["ss"], func=AF.Ln),
                    (R["kss"],), (R["kss"],), "ln"))
                st_.append(("S", lambda eng, R=R, bias=bias: eng.activation(
                    out=R["rn"], in_=R["ss"], func=AF.Exp, scale=-0.5, bias=bias),
                    (R["kss"],), (R["krn"],), "ex"))
                st_.append(("T", lambda eng, R=R, bk=bk: eng.matmul(
                    out=bank(bk), lhsT=C["BC2F"], rhs=R["rn"], start=True, stop=True),
                    (R["krn"],) + CR, (f"ps{bk}",), "rb"))
                st_.append(("V", lambda eng, R=R, bk=bk, scol=scol: eng.scalar_tensor_tensor(
                    out=R["ta"], in0=R["qraw"], scalar=scol, in1=bank(bk),
                    op0=ALU.mult, op1=ALU.mult),
                    (R["kqraw"], f"ps{bk}") + FBR, (R["kta"], "r_tail"), "qn"))
                st_.append(("V", lambda eng, R=R: eng.tensor_copy(
                    out=R["qnb"], in_=R["ta"]),
                    (R["kta"],), (R["kqnb"], "r_tail"), "qb"))
                st_.append(("T", lambda eng, R=R, bk=bk: eng.matmul(
                    out=bank(bk), lhsT=C["PERM"], rhs=R["qnb"], start=True, stop=True),
                    (R["kqnb"],) + CR, (f"ps{bk}",), "pm"))
                st_.append(("G", lambda eng, R=R, tcos=tcos, ch=ch: eng.tensor_tensor(
                    out=R["ta"], in0=R["ta"],
                    in1=ROTV(tcos)[:, ch * 512:(ch + 1) * 512], op=ALU.mult),
                    (R["kta"], f"rot.{tcos}", "r_rot"), (R["kta"],), "rc"))
                st_.append(("V", lambda eng, R=R, tsin=tsin, ch=ch, bk=bk: eng.tensor_tensor(
                    out=R["tb"], in0=bank(bk),
                    in1=ROTV(tsin)[:, ch * 512:(ch + 1) * 512], op=ALU.mult),
                    (f"ps{bk}", f"rot.{tsin}", "r_rot"), (R["ktb"], "r_tail"), "rs"))
                dst = (qT_t(f) if qk == 0 else kTf_t(f))[:, hc0 + ch * 512: hc0 + (ch + 1) * 512]
                dk = f"fqT.{f}" if qk == 0 else f"fkT.{f}"
                st_.append(("G", lambda eng, R=R, dst=dst: eng.tensor_tensor(
                    out=dst, in0=R["ta"], in1=R["tb"], op=ALU.add),
                    (R["kta"], R["ktb"]), (dk, "r_qT" if qk == 0 else "r_kT"), "ra"))
                steps.append(st_)
            for i in range(len(steps[0])):
                for ch in range(2):
                    eng, fn, rd, wr, nm = steps[ch][i]
                    B.emit(eng, fn, reads=rd, writes=wr, tag=f"{L}{nm}{qk}.{hh}.{f}.{ch}")

        f1mm(0)
        for t in range(1, 8):
            f1mm(t)
            f1post(t - 1)
        f1post(7)
        for ttl in range(8):
            tt = hh * 8 + ttl
            for dc in range(DC):
                def mm(eng, dc=dc, ttl=ttl):
                    return eng.matmul(
                        out=bank(5), lhsT=hT_t(dc)[:, ttl * 128:(ttl + 1) * 128],
                        rhs=wqkv_t(dc, 1024, 512), start=(dc == 0), stop=(dc == DC - 1))
                B.emit("T", mm, reads=(f"fqkv.{dc}", f"hT.{dc}", "r_wqkv", "r_wqspare",
                                        "r_w1b0", "r_w1b1", "r_hT"),
                       writes=("ps5",), tag=f"{L}v.{tt}.{dc}")

            def vev(eng, tt=tt):
                b0 = o_v + tt * 520
                dst = ar[:, b0: b0 + 520].rearrange("p (h c) -> p h c", h=FH)
                return eng.tensor_copy(out=dst[:, :, 0:64], in_=bank(5))
            B.emit("V", vev, reads=("ps5",), writes=(f"fv.{tt}", "r_v"), tag=f"{L}ve.{tt}")

    for ic in range(4):
        def ld(eng, ic=ic):
            return eng.dma_start(out=ar[:, o_wo + ic * 1024: o_wo + (ic + 1) * 1024],
                                 in_=fwo_d[l, ic * 128:(ic + 1) * 128, :])
        B.emit("G", ld, dma="g_fwo", writes=(f"fwo.{ic}", "r_wqkv"), tag=f"{L}ldwo{ic}")

    negmA = NEGM[:, 0:4]
    negmB = NEGM[:, 4:8]
    negmrA = NEGMR
    negmrB = NEGMR2
    PTF = [ar[:, o_f1s + i * 1024: o_f1s + (i + 1) * 1024] for i in range(2)]
    PTFK = ["f2pt0", "f2pt1"]
    RD2 = ctx["SCR01"]
    RD2B = ctx["RDENB"]
    HW = [(hp, w) for w in range(4) for hp in range(FH // 2)]

    def emitS1p(pi):
        hp, w = HW[pi]
        fq = hp
        SIDE = [(2 * hp, 0, 0), (2 * hp + 1, 64, 6)]

        def s1stage(r, side):
            h, pr, bp = SIDE[side]
            ti = 4 * w + r
            tj0 = max(0, ti - 4)
            njt = ti - tj0 + 1
            nj = njt * 128
            c0 = bp * 512
            segs = [(0, min(nj, 512))] + ([(512, nj)] if nj > 512 else [])
            bkeys = (f"ps{bp}",) if nj <= 512 else (f"ps{bp}", f"ps{bp + 1}")
            dpos = (ti - tj0) * 128
            for (s0, s1) in segs:
                def mm(eng, fq=fq, pr=pr, ti=ti, tj0=tj0, s0=s0, s1=s1, c0=c0,
                       stp=(s1 != dpos + 128 and not (ti >= 4 and s0 == 0))):
                    return eng.matmul(
                        out=ps[:, c0 + s0:c0 + s1],
                        lhsT=qT_t(fq)[pr:pr + 64, ti * 128:(ti + 1) * 128],
                        rhs=kTf_t(fq)[pr:pr + 64, tj0 * 128 + s0: tj0 * 128 + s1],
                        start=True, stop=stp)
                B.emit("T", mm, reads=(f"fqT.{fq}", f"fkT.{fq}", "r_qT", "r_kT"),
                       writes=bkeys, tag=f"{L}s1.{h}.{w}.{r}")
            def mskd(eng, dpos=dpos, c0=c0):
                return eng.matmul(out=ps[:, c0 + dpos:c0 + dpos + 128],
                                  lhsT=C["IDENT"], rhs=C["MTF"], start=False,
                                  stop=True)
            B.emit("T", mskd, reads=CR, writes=bkeys, tag=f"{L}s1d.{h}.{w}.{r}")
            if ti >= 4:
                def mskf(eng, c0=c0):
                    return eng.matmul(out=ps[:, c0:c0 + 128], lhsT=C["IDENT"],
                                      rhs=C["MTC"], start=False, stop=True)
                B.emit("T", mskf, reads=CR, writes=bkeys, tag=f"{L}s1f.{h}.{w}.{r}")

        def s1red(r, side):
            h, pr, bp = SIDE[side]
            ti = 4 * w + r
            nj = (ti - max(0, ti - 4) + 1) * 128
            bkeys = (f"ps{bp}",) if nj <= 512 else (f"ps{bp}", f"ps{bp + 1}")
            ng = negmA if side == 0 else negmB
            ngk = "negmA" if side == 0 else "negmB"

            def red(eng, nj=nj, r=r, bp=bp, ng=ng):
                return eng.tensor_reduce(out=ng[:, r:r + 1],
                                         in_=ps[:, bp * 512:bp * 512 + nj],
                                         axis=AX.X, op=ALU.max, negate=True)
            B.emit("V", red, reads=bkeys, writes=(ngk,), tag=f"{L}s1r.{h}.{w}.{r}")

        s1stage(0, 0)
        s1stage(0, 1)
        for r in range(1, 4):
            s1stage(r, 0)
            s1stage(r, 1)
            s1red(r - 1, 0)
            s1red(r - 1, 1)
        s1red(3, 0)
        s1red(3, 1)

    def emitS2p(pi):
        hp, w = HW[pi]
        fq = hp
        icols0 = w * 512
        SIDE = [(2 * hp, 0), (2 * hp + 1, 64)]
        B.emit("T", lambda eng: eng.transpose(out=bank(2, 0, 128, 0, 4), in_=negmA,
               identity=C["IDF"]),
               reads=("negmA",) + CR, writes=("ps2",), tag=f"{L}ntrA.{hp}.{w}")
        B.emit("T", lambda eng: eng.transpose(out=bank(2, 128, 256, 0, 4), in_=negmB,
               identity=C["IDF"]),
               reads=("negmB",) + CR, writes=("ps2",), tag=f"{L}ntrB.{hp}.{w}")
        B.emit("V", lambda eng: eng.tensor_copy(out=negmrA, in_=bank(2, 0, 128, 0, 4)),
               reads=("ps2",), writes=("negmrA",), tag=f"{L}nevA.{hp}.{w}")
        B.emit("V", lambda eng: eng.tensor_copy(out=negmrB, in_=bank(2, 128, 256, 0, 4)),
               reads=("ps2",), writes=("negmrB",), tag=f"{L}nevB.{hp}.{w}")
        tj0w = max(0, 4 * w - 4)
        tj1w = 4 * w + 3

        def s2stage(tj):
            zs = []
            tp = tj - 4 * w
            ka = max(0, tp)
            kb = min(3, tp + 4)
            masks = []
            if 0 <= tp <= 3:
                masks.append((tp, "MTC"))
            if 0 <= tp + 4 <= 3:
                masks.append((tp + 4, "MTF"))
            for side in range(2):
                h, pr = SIDE[side]
                sbk = 2 + side

                def mm(eng, tj=tj, fq=fq, pr=pr, icols0=icols0, sbk=sbk):
                    return eng.matmul(
                        out=bank(sbk),
                        lhsT=kTf_t(fq)[pr:pr + 64, tj * 128:(tj + 1) * 128],
                        rhs=qT_t(fq)[pr:pr + 64, icols0: icols0 + 512],
                        start=True, stop=False)
                B.emit("T", mm, reads=(f"fqT.{fq}", f"fkT.{fq}", "r_qT", "r_kT"),
                       writes=(f"ps{sbk}",), tag=f"{L}s2.{h}.{w}.{tj}")
            for side in range(2):
                h, pr = SIDE[side]
                sbk = 2 + side
                ngr = negmrA if side == 0 else negmrB
                ngrk = "negmrA" if side == 0 else "negmrB"
                for r in range(ka, kb + 1):
                    def selm(eng, r=r, sbk=sbk, ngr=ngr, stp=(r == kb)):
                        return eng.matmul(out=bank(sbk, r * 128, (r + 1) * 128),
                                          lhsT=C["SEL4B"][:, r * 128:(r + 1) * 128],
                                          rhs=ngr, start=False, stop=stp)
                    B.emit("T", selm, reads=(ngrk,) + CR, writes=(f"ps{sbk}",),
                           tag=f"{L}s2m.{h}.{w}.{tj}.{r}")
            pt = PTF[tj % 2]
            ptk = PTFK[tj % 2]

            def ex(eng, pt=pt):
                return eng.activation(out=pt[:, 0:1024], in_=bank(2, 0, 1024),
                                      func=AF.Exp)
            B.emit("S", ex, reads=("ps2", "ps3"), writes=(ptk, "r_tail"),
                   tag=f"{L}s2e.{hp}.{w}.{tj}")
            for side in range(2):
                for (r, mk) in masks:
                    tri = "TRIC" if mk == "MTC" else "TRIF"

                    def vmask(eng, pt=pt, side=side, r=r, tri=tri):
                        sl = pt[:, side * 512 + r * 128: side * 512 + (r + 1) * 128]
                        return eng.tensor_tensor(out=sl, in0=sl, in1=C[tri],
                                                 op=ALU.mult)
                    B.emit("V", vmask, reads=(ptk,) + CR, writes=(ptk, "r_tail"),
                           tag=f"{L}s2vm.{hp}.{w}.{tj}.{side}.{r}")
            for side in range(2):
                if ka > 0:
                    B.emit("V", lambda eng, pt=pt, side=side, ka=ka: eng.memset(
                        pt[:, side * 512:side * 512 + ka * 128], 0.0),
                        writes=(ptk, "r_tail"), tag=f"{L}pz0.{hp}.{w}.{tj}.{side}")
                if kb < 3:
                    B.emit("V", lambda eng, pt=pt, side=side, kb=kb: eng.memset(
                        pt[:, side * 512 + (kb + 1) * 128:side * 512 + 512], 0.0),
                        writes=(ptk, "r_tail"), tag=f"{L}pz1.{hp}.{w}.{tj}.{side}")

        def s2pv(tj):
            pt = PTF[tj % 2]
            ptk = PTFK[tj % 2]
            for side in range(2):
                h, pr = SIDE[side]
                nb = 4 + side

                def pv(eng, tj=tj, h=h, pt=pt, side=side, nb=nb,
                       first=(tj == tj0w), last=(tj == tj1w)):
                    return eng.matmul(out=bank(nb, 0, 512, 0, 65),
                                      lhsT=vf_t(tj, h),
                                      rhs=pt[:, side * 512:(side + 1) * 512],
                                      start=first, stop=last)
                B.emit("T", pv, reads=(f"fv.{tj}", ptk, "r_v"), writes=(f"ps{nb}",),
                       tag=f"{L}pv.{h}.{w}.{tj}")

        s2stage(tj0w)
        for k, tj in enumerate(range(tj0w + 1, tj1w + 1)):
            s2stage(tj)
            s2pv(tj - 1)
            if k == 2 and pi + 1 < len(HW):
                emitS1p(pi + 1)
        s2pv(tj1w)
        RDK = ("lnS0", "lnS1")
        i_ln = B.emit("S", lambda eng: eng.activation(
            out=RD2, in_=bank(4, 0, 1024, 64, 65), func=AF.Ln),
            reads=("ps4", "ps5"), writes=RDK, tag=f"{L}ld.{hp}.{w}")
        B.emit("S", lambda eng: eng.activation(
            out=RD2B, in_=RD2, func=AF.Exp, scale=-1.0),
            reads=RDK, writes=RDK + ("lnS2",), tag=f"{L}rd.{hp}.{w}")
        for side in range(2):
            h, pr = SIDE[side]
            nb = 4 + side
            rb = 2 + side
            B.emit("V", lambda eng, nb=nb: eng.tensor_copy(out=TF2[0:64, 0:512],
                   in_=bank(nb, 0, 512, 0, 64)),
                   reads=(f"ps{nb}",), writes=("tf2",), tag=f"{L}ne.{h}.{w}",
                   extra_deps=(i_ln,))
            B.emit("T", lambda eng, rb=rb, side=side: eng.matmul(
                out=bank(rb, 0, 512, 0, 64), lhsT=C["ONES1B"][:, 0:64],
                rhs=RD2B[:, side * 512:(side + 1) * 512], start=True, stop=True),
                reads=RDK + ("lnS2",) + CR, writes=(f"ps{rb}",), tag=f"{L}rb.{h}.{w}")

            def oev(eng, fq=fq, pr=pr, icols0=icols0, rb=rb):
                return eng.tensor_tensor(
                    out=oT_t(fq)[pr:pr + 64, icols0:icols0 + 512],
                    in0=TF2[0:64, 0:512], in1=bank(rb, 0, 512, 0, 64), op=ALU.mult)
            B.emit("V", oev, reads=("tf2", f"ps{rb}"), writes=(f"oT.{fq}", "r_hT"),
                   tag=f"{L}oe.{h}.{w}")

    emitS1p(0)
    for pi in range(len(HW)):
        emitS2p(pi)

    def w1load(hh, c):
        buf = c % 3
        hcs = HCS[c]

        def lda(eng, c=c, buf=buf):
            dst = ar[:, o_w1 + buf * 2048: o_w1 + buf * 2048 + 1024]
            return eng.dma_start(out=dst, in_=fw1_d[l, c])
        B.emit("G", lda, dma=f"g_w1p{buf}", writes=(f"w1a.{buf}", f"r_w1b{buf}"),
               tag=f"{L}lw1a.{hh}.{c}")

        def ldg(eng, c=c, buf=buf):
            dst = ar[:, o_w1 + buf * 2048 + 1024: o_w1 + buf * 2048 + 2048]
            return eng.dma_start(out=dst, in_=fw1_d[l, NHC + c])
        B.emit("G", ldg, dma=f"g_w1p{buf}", writes=(f"w1g.{buf}", f"r_w1b{buf}"),
               tag=f"{L}lw1g.{hh}.{c}")

    def w2load(c):
        def ld(eng, c=c):
            return eng.dma_start(
                out=ar[0:HCS[c], o_w2 + c * 1024: o_w2 + (c + 1) * 1024],
                in_=fw2_d[l, c * 128: c * 128 + HCS[c], :])
        B.emit("G", ld, dma="g_w2",
               writes=(f"fw2.{c}", "r_qT", "r_kT", "r_v"), tag=f"{L}ldw2.{c}")

    for c in range(3):
        w1load(0, c)
    for c in range(11):
        w2load(c)

    for tc in range(NCH):
        for dc in range(DC):
            bk = 6 + dc % 2
            for ic in range(4):
                def mm(eng, tc=tc, dc=dc, ic=ic, bk=bk):
                    return eng.matmul(out=bank(bk), lhsT=wo_tl(ic, dc),
                                      rhs=oT_t(ic)[:, tc * 512:(tc + 1) * 512],
                                      start=(ic == 0), stop=(ic == 3))
                B.emit("T", mm, reads=(f"fwo.{ic}", f"oT.{ic}", "r_wqkv", "r_hT"),
                       writes=(f"ps{bk}",), tag=f"{L}wo.{tc}.{dc}.{ic}")

            def res(eng, tc=tc, dc=dc, bk=bk):
                sl = xt[:, dc * T + tc * 512: dc * T + (tc + 1) * 512]
                return eng.tensor_tensor(out=sl, in0=sl, in1=bank(bk), op=ALU.add)
            B.emit("V", res, reads=(f"ps{bk}", f"xt.{dc}"), writes=(f"xt.{dc}",),
                   tag=f"{L}wr.{tc}.{dc}")


    for dc in range(DC):
        def bc(eng, dc=dc):
            return eng.tensor_copy(out=BCOL[:, dc:dc + 1],
                                   in_=xt[:, dc * T + 1023: dc * T + 1024])
        B.emit("V", bc, reads=(f"xt.{dc}",), writes=("bcol",), tag=f"{L}bc.{dc}")

    for hh in range(2):
        hc0 = hh * 1024
        for ch in range(2):
            c0 = hc0 + ch * 512
            for dc in range(DC):
                shifted = dc >= 4
                if not shifted:
                    def cst(eng, dc=dc, c0=c0):
                        return eng.tensor_copy(out=XB,
                                               in_=xt[:, dc * T + c0: dc * T + c0 + 512])
                    B.emit("V", cst, reads=(f"xt.{dc}",), writes=("xb",),
                           tag=f"{L}3c.{hh}.{ch}.{dc}")
                elif c0 == 0:
                    def cst(eng, dc=dc):
                        return eng.tensor_copy(out=XB[:, 1:512],
                                               in_=xt[:, dc * T: dc * T + 511])
                    B.emit("V", cst, reads=(f"xt.{dc}",), writes=("xb",),
                           tag=f"{L}3c.{hh}.{ch}.{dc}")
                    B.emit("V", lambda eng: eng.memset(XB[:, 0:1], 0.0),
                           writes=("xb",), tag=f"{L}3cz.{hh}.{ch}.{dc}")
                elif hh == 1 and ch == 0:
                    def cst(eng, dc=dc, c0=c0):
                        return eng.tensor_copy(out=XB[:, 1:512],
                                               in_=xt[:, dc * T + c0: dc * T + c0 + 511])
                    B.emit("V", cst, reads=(f"xt.{dc}",), writes=("xb",),
                           tag=f"{L}3c.{hh}.{ch}.{dc}")
                    B.emit("V", lambda eng, dc=dc: eng.tensor_copy(
                        out=XB[:, 0:1].bitcast(F32) if False else XB[:, 0:1],
                        in_=BCOL[:, dc:dc + 1]),
                        reads=("bcol",), writes=("xb",), tag=f"{L}3cb.{hh}.{ch}.{dc}")
                else:
                    def cst(eng, dc=dc, c0=c0):
                        return eng.tensor_copy(
                            out=XB, in_=xt[:, dc * T + c0 - 1: dc * T + c0 + 511])
                    B.emit("V", cst, reads=(f"xt.{dc}",), writes=("xb",),
                           tag=f"{L}3c.{hh}.{ch}.{dc}")
                B.emit("S", lambda eng: eng.activation(out=SQ, in_=XB, func=AF.Square),
                       reads=("xb",), writes=("sq",), tag=f"{L}3q.{hh}.{ch}.{dc}")

                def s1(eng, dc=dc):
                    return eng.matmul(out=bank(6, 0, 512, 0, 1), lhsT=C["ONESC"],
                                      rhs=XB, start=(dc == 0), stop=(dc == DC - 1))
                B.emit("T", s1, reads=("xb",) + CR, writes=("ps6",),
                       tag=f"{L}31.{hh}.{ch}.{dc}")

                def s2(eng, dc=dc):
                    return eng.matmul(out=bank(7, 0, 512, 0, 1), lhsT=C["ONESC"],
                                      rhs=SQ, start=(dc == 0), stop=(dc == DC - 1))
                B.emit("T", s2, reads=("sq",) + CR, writes=("ps7",),
                       tag=f"{L}32.{hh}.{ch}.{dc}")
            B.emit("V", lambda eng: eng.tensor_scalar(out=S0,
                   in0=bank(6, 0, 512, 0, 1), scalar1=1.0 / D, scalar2=None,
                   op0=ALU.mult), reads=("ps6",), writes=("lnS0",),
                   tag=f"{L}3mu.{hh}.{ch}")
            B.emit("V", lambda eng: eng.tensor_scalar(out=S1,
                   in0=bank(7, 0, 512, 0, 1), scalar1=1.0 / D, scalar2=None,
                   op0=ALU.mult), reads=("ps7",), writes=("lnS1",),
                   tag=f"{L}3ms.{hh}.{ch}")
            B.emit("V", lambda eng: eng.tensor_tensor(
                out=S2, in0=S0, in1=S0, op=ALU.mult),
                reads=("lnS0",), writes=("lnS2",), tag=f"{L}3v1.{hh}.{ch}")
            B.emit("V", lambda eng: eng.tensor_tensor(
                out=S2, in0=S1, in1=S2, op=ALU.subtract),
                reads=("lnS1", "lnS2"), writes=("lnS2",), tag=f"{L}3v2.{hh}.{ch}")
            B.emit("S", lambda eng: eng.activation(out=S2, in_=S2,
                   func=AF.Ln, bias=LN_EPS), reads=("lnS2",), writes=("lnS2",),
                   tag=f"{L}3l.{hh}.{ch}")
            B.emit("S", lambda eng: eng.activation(
                out=S1, in_=S2, func=AF.Exp, scale=-0.5),
                reads=("lnS2",), writes=("lnS1",), tag=f"{L}3e.{hh}.{ch}")
            B.emit("V", lambda eng: eng.scalar_tensor_tensor(
                out=S0, in0=S0, scalar=-1.0, in1=S1, op0=ALU.mult, op1=ALU.mult),
                reads=("lnS0", "lnS1"), writes=("lnS0",), tag=f"{L}3B.{hh}.{ch}")
            B.emit("T", lambda eng: eng.matmul(out=bank(6), lhsT=C["ONES1F"],
                   rhs=S1, start=True, stop=True), reads=("lnS1",) + CR,
                   writes=("ps6",), tag=f"{L}3bA.{hh}.{ch}")
            B.emit("T", lambda eng: eng.matmul(out=bank(7), lhsT=C["ONES1F"],
                   rhs=S0, start=True, stop=True), reads=("lnS0",) + CR,
                   writes=("ps7",), tag=f"{L}3bB.{hh}.{ch}")
            for dc in range(DC):
                shifted = dc >= 4
                if not shifted:
                    def a1(eng, dc=dc, c0=c0):
                        return eng.tensor_tensor(
                            out=TF0[:, 0:512], in0=xt[:, dc * T + c0: dc * T + c0 + 512],
                            in1=bank(6), op=ALU.mult)
                    B.emit("V", a1, reads=(f"xt.{dc}", "ps6"), writes=("tf0",),
                           tag=f"{L}3x1.{hh}.{ch}.{dc}")
                elif c0 == 0:
                    def a1(eng, dc=dc):
                        return eng.tensor_tensor(
                            out=TF0[:, 1:512], in0=xt[:, dc * T: dc * T + 511],
                            in1=bank(6, 1, 512), op=ALU.mult)
                    B.emit("V", a1, reads=(f"xt.{dc}", "ps6"), writes=("tf0",),
                           tag=f"{L}3x1.{hh}.{ch}.{dc}")
                    B.emit("V", lambda eng: eng.tensor_tensor(
                        out=TF0[:, 0:1], in0=C["ZCOL"], in1=bank(6, 0, 1), op=ALU.mult),
                        reads=CR + ("ps6",), writes=("tf0",), tag=f"{L}3xz.{hh}.{ch}.{dc}")
                elif hh == 1 and ch == 0:
                    def a1(eng, dc=dc, c0=c0):
                        return eng.tensor_tensor(
                            out=TF0[:, 1:512], in0=xt[:, dc * T + c0: dc * T + c0 + 511],
                            in1=bank(6, 1, 512), op=ALU.mult)
                    B.emit("V", a1, reads=(f"xt.{dc}", "ps6"), writes=("tf0",),
                           tag=f"{L}3x1.{hh}.{ch}.{dc}")
                    B.emit("V", lambda eng, dc=dc: eng.tensor_tensor(
                        out=TF0[:, 0:1], in0=BCOL[:, dc:dc + 1], in1=bank(6, 0, 1),
                        op=ALU.mult),
                        reads=("bcol", "ps6"), writes=("tf0",), tag=f"{L}3xb.{hh}.{ch}.{dc}")
                else:
                    def a1(eng, dc=dc, c0=c0):
                        return eng.tensor_tensor(
                            out=TF0[:, 0:512],
                            in0=xt[:, dc * T + c0 - 1: dc * T + c0 + 511],
                            in1=bank(6), op=ALU.mult)
                    B.emit("V", a1, reads=(f"xt.{dc}", "ps6"), writes=("tf0",),
                           tag=f"{L}3x1.{hh}.{ch}.{dc}")
                B.emit("V", lambda eng: eng.tensor_tensor(out=TF1[:, 0:512],
                       in0=TF0[:, 0:512], in1=bank(7), op=ALU.add),
                       reads=("tf0", "ps7"), writes=("tf1",), tag=f"{L}3x2.{hh}.{ch}.{dc}")

                def a3(eng, dc=dc, ch=ch):
                    return eng.tensor_scalar(
                        out=h2n_t(dc)[:, ch * 512:(ch + 1) * 512], in0=TF1[:, 0:512],
                        scalar1=LNG(l, dc), scalar2=LNB(l, dc), op0=ALU.mult,
                        op1=ALU.add)
                B.emit("V", a3, reads=("tf1",) + FBR, writes=(f"h2n.{dc}", "r_hT"),
                       tag=f"{L}3x3.{hh}.{ch}.{dc}")

        if hh == 1:
            for c in range(3):
                w1load(hh, c)
        for c in range(NHC):
            hcs = HCS[c]
            buf = c % 3

            for ch in range(2):
                for dc in range(DC):
                    def mma(eng, dc=dc, ch=ch, buf=buf, hcs=hcs):
                        w = ar[:, o_w1 + buf * 2048 + dc * 128: o_w1 + buf * 2048 + dc * 128 + 128]
                        return eng.matmul(out=bank(ch, 0, 512, 0, hcs),
                                          lhsT=w[:, 0:hcs],
                                          rhs=h2n_t(dc)[:, ch * 512:(ch + 1) * 512],
                                          start=(dc == 0), stop=(dc == DC - 1))
                    B.emit("T", mma, reads=(f"w1a.{buf}", f"h2n.{dc}", "r_hT", f"r_w1b{buf}"),
                           writes=(f"ps{ch}",), tag=f"{L}wa.{hh}.{c}.{ch}.{dc}")

                    def mmg(eng, dc=dc, ch=ch, buf=buf, hcs=hcs):
                        w = ar[:, o_w1 + buf * 2048 + 1024 + dc * 128: o_w1 + buf * 2048 + 1024 + dc * 128 + 128]
                        return eng.matmul(out=bank(2 + ch, 0, 512, 0, hcs),
                                          lhsT=w[:, 0:hcs],
                                          rhs=h2n_t(dc)[:, ch * 512:(ch + 1) * 512],
                                          start=(dc == 0), stop=(dc == DC - 1))
                    B.emit("T", mmg, reads=(f"w1g.{buf}", f"h2n.{dc}", "r_hT", f"r_w1b{buf}"),
                           writes=(f"ps{2 + ch}",), tag=f"{L}wg.{hh}.{c}.{ch}.{dc}")
                def gel(eng, ch=ch, hcs=hcs, c=c):
                    return eng.activation(out=GG[0:hcs, :],
                                          in_=bank(2 + ch, 0, 512, 0, hcs),
                                          func=AF.Gelu, bias=B1G(l, c)[0:hcs, 0:1])
                B.emit("S", gel, reads=(f"ps{2 + ch}",) + FBR, writes=("gg", "r_v"),
                       tag=f"{L}gg.{hh}.{c}.{ch}")

                def umul(eng, c=c, ch=ch, hcs=hcs):
                    return eng.scalar_tensor_tensor(
                        out=u_t(c)[0:hcs, ch * 512:(ch + 1) * 512],
                        in0=bank(ch, 0, 512, 0, hcs), scalar=B1A(l, c)[0:hcs, 0:1],
                        in1=GG[0:hcs, :], op0=ALU.add, op1=ALU.mult)
                B.emit("V", umul, reads=(f"ps{ch}", "gg", "r_v") + FBR,
                       writes=(f"u.{c}", u_alias[c]),
                       tag=f"{L}um.{hh}.{c}.{ch}")
            if c + 3 < NHC:
                w1load(hh, c + 3)
            if hh == 0 and 11 + c < NHC:
                w2load(11 + c)

        for ch in range(2):
            for dc in range(DC):
                bk = 4 + dc % 2
                for c in range(NHC):
                    def mm(eng, dc=dc, ch=ch, c=c, bk=bk):
                        return eng.matmul(out=bank(bk),
                                          lhsT=w2_tl(c, dc)[0:HCS[c], :],
                                          rhs=u_t(c)[0:HCS[c], ch * 512:(ch + 1) * 512],
                                          start=(c == 0), stop=(c == NHC - 1))
                    B.emit("T", mm, reads=(f"fw2.{c}", f"u.{c}", u_alias[c],
                                            "r_qT", "r_kT", "r_v"),
                           writes=(f"ps{bk}",), tag=f"{L}w2.{hh}.{dc}.{ch}.{c}")
                c0 = hc0 + ch * 512

                def res(eng, dc=dc, c0=c0, bk=bk):
                    sl = xt[:, dc * T + c0: dc * T + c0 + 512]
                    return eng.scalar_tensor_tensor(out=sl, in0=bank(bk),
                                                    scalar=B2C(l, dc), in1=sl,
                                                    op0=ALU.add, op1=ALU.add)
                B.emit("V", res, reads=(f"ps{bk}", f"xt.{dc}") + FBR,
                       writes=(f"xt.{dc}",), tag=f"{L}w2r.{hh}.{dc}.{ch}")
            if fin is not None:
                fin(hh * 2 + ch)



def _host_constants():
    cstf = np.zeros((128, 400), np.float32)
    cstf[:, 0:128] = np.eye(128, dtype=np.float32)
    cstf[0, 128:256] = 1.0
    cstf[0, 257:321] = 1.0
    cstf[1, 321:385] = 1.0
    for i, v in enumerate([EPS_A, LN_EPS, LN8, -1.0, -0.5, 1.0, 0.0]):
        cstf[:, 385 + i] = v
    sel = np.zeros((128, 512), np.float32)
    for c in range(4):
        sel[c, c * 128:(c + 1) * 128] = 1.0
    cstb = np.zeros((128, 1664), np.float32)
    cstb[:, 0:128] = np.eye(128, dtype=np.float32)
    idx = np.arange(128)
    cstb[:, 128:256] = np.where(idx[None, :] >= idx[:, None], 0.0, NEG)
    cstb[:, 256:384] = np.where(idx[None, :] <= idx[:, None], 0.0, NEG)
    P = np.zeros((128, 128), np.float32)
    for m in range(128):
        bse = (m // 64) * 64
        r = m % 64
        if r < 32:
            P[bse + r + 32, m] = -1.0
        else:
            P[bse + r - 32, m] = 1.0
    cstb[:, 384:512] = P
    cstb[0:64, 512] = 1.0
    cstb[64:128, 513] = 1.0
    cstb[:, 514] = 1.0
    cstb[:, 515:643] = NEG
    cstb[0:4, 643:1155] = sel[0:4, :]
    cstb[0, 1280:1408] = 1.0
    cstb[:, 1408:1536] = (idx[None, :] >= idx[:, None])
    cstb[:, 1536:1664] = (idx[None, :] <= idx[:, None])
    return cstf, cstb.astype(ml_dtypes.bfloat16)


def _host_rot():
    n = T
    half = np.arange(0, FDH, 2, dtype=np.float64)
    inv_freq = 1.0 / 10000.0 ** (half / FDH)
    t = np.arange(n, dtype=np.float64)
    freqs = t[:, None] * inv_freq[None, :]
    freqs = np.concatenate([freqs, freqs], axis=-1)
    bse = (half + 0.4 * FDH) / (1.4 * FDH)
    power = (t - n // 2) / float(WIN // 2)
    scale = bse[None, :] ** power[:, None]
    scale = np.concatenate([scale, scale], axis=-1)
    cos, sin = np.cos(freqs), np.sin(freqs)
    tabs = [cos * scale, sin * scale, cos / scale, sin / scale]
    out = np.zeros((4, 128, T), np.float32)
    for i, tb in enumerate(tabs):
        tt = tb.T.astype(np.float32)
        out[i, 0:64] = tt
        out[i, 64:128] = tt
    return out


def _host_fb(ins):
    fb = np.zeros((128, 160), np.float32)
    for l in range(2):
        qs = np.asarray(ins["L0_ffn_q_scale"][l], np.float32)
        ks = np.asarray(ins["L0_ffn_k_scale"][l], np.float32)
        fb[0:64, l * 2] = qs
        fb[64:128, l * 2] = qs
        fb[0:64, l * 2 + 1] = ks
        fb[64:128, l * 2 + 1] = ks
        fb[:, 4 + l * 16: 12 + l * 16] = np.asarray(
            ins["L0_ffn_ln_g"][l], np.float32).reshape(8, 128).T
        fb[:, 12 + l * 16: 20 + l * 16] = np.asarray(
            ins["L0_ffn_ln_b"][l], np.float32).reshape(8, 128).T
        fb[:, 36 + l * 8: 44 + l * 8] = np.asarray(
            ins["L0_ffn_b2"][l], np.float32).reshape(8, 128).T
        b1 = np.asarray(ins["L0_ffn_b1"][l], np.float32)
        b1a = np.zeros(22 * 128, np.float32)
        b1a[0:HID] = b1[0:HID]
        b1g = np.zeros(22 * 128, np.float32)
        b1g[0:HID] = b1[HID:]
        fb[:, 52 + l * 43: 52 + l * 43 + 22] = b1a.reshape(22, 128).T
        fb[:, 52 + l * 43 + 22: 52 + l * 43 + 44] = b1g.reshape(22, 128).T
    return fb


def _host_fw1p(w1):
    out = np.zeros((2, 2 * NHC, 128, 1024), np.float32)
    for l in range(2):
        for half in range(2):
            for c in range(NHC):
                hcs = HCS[c]
                blk = w1[l][:, half * HID + c * 128: half * HID + c * 128 + hcs]
                blk = blk.reshape(8, 128, hcs).transpose(1, 0, 2)
                out[l, half * NHC + c, :, :].reshape(128, 8, 128)[:, :, 0:hcs] = blk
    return np.ascontiguousarray(out)


def host_inputs(ins):
    cstf, cstb = _host_constants()
    common = {
        "wqT": np.ascontiguousarray(np.asarray(ins["L0_attn_Wq"]).T),
        "wkvT": np.ascontiguousarray(np.asarray(ins["L0_attn_Wkv"]).T),
        "woT": np.ascontiguousarray(np.asarray(ins["L0_attn_Wo"]).T),
        "fqkv": np.ascontiguousarray(np.asarray(ins["L0_ffn_wqkv"], np.float32)),
        "fwo": np.ascontiguousarray(np.asarray(ins["L0_ffn_wo"], np.float32)),
        "fw1p": _host_fw1p(np.asarray(ins["L0_ffn_w1"], np.float32)),
        "fw2": np.ascontiguousarray(np.asarray(ins["L0_ffn_w2"], np.float32)),
        "fb": _host_fb(ins), "cstf": cstf, "cstb": cstb, "rot": _host_rot(),
    }
    return common


_NC_CACHE = {}


def kernel(**ins):
    if "main" not in _NC_CACHE:
        _NC_CACHE["main"] = build_program()
    nc = _NC_CACHE["main"]
    x = np.asarray(ins["L0_attn_x"])
    common = host_inputs(ins)
    in_maps = []
    for c in range(8):
        m = dict(common)
        m["xt"] = np.ascontiguousarray(x[c].T.astype(np.float32))
        in_maps.append(m)
    res = run_bass_kernel_spmd(nc, in_maps, core_ids=list(range(8)))
    out = np.stack([np.ascontiguousarray(np.asarray(r["out_xt"]).T)
                    for r in res.results], axis=0)
    return out.astype(np.float32)
